# version 09
# baseline (speedup 1.0000x reference)
import contextlib
import numpy as np
import concourse.bass as bass
import concourse.mybir as mybir
from concourse.bass_utils import run_bass_kernel_spmd

F32 = mybir.dt.float32
BF16 = mybir.dt.bfloat16
AF = mybir.ActivationFunctionType
ALU = mybir.AluOpType
AX = mybir.AxisListType.X

T = 2048
D = 2048
HL = 8
HD = 128
NT = 16
NF = 16
NEG = -1e30
EPS = 1e-5
SCALE = float(HD) ** -0.5

DEBUG = False


class Sem:
    __slots__ = ("h", "v")

    def __init__(self, h):
        self.h = h
        self.v = 0

    def inc(self, inst, n=1):
        self.v += n
        return inst.then_inc(self.h, n)


def build_nc():
    nc = bass.Bass("TRN2", target_bir_lowering=False,
                   detect_race_conditions=False)
    es = contextlib.ExitStack()

    din = {}
    for name, shape in [
        ("xb", [T, D]), ("wq", [D, HL * HD]), ("wk", [D, HL * HD]),
        ("wv", [D, HL * HD]), ("wout", [4, D, 512]), ("router", [16, 128, 8]),
        ("wa", [D, 1024]), ("wb", [4, 8, 128, 512]),
        ("cosf", [128, T]), ("sinf", [128, T]),
        ("rmat", [128, 128]), ("identf", [128, 128]),
        ("maskS", [128, 128]), ("maskST", [128, 128]),
        ("ones_row", [1, 128]), ("ones_col", [128, 1]), ("one1", [1, 1]),
    ]:
        din[name] = nc.dram_tensor(name, shape, F32, kind="ExternalInput")
    out_d = nc.dram_tensor("out", [1024, D], F32, kind="ExternalOutput")

    xn_dram = nc.dram_tensor("xn_dram", [128, NF * T], BF16)
    attn_dram = nc.dram_tensor("attn_dram", [HL, 128, T], BF16)
    brows_dram = nc.dram_tensor("brows_dram", [128, 128], F32)
    cc_in = nc.dram_tensor("cc_in", [HL, 128, 1024], BF16)
    cc_out = nc.dram_tensor("cc_out", [2, HL, 128, 1024], BF16)
    groups = [[0, 1], [2, 3], [4, 5], [6, 7]]

    dbg = {}
    if DEBUG:
        for nm, shp, dt in [("attn", [HL, 128, T], BF16),
                            ("brows", [128, 128], F32),
                            ("xn", [128, NF * T], BF16),
                            ("q", [128, HL * T], BF16),
                            ("k", [128, HL * T], BF16),
                            ("v", [128, HL * T], BF16),
                            ("pscr", [128, T], BF16),
                            ("mneg", [128, 128], F32),
                            ("ssum", [128, 128], F32),
                            ("ccout", [2, HL, 128, 1024], BF16)]:
            dbg[nm] = nc.dram_tensor("dbg_" + nm, shp, dt, kind="ExternalOutput")

    def sbuf(stack, name, shape, dt, side="left"):
        return stack.enter_context(nc.sbuf_tensor(name, shape, dt, side=side))

    def psum(stack, name, shape, dt):
        return stack.enter_context(nc.psum_tensor(name, shape, dt))

    sems = {}

    def S(name):
        s = Sem(es.enter_context(nc.semaphore(name)))
        sems[name] = s
        return s

    sd = S("sd")
    sg = S("sg")
    dsem = {n: S("dma_" + n) for n in
            ["cst", "cs2", "wq", "wk", "wv", "x0", "x1", "xnb", "brow",
             "at0", "at1", "xno", "wa", "rw", "ful", "xo",
             "wo0", "wo1", "out0", "out1"]}
    vsf = S("vsf")
    ssf = S("ssf")

    def vbar(v, inst):
        vsf.v += 1
        inst.then_inc(vsf.h, 1)
        v.wait_ge(vsf.h, vsf.v)

    def sbar(sc, inst):
        ssf.v += 1
        inst.then_inc(ssf.h, 1)
        sc.wait_ge(ssf.h, ssf.v)
    ld = {}

    identb = sbuf(es, "identb", [128, 128], BF16)
    onesr_sb = sbuf(es, "onesr_sb", [1, 128], F32)
    onesc_sb = sbuf(es, "onesc_sb", [128, 1], BF16)
    one1_sb = sbuf(es, "one1_sb", [1, 1], F32)

    es_p2c = contextlib.ExitStack()
    xn_fm = sbuf(es_p2c, "xn_fm", [128, NF * T], BF16)
    identf = sbuf(es_p2c, "identf_sb", [128, 128], F32)
    maskS_sb = sbuf(es_p2c, "maskS_sb", [128, 128], BF16)
    maskST_sb = sbuf(es_p2c, "maskST_sb", [128, 128], BF16)

    es_rope = contextlib.ExitStack()
    cos_sb = sbuf(es_rope, "cos_sb", [128, T], BF16)
    sin_sb = sbuf(es_rope, "sin_sb", [128, T], BF16)
    rmat_sb = sbuf(es_rope, "rmat_sb", [128, 128], BF16)

    es_wq = contextlib.ExitStack()
    wq_sb = sbuf(es_wq, "wq_sb", [128, NF * HL * HD], BF16)
    es_p1s = contextlib.ExitStack()
    praw = [sbuf(es_p1s, f"praw{i}", [128, 512], BF16) for i in range(2)]
    t1s = sbuf(es_p1s, "t1s", [128, 512], BF16)
    t2s = sbuf(es_p1s, "t2s", [128, 512], BF16)
    es_wk = contextlib.ExitStack()
    wk_sb = sbuf(es_wk, "wk_sb", [128, NF * HL * HD], BF16)

    es_mid = contextlib.ExitStack()

    p0 = {n: S("p0_" + n) for n in ["lnv", "lns", "st", "rst", "xn", "tr", "ev"]}
    es_p0 = contextlib.ExitStack()
    xstg = [sbuf(es_p0, f"xstg{i}", [128, D], F32, side="right") for i in range(2)]
    sqscr = [sbuf(es_p0, f"sqscr{i}", [128, D], BF16, side="right") for i in range(2)]
    xn_tok = [sbuf(es_p0, f"xn_tok{i}", [128, D], BF16, side="right") for i in range(2)]
    stat = sbuf(es_p0, "stat", [128, 8 * NT], F32, side="right")
    c_ = lambda j, tt: stat[:, j * NT + tt: j * NT + tt + 1]
    es_ptr = contextlib.ExitStack()
    ps_tr = [psum(es_ptr, f"ps_tr{i}", [128, 512], BF16) for i in range(2)]

    with nc.Block("p0") as block:
        @block.gpsimd
        def _(g):
            dsem["cst"].inc(g.dma_start(out=identb[:, :], in_=din["identf"][:, :]), 16)
            dsem["cst"].inc(g.dma_start(out=cos_sb[:, :], in_=din["cosf"][:, :]), 16)
            dsem["cst"].inc(g.dma_start(out=sin_sb[:, :], in_=din["sinf"][:, :]), 16)
            dsem["cst"].inc(g.dma_start(out=rmat_sb[:, :], in_=din["rmat"][:, :]), 16)
            dsem["cst"].inc(g.dma_start(out=maskS_sb[:, :], in_=din["maskS"][:, :]), 16)
            dsem["cst"].inc(g.dma_start(out=maskST_sb[:, :], in_=din["maskST"][:, :]), 16)
            dsem["cst"].inc(g.dma_start(out=onesc_sb[:, :], in_=din["ones_col"][:, :]), 16)
            src = din["wq"][:, :].rearrange("(f p) c -> p f c", p=128)
            dsem["wq"].inc(g.dma_start(out=wq_sb[:, :].rearrange("p (f c) -> p f c", f=NF),
                                       in_=src), 16)
            src = din["wk"][:, :].rearrange("(f p) c -> p f c", p=128)
            dsem["wk"].inc(g.dma_start(out=wk_sb[:, :].rearrange("p (f c) -> p f c", f=NF),
                                       in_=src), 16)

        @block.sync
        def _(s):
            dsem["cs2"].inc(s.dma_start(out=identf[:, :], in_=din["identf"][:, :]), 16)
            dsem["cs2"].inc(s.dma_start(out=onesr_sb[:, :], in_=din["ones_row"][:, :]), 16)
            dsem["cs2"].inc(s.dma_start(out=one1_sb[:, :], in_=din["one1"][:, :]), 16)
            for tt in range(NT):
                if tt >= 2:
                    s.wait_ge(p0["xn"].h, tt - 1)
                    s.wait_ge(p0["lns"].h, tt - 1)
                dsem["x%d" % (tt % 2)].inc(
                    s.dma_start(out=xstg[tt % 2][:, :],
                                in_=din["xb"][tt * 128:(tt + 1) * 128, :]), 16)

        def xwait(eng, tt):
            eng.wait_ge(dsem["x%d" % (tt % 2)].h, 16 * (tt // 2 + 1))

        @block.vector
        def _(v):
            for tt in range(NT):
                xwait(v, tt)
                vbar(v, v.reduce_sum(c_(0, tt), xstg[tt % 2][:, :], axis=AX))
                v.wait_ge(p0["lns"].h, tt + 1)
                vbar(v, v.tensor_scalar(c_(2, tt), c_(0, tt), 1.0 / D, None,
                                        ALU.mult))
                vbar(v, v.tensor_tensor(c_(3, tt), c_(2, tt), c_(2, tt), ALU.mult))
                vbar(v, v.tensor_scalar(c_(4, tt), c_(1, tt), 1.0 / D, c_(3, tt),
                                        ALU.mult, ALU.subtract))
                vbar(v, v.tensor_scalar(c_(5, tt), c_(4, tt), EPS, None, ALU.add))
                p0["st"].inc(v.reciprocal(c_(6, tt), c_(5, tt)))
                v.wait_ge(p0["rst"].h, tt + 1)
                if tt >= 2:
                    v.wait_ge(p0["tr"].h, 4 * (tt - 1))
                p0["xn"].inc(
                    v.tensor_scalar(xn_tok[tt % 2][:, :], xstg[tt % 2][:, :],
                                    c_(2, tt), c_(7, tt), ALU.subtract, ALU.mult))

        @block.tensor
        def _(t):
            t.wait_ge(dsem["cst"].h, 112)
            for tt in range(NT):
                t.wait_ge(p0["xn"].h, tt + 1)
                for g4 in range(4):
                    idx = tt * 4 + g4
                    if idx >= 2:
                        t.wait_ge(p0["ev"].h, idx - 1)
                    for j in range(4):
                        fc = g4 * 4 + j
                        tr = t.transpose(ps_tr[idx % 2][:, j * 128:(j + 1) * 128],
                                         xn_tok[tt % 2][:, fc * 128:(fc + 1) * 128],
                                         identb[:, :])
                    p0["tr"].inc(tr)

        @block.scalar
        def _(sc):
            for tt in range(NT):
                xwait(sc, tt)
                p0["lns"].inc(
                    sc.activation(sqscr[tt % 2][:, :], xstg[tt % 2][:, :],
                                  AF.Square, accum_out=c_(1, tt)))
                sc.wait_ge(p0["st"].h, tt + 1)
                p0["rst"].inc(sc.activation(c_(7, tt), c_(6, tt), AF.Sqrt))
                if tt >= 1:
                    for g4 in range(4):
                        idx = (tt - 1) * 4 + g4
                        sc.wait_ge(p0["tr"].h, idx + 1)
                        dst = xn_fm[:, :].rearrange("p (f t) -> p f t", f=NF)[
                            :, g4 * 4:(g4 + 1) * 4, (tt - 1) * 128: tt * 128]
                        p0["ev"].inc(sc.copy(
                            dst, ps_tr[idx % 2][:, :].rearrange("p (f t) -> p f t", f=4)))
            for g4 in range(4):
                idx = (NT - 1) * 4 + g4
                sc.wait_ge(p0["tr"].h, idx + 1)
                dst = xn_fm[:, :].rearrange("p (f t) -> p f t", f=NF)[
                    :, g4 * 4:(g4 + 1) * 4, (NT - 1) * 128: NT * 128]
                p0["ev"].inc(sc.copy(
                    dst, ps_tr[idx % 2][:, :].rearrange("p (f t) -> p f t", f=4)))

    es_p0.close()

    q_sb = sbuf(es_mid, "q_sb", [128, HL * T], BF16, side="right")
    es_pq = contextlib.ExitStack()
    ps_q = [psum(es_pq, f"ps_q{i}", [128, 512], F32) for i in range(4)]
    ps_r = [psum(es_pq, f"ps_r{i}", [128, 512], F32) for i in range(2)]
    IDXS = [(h, ts) for h in range(HL) for ts in range(4)]

    def qkv_rope_phase(block, w_sbuf, out_sbuf, scale, wsem, pm, pe, pr, pv):
        @block.tensor
        def _(t):
            t.wait_ge(wsem.h, 16)
            for i, (h, ts) in enumerate(IDXS):
                if i >= 4:
                    t.wait_ge(pe.h, i - 3)
                for fc in range(NF):
                    mm = t.matmul(ps_q[i % 4][:, :],
                                  w_sbuf[:, fc * HL * HD + h * HD:
                                         fc * HL * HD + (h + 1) * HD],
                                  xn_fm[:, fc * T + ts * 512: fc * T + (ts + 1) * 512],
                                  start=(fc == 0), stop=(fc == NF - 1))
                pm.inc(mm)
                if i >= 1:
                    j = i - 1
                    t.wait_ge(pe.h, j + 1)
                    if j >= 2:
                        t.wait_ge(pv.h, j - 1)
                    pr.inc(t.matmul(ps_r[j % 2][:, :], rmat_sb[:, :],
                                    praw[j % 2][:, :], start=True, stop=True))
            j = len(IDXS) - 1
            t.wait_ge(pe.h, j + 1)
            t.wait_ge(pv.h, j - 1)
            pr.inc(t.matmul(ps_r[j % 2][:, :], rmat_sb[:, :],
                            praw[j % 2][:, :], start=True, stop=True))

        @block.scalar
        def _(sc):
            for i, (h, ts) in enumerate(IDXS):
                sc.wait_ge(pm.h, i + 1)
                if i >= 2:
                    sc.wait_ge(pv.h, i - 1)
                pe.inc(sc.activation(praw[i % 2][:, :], ps_q[i % 4][:, :],
                                     AF.Copy, scale=scale))

        @block.vector
        def _(v):
            v.wait_ge(dsem["cst"].h, 112)
            for i, (h, ts) in enumerate(IDXS):
                v.wait_ge(pr.h, i + 1)
                v.tensor_tensor(t2s[:, :], ps_r[i % 2][:, :],
                                sin_sb[:, ts * 512:(ts + 1) * 512], ALU.mult)
                vbar(v, v.tensor_tensor(t1s[:, :], praw[i % 2][:, :],
                                        cos_sb[:, ts * 512:(ts + 1) * 512],
                                        ALU.mult))
                pv.inc(v.tensor_tensor(
                    out_sbuf[:, h * T + ts * 512: h * T + (ts + 1) * 512],
                    t1s[:, :], t2s[:, :], ALU.add))

    with nc.Block("p1a") as block:
        p1a = {n: S("p1a_" + n) for n in ["m", "e", "r", "v"]}
        qkv_rope_phase(block, wq_sb, q_sb, SCALE, dsem["wq"],
                       p1a["m"], p1a["e"], p1a["r"], p1a["v"])

    k_sb = sbuf(es_mid, "k_sb", [128, HL * T], BF16, side="right")
    with nc.Block("p1b") as block:
        p1b = {n: S("p1b_" + n) for n in ["m", "e", "r", "v"]}
        qkv_rope_phase(block, wk_sb, k_sb, 1.0, dsem["wk"],
                       p1b["m"], p1b["e"], p1b["r"], p1b["v"])

    es_wk.close()
    v_sb = sbuf(es_mid, "v_sb", [128, HL * T], BF16, side="right")
    wv_sb = wq_sb
    with nc.Block("p1c") as block:
        @block.gpsimd
        def _(g):
            src = din["wv"][:, :].rearrange("(f p) c -> p f c", p=128)
            dsem["wv"].inc(g.dma_start(out=wv_sb[:, :].rearrange("p (f c) -> p f c", f=NF),
                                       in_=src), 16)
        p1c = {n: S("p1c_" + n) for n in ["m", "e", "tr", "ev"]}

        @block.tensor
        def _(t):
            t.wait_ge(dsem["wv"].h, 16)
            for i, (h, ts) in enumerate(IDXS):
                if i >= 4:
                    t.wait_ge(p1c["e"].h, i - 3)
                for fc in range(NF):
                    mm = t.matmul(ps_q[i % 4][:, :],
                                  wv_sb[:, fc * HL * HD + h * HD:
                                        fc * HL * HD + (h + 1) * HD],
                                  xn_fm[:, fc * T + ts * 512: fc * T + (ts + 1) * 512],
                                  start=(fc == 0), stop=(fc == NF - 1))
                p1c["m"].inc(mm)
                if i >= 1:
                    j = i - 1
                    t.wait_ge(p1c["e"].h, j + 1)
                    if j >= 2:
                        t.wait_ge(p1c["ev"].h, j - 1)
                    for jj in range(4):
                        tr = t.transpose(ps_tr[j % 2][:, jj * 128:(jj + 1) * 128],
                                         praw[j % 2][:, jj * 128:(jj + 1) * 128],
                                         identb[:, :])
                    p1c["tr"].inc(tr)
            j = len(IDXS) - 1
            t.wait_ge(p1c["e"].h, j + 1)
            t.wait_ge(p1c["ev"].h, j - 1)
            for jj in range(4):
                tr = t.transpose(ps_tr[j % 2][:, jj * 128:(jj + 1) * 128],
                                 praw[j % 2][:, jj * 128:(jj + 1) * 128],
                                 identb[:, :])
            p1c["tr"].inc(tr)

        @block.scalar
        def _(sc):
            for i, (h, ts) in enumerate(IDXS):
                sc.wait_ge(p1c["m"].h, i + 1)
                if i >= 2:
                    sc.wait_ge(p1c["tr"].h, i - 1)
                p1c["e"].inc(sc.activation(praw[i % 2][:, :], ps_q[i % 4][:, :],
                                           AF.Copy))
                if i >= 1:
                    hp, tsp = IDXS[i - 1]
                    sc.wait_ge(p1c["tr"].h, i)
                    p1c["ev"].inc(sc.copy(
                        v_sb[:, hp * T + tsp * 512: hp * T + (tsp + 1) * 512],
                        ps_tr[(i - 1) % 2][:, :]))
            j = len(IDXS) - 1
            h, ts = IDXS[j]
            sc.wait_ge(p1c["tr"].h, j + 1)
            p1c["ev"].inc(sc.copy(
                v_sb[:, h * T + ts * 512: h * T + (ts + 1) * 512],
                ps_tr[j % 2][:, :]))

    es_pq.close()
    es_ptr.close()
    es_wk_dummy = None
    es_p1s.close()
    es_wq.close()
    es_rope.close()

    abuf = [sbuf(es_mid, f"abuf{i}", [128, 512], BF16, side="right")
            for i in range(2)]
    pscr = sbuf(es_mid, "pscr", [128, T], BF16, side="right")
    mneg = sbuf(es_mid, "mneg", [128, 128], F32, side="right")
    ssum = sbuf(es_mid, "ssum", [128, 128], F32, side="right")
    lns = sbuf(es_mid, "lns", [128, 128], F32, side="right")
    bh = sbuf(es_mid, "bh", [128, 128], F32, side="right")
    brow = sbuf(es_mid, "brow", [1, T], F32, side="right")
    eexp = [sbuf(es_mid, f"eexp{i}", [1, 512], BF16, side="right") for i in range(2)]
    onesr_bf = sbuf(es_mid, "onesr_bf", [1, 128], BF16, side="right")
    expb = [sbuf(es_mid, f"expb{i}", [128, 512], BF16, side="right") for i in range(2)]
    mexpb = [sbuf(es_mid, f"mexpb{i}", [128, 512], BF16, side="right") for i in range(2)]

    ccsem = S("ccsem")
    p2 = {n: S("p2_" + n) for n in ["s1", "msk", "exp", "st", "tB", "eB",
                                    "dg", "pt", "avd", "ae", "eb", "bc",
                                    "ebe", "pb"]}

    def kchunks(qt):
        kend = (qt + 1) * 128
        return [(kc * 512, min(512, kend - kc * 512))
                for kc in range((kend + 511) // 512)]

    EV2 = [(h, qs, kt) for h in range(HL) for qs in range(4)
           for kt in range(4 * qs + 4)]
    EIDX = {e: i for i, e in enumerate(EV2)}
    LASTE = {(h, qs): EIDX[(h, qs, 4 * qs + 3)] for h in range(HL)
             for qs in range(4)}
    DGC = np.cumsum([1 if kt >= 4 * qs else 0 for (h, qs, kt) in EV2]).tolist()
    brow_done = {}
    adma_done = {}

    with nc.Block("p2") as block:
        es_ps2 = contextlib.ExitStack()
        ps_s = [psum(es_ps2, f"ps_s{i}", [128, 2048], F32) for i in range(2)]

        @block.sync
        def _(s):
            s.wait_ge(p0["ev"].h, 64)
            dsem["xnb"].inc(s.dma_start(out=xn_dram[:, :], in_=xn_fm[:, :]), 16)


        @block.tensor
        def _(t):
            for h in range(HL):
                for qt in range(NT):
                    idx = h * NT + qt
                    if idx >= 2:
                        t.wait_ge(p2["exp"].h, idx - 1)
                    for (k0, n) in kchunks(qt):
                        mm = t.matmul(ps_s[idx % 2][:, k0:k0 + n],
                                      q_sb[:, h * T + qt * 128: h * T + (qt + 1) * 128],
                                      k_sb[:, h * T + k0: h * T + k0 + n],
                                      start=True, stop=True)
                    p2["s1"].inc(mm)

        @block.vector
        def _(v):
            for h in range(HL):
                for qt in range(NT):
                    idx = h * NT + qt
                    kend = (qt + 1) * 128
                    v.wait_ge(p2["s1"].h, idx + 1)
                    vbar(v, v.tensor_tensor(ps_s[idx % 2][:, kend - 128:kend],
                                            ps_s[idx % 2][:, kend - 128:kend],
                                            maskS_sb[:, :], ALU.add))
                    p2["msk"].inc(v.reduce_max(mneg[:, idx:idx + 1],
                                               ps_s[idx % 2][:, 0:kend],
                                               axis=AX, negate=True))

        @block.scalar
        def _(sc):
            for h in range(HL):
                for qt in range(NT):
                    idx = h * NT + qt
                    kend = (qt + 1) * 128
                    sc.wait_ge(p2["msk"].h, idx + 1)
                    p2["exp"].inc(sc.activation(
                        pscr[:, 0:kend], ps_s[idx % 2][:, 0:kend], AF.Exp,
                        bias=mneg[:, idx:idx + 1], accum_out=ssum[:, idx:idx + 1]))
            sc.wait_ge(p2["exp"].h, 128)
            p2["st"].inc(sc.activation(lns[:, :], ssum[:, :], AF.Ln))

        @block.vector
        def _(v):
            v.wait_ge(p2["st"].h, 1)
            v.wait_ge(p2["msk"].h, 128)
            p2["st"].inc(v.tensor_tensor(bh[:, :], mneg[:, :], lns[:, :],
                                         ALU.subtract))

        @block.tensor
        def _(t):
            t.wait_ge(p2["st"].h, 2)
            t.wait_ge(dsem["cs2"].h, 48)
            p2["tB"].inc(t.transpose(ps_s[0][:, 0:128], bh[:, :], identf[:, :]))

        @block.vector
        def _(v):
            v.wait_ge(p2["tB"].h, 1)
            p2["eB"].inc(v.tensor_copy(bh[:, :], ps_s[0][:, 0:128]))

        @block.sync
        def _(s):
            s.wait_ge(p2["eB"].h, 1)
            dsem["brow"].inc(s.dma_start(out=brows_dram[:, :], in_=bh[:, :]), 16)

            def emit_chunk(hh, qs):
                gidx = hh * 4 + qs
                s.wait_ge(p2["ae"].h, gidx + 1)
                dsem["at%d" % (gidx % 2)].inc(s.dma_start(
                    out=attn_dram[hh, :, qs * 512:(qs + 1) * 512],
                    in_=abuf[gidx % 2][:, :]), 16)

            for h in range(HL):
                if h >= 1:
                    for qs in range(4):
                        emit_chunk(h - 1, qs)
                    s.wait_ge(p2["dg"].h, LASTE[(h - 1, 3)] + 1)
                dsem["brow"].inc(s.dma_start(
                    out=brow[:, :],
                    in_=brows_dram[:, :].rearrange("(h r) c -> h (r c)", h=HL)[
                        h:h + 1, :]), 16)
            for qs in range(4):
                emit_chunk(HL - 1, qs)

        @block.gpsimd
        def _(g):
            dsem["cst"].inc(g.dma_start(out=onesr_bf[:, :], in_=din["ones_row"][:, :]), 16)
            pid = g.partition_id()
            oth = (1 - pid % 2) * 1024
            g.wait_ge(dsem["at0"].h, 16 * 16)
            g.wait_ge(dsem["at1"].h, 16 * 16)
            sg.inc(g.dma_start(out=cc_in[:, :, :],
                               in_=attn_dram[:, :, bass.ds(oth, 1024)]), 16)
            g.wait_ge(sg.h, sg.v)
            ccsem.inc(g.collective_compute(
                "AllGather", ALU.bypass, replica_groups=groups,
                ins=[cc_in.ap().opt()], outs=[cc_out.ap().opt()]))
            if DEBUG:
                g.wait_ge(sd.h, ld["attn_dram"])
                sg.inc(g.dma_start(out=dbg["mneg"].ap(), in_=mneg[:, :]), 16)
                sg.inc(g.dma_start(out=dbg["ssum"].ap(), in_=ssum[:, :]), 16)
                sg.inc(g.dma_start(out=dbg["q"].ap(), in_=q_sb[:, :]), 16)
                sg.inc(g.dma_start(out=dbg["k"].ap(), in_=k_sb[:, :]), 16)
                sg.inc(g.dma_start(out=dbg["v"].ap(), in_=v_sb[:, :]), 16)
                sg.inc(g.dma_start(out=dbg["pscr"].ap(), in_=pscr[:, :]), 16)
                g.wait_ge(sg.h, sg.v)

        es_ps2.close()
        es_ps2b = contextlib.ExitStack()
        ps_st = [psum(es_ps2b, f"ps_st{i}", [128, 512], F32) for i in range(4)]
        ps_av = [psum(es_ps2b, f"ps_av{i}", [128, 512], F32) for i in range(2)]
        ps_bc = [psum(es_ps2b, f"ps_bc{i}", [128, 512], F32) for i in range(2)]

        @block.tensor
        def _(t):
            def OFF(qs_, kt_):
                return max(0, kt_ - 4 * qs_) * 128

            t.wait_ge(dsem["cst"].h, 128)
            next_av = [0]

            def emit_av():
                j = next_av[0]
                jh, jqs, jkt = EV2[j]
                jo = OFF(jqs, jkt)
                t.wait_ge(p2["pb"].h, j + 1)
                p2["avd"].inc(t.matmul(
                    ps_av[(jh * 4 + jqs) % 2][:, jo:512],
                    v_sb[:, jh * T + jkt * 128: jh * T + (jkt + 1) * 128],
                    pscr[:, (j % 4) * 512 + jo:(j % 4 + 1) * 512],
                    start=(jkt == 0), stop=(jkt == 4 * jqs + 3)))
                next_av[0] += 1

            for i, (h, qs, kt) in enumerate(EV2):
                if kt == 0:
                    while next_av[0] < i:
                        emit_av()
                    gidx = h * 4 + qs
                    if gidx >= 2:
                        t.wait_ge(p2["ae"].h, gidx - 1)
                    t.wait_ge(p2["eb"].h, gidx + 1)
                    if gidx >= 2:
                        t.wait_ge(p2["ebe"].h, gidx - 1)
                    p2["bc"].inc(t.matmul(ps_bc[gidx % 2][:, :], onesr_bf[:, :],
                                          eexp[gidx % 2][:, :],
                                          start=True, stop=True))
                if i >= 4:
                    t.wait_ge(p2["pt"].h, i - 3)
                off = OFF(qs, kt)
                p2["dg"].inc(t.matmul(
                    ps_st[i % 4][:, off:512],
                    k_sb[:, h * T + kt * 128: h * T + (kt + 1) * 128],
                    q_sb[:, h * T + qs * 512 + off: h * T + (qs + 1) * 512],
                    start=True, stop=True))
                while i - next_av[0] >= 3:
                    emit_av()
            while next_av[0] < len(EV2):
                emit_av()

        @block.vector
        def _(v):
            def OFF(qs_, kt_):
                return max(0, kt_ - 4 * qs_) * 128

            tri_b = bass.AP(maskST_sb[:, :].tensor, 0,
                            [[128, 128], [0, 4], [1, 128]])

            def biasmul(i):
                jh, jqs, jkt = EV2[i]
                jo = OFF(jqs, jkt)
                jg = jh * 4 + jqs
                v.wait_ge(p2["pt"].h, i + 1)
                qsl = slice((i % 4) * 512 + jo, (i % 4 + 1) * 512)
                if jkt >= 4 * jqs:
                    mtt = v.tensor_tensor(
                        pscr[:, (i % 4) * 512 + jo:(i % 4) * 512 + jo + 128],
                        pscr[:, (i % 4) * 512 + jo:(i % 4) * 512 + jo + 128],
                        mexpb[jg % 2][:, jo:jo + 128], ALU.mult)
                    if jo + 128 < 512:
                        p2["pb"].inc(v.tensor_tensor(
                            pscr[:, (i % 4) * 512 + jo + 128:(i % 4 + 1) * 512],
                            pscr[:, (i % 4) * 512 + jo + 128:(i % 4 + 1) * 512],
                            expb[jg % 2][:, jo + 128:512], ALU.mult))
                    else:
                        p2["pb"].inc(mtt)
                else:
                    p2["pb"].inc(v.tensor_tensor(
                        pscr[:, qsl], pscr[:, qsl],
                        expb[jg % 2][:, jo:512], ALU.mult))

            for i, (h, qs, kt) in enumerate(EV2):
                if i >= 1:
                    biasmul(i - 1)
                if kt == 0:
                    gidx = h * 4 + qs
                    v.wait_ge(p2["ebe"].h, gidx + 1)
                    v.tensor_tensor(
                        mexpb[gidx % 2][:, :].rearrange("p (a b) -> p a b", a=4),
                        expb[gidx % 2][:, :].rearrange("p (a b) -> p a b", a=4),
                        tri_b, ALU.mult)
            biasmul(len(EV2) - 1)

        @block.scalar
        def _(sc):
            for i, (h, qs, kt) in enumerate(EV2):
                gidx = h * 4 + qs
                if kt == 0:
                    sc.wait_ge(dsem["brow"].h, 16 * (h + 2))
                    if gidx >= 2:
                        g2 = gidx - 2
                        sc.wait_ge(p2["pb"].h, LASTE[(g2 // 4, g2 % 4)] + 1)
                        sc.wait_ge(p2["bc"].h, gidx - 1)
                    p2["eb"].inc(sc.activation(eexp[gidx % 2][:, :],
                                               brow[0:1, qs * 512:(qs + 1) * 512],
                                               AF.Exp))
                    sc.wait_ge(p2["bc"].h, gidx + 1)
                    p2["ebe"].inc(sc.copy(expb[gidx % 2][:, :],
                                          ps_bc[gidx % 2][:, :]))
                sc.wait_ge(p2["dg"].h, i + 1)
                if i >= 4:
                    sc.wait_ge(p2["avd"].h, i - 3)
                off = max(0, kt - 4 * qs) * 128
                p2["pt"].inc(sc.activation(
                    pscr[:, (i % 4) * 512 + off:(i % 4 + 1) * 512],
                    ps_st[i % 4][:, off:512], AF.Exp))
                if kt == 4 * qs + 3:
                    sc.wait_ge(p2["avd"].h, i + 1)
                    if gidx >= 2:
                        sc.wait_ge(dsem["at%d" % (gidx % 2)].h,
                                   16 * ((gidx - 2) // 2 + 1))
                    p2["ae"].inc(sc.copy(abuf[gidx % 2][:, :],
                                         ps_av[gidx % 2][:, :]))

        es_ps2b.close()

    es_mid.close()
    es_p2c.close()


    es_p5l = contextlib.ExitStack()
    attn_full = sbuf(es_p5l, "attn_full", [128, 16 * 1024], BF16)
    at_sb = sbuf(es_p5l, "at_sb", [128, 8 * 1024], BF16)
    wout_sb = [sbuf(es_p5l, f"wout_sb{i}", [128, 16 * 512], BF16) for i in range(2)]
    wb_sb = [sbuf(es_p5l, f"wb_sb{i}", [128, 8 * 512], BF16) for i in range(2)]
    wdone = {}
    es_p34 = contextlib.ExitStack()
    xn_own = sbuf(es_p34, "xn_own", [128, NF * 1024], BF16)
    wa_sb = sbuf(es_p34, "wa_sb", [128, NF * 1024], BF16)
    rw_sb = sbuf(es_p34, "rw_sb", [128, 128], BF16)
    as_sb = sbuf(es_p34, "as_sb", [128, 8 * 1024], BF16)
    sqb = [sbuf(es_p34, f"sqb{i}", [128, 1024], BF16) for i in range(2)]
    ssrow = sbuf(es_p34, "ssrow", [1, 1024], F32)
    nrow = sbuf(es_p34, "nrow", [1, 1024], F32)
    rsrow = sbuf(es_p34, "rsrow", [1, 1024], F32)
    rscol = sbuf(es_p34, "rscol", [128, 8], F32)
    lcol = sbuf(es_p34, "lcol", [128, 8], F32)
    ecol = sbuf(es_p34, "ecol", [128, 8], F32)
    mrcol = sbuf(es_p34, "mrcol", [128, 8], F32)
    indcol = sbuf(es_p34, "indcol", [128, 8], F32)
    cmb = sbuf(es_p34, "cmb", [128, 64], F32)
    tstat = sbuf(es_p34, "tstat", [128, 8], F32)

    p3 = {n: S("p3_" + n) for n in ["sq", "ss", "se", "rs", "rc", "lg", "le",
                                    "tp", "am", "ae", "at", "ate"]}

    with nc.Block("p34") as block:
        @block.gpsimd
        def _(g):
            pid = g.partition_id()
            own = (pid % 2) * 1024
            partner = 1 - (pid % 2)
            g.wait_ge(dsem["xnb"].h, 16)
            dsem["xno"].inc(g.dma_start(
                out=xn_own[:, :].rearrange("p (f t) -> p f t", f=NF),
                in_=xn_dram[:, :].rearrange("p (f t) -> p f t", f=NF)[
                    :, :, bass.ds(own, 1024)]), 16)
            dsem["wa"].inc(g.dma_start(
                out=wa_sb[:, :].rearrange("p (f c) -> p f c", f=NF),
                in_=din["wa"][:, :].rearrange("(f p) c -> p f c", p=128)), 16)
            dsem["rw"].inc(g.dma_start(
                out=rw_sb[:, 0:NF * 8].rearrange("p (f c) -> p f c", f=NF),
                in_=din["router"][:, :, :].rearrange("f p c -> p f c")), 16)
            g.wait_ge(dsem["at0"].h, 16 * 16)
            g.wait_ge(dsem["at1"].h, 16 * 16)
            dsem["ful"].inc(g.dma_start(
                out=attn_full[:, 0:HL * 1024].rearrange("p (h t) -> p h t", h=HL),
                in_=attn_dram[:, :, bass.ds(own, 1024)].rearrange(
                    "h p t -> p h t")), 16)
            for os in range(2):
                dsem["wo%d" % os].inc(g.dma_start(
                    out=wout_sb[os][:, :].rearrange("p (gi c) -> p gi c", gi=16),
                    in_=din["wout"][os].rearrange("(gi p) c -> p gi c", p=128)), 16)
                dsem["wo%d" % os].inc(g.dma_start(
                    out=wb_sb[os][:, :].rearrange("p (e c) -> p e c", e=8),
                    in_=din["wb"][os].rearrange("e p c -> p e c")), 16)
            g.wait_ge(ccsem.h, 1)
            dsem["ful"].inc(g.dma_start(
                out=attn_full[:, HL * 1024:].rearrange("p (h t) -> p h t", h=HL),
                in_=cc_out[partner].rearrange("h p t -> p h t")), 16)

        @block.scalar
        def _(sc):
            sc.wait_ge(dsem["xno"].h, 16)
            for fc in range(NF):
                if fc >= 2:
                    sc.wait_ge(p3["ss"].h, fc - 1)
                p3["sq"].inc(sc.activation(sqb[fc % 2][:, :],
                                           xn_own[:, fc * 1024:(fc + 1) * 1024],
                                           AF.Square))

        es_ps34 = contextlib.ExitStack()
        ps_ss = [psum(es_ps34, f"ps_ss{i}", [1, 512], F32) for i in range(2)]
        ps_rc = psum(es_ps34, "ps_rc", [128, 8], F32)
        ps_lg = [psum(es_ps34, f"ps_lg{i}", [128, 8], F32) for i in range(2)]
        ps_a = [psum(es_ps34, f"ps_a{i}", [128, 512], F32) for i in range(2)]
        ps_t4 = psum(es_ps34, "ps_t4", [128, 512], BF16)

        @block.tensor
        def _(t):
            for fc in range(NF):
                t.wait_ge(p3["sq"].h, fc + 1)
                for th in range(2):
                    mm = t.matmul(ps_ss[th][:, :], onesc_sb[:, :],
                                  sqb[fc % 2][:, th * 512:(th + 1) * 512],
                                  start=(fc == 0), stop=(fc == NF - 1))
                p3["ss"].inc(mm)
            t.wait_ge(p3["rs"].h, 1)
            for tt in range(8):
                mm = t.matmul(ps_rc[:, tt:tt + 1],
                              rsrow[0:1, tt * 128:(tt + 1) * 128],
                              one1_sb[:, :], start=(tt == 0), stop=(tt == 7))
            p3["rc"].inc(mm)
            t.wait_ge(dsem["rw"].h, 16)
            for tt in range(8):
                if tt >= 2:
                    t.wait_ge(p3["le"].h, tt - 1)
                for fc in range(NF):
                    mm = t.matmul(ps_lg[tt % 2][:, :],
                                  xn_own[:, fc * 1024 + tt * 128:
                                         fc * 1024 + (tt + 1) * 128],
                                  rw_sb[:, fc * 8:(fc + 1) * 8],
                                  start=(fc == 0), stop=(fc == NF - 1))
                p3["lg"].inc(mm)
            t.wait_ge(dsem["wa"].h, 16)
            for tt in range(8):
                for eg in range(2):
                    aidx = tt * 2 + eg
                    if aidx >= 2:
                        t.wait_ge(p3["ae"].h, aidx - 1)
                    for fc in range(NF):
                        mm = t.matmul(ps_a[aidx % 2][:, :],
                                      xn_own[:, fc * 1024 + tt * 128:
                                             fc * 1024 + (tt + 1) * 128],
                                      wa_sb[:, fc * 1024 + eg * 512:
                                            fc * 1024 + (eg + 1) * 512],
                                      start=(fc == 0), stop=(fc == NF - 1))
                    p3["am"].inc(mm)
            for e in range(8):
                for tg in range(2):
                    tidx = e * 2 + tg
                    need_aidx = (tg * 4 + 3) * 2 + (e // 4)
                    t.wait_ge(p3["ae"].h, need_aidx + 1)
                    if tidx >= 1:
                        t.wait_ge(p3["ate"].h, tidx)
                    for j in range(4):
                        tt = tg * 4 + j
                        tr = t.transpose(ps_t4[:, j * 128:(j + 1) * 128],
                                         as_sb[:, tt * 1024 + e * 128:
                                               tt * 1024 + (e + 1) * 128],
                                         identb[:, :])
                    p3["at"].inc(tr)

        @block.vector
        def _(v):
            v.wait_ge(p3["ss"].h, NF)
            v.tensor_copy(ssrow[:, 0:512], ps_ss[0][:, :])
            p3["se"].inc(v.tensor_copy(ssrow[:, 512:1024], ps_ss[1][:, :]))

        @block.scalar
        def _(sc):
            sc.wait_ge(p3["se"].h, 1)
            p3["se"].inc(sc.activation(nrow[:, :], ssrow[:, :], AF.Sqrt))

        @block.vector
        def _(v):
            v.wait_ge(p3["se"].h, 2)
            vbar(v, v.tensor_scalar(ssrow[:, :], nrow[:, :], 1e-12, None,
                                    ALU.max))
            p3["rs"].inc(v.reciprocal(rsrow[:, :], ssrow[:, :]))
            v.wait_ge(p3["rc"].h, 1)
            for tt in range(8):
                cp = v.tensor_copy(rscol[:, tt:tt + 1], ps_rc[:, tt:tt + 1])
            vbar(v, cp)
            for tt in range(8):
                v.wait_ge(p3["lg"].h, tt + 1)
                vbar(v, v.tensor_scalar(lcol[:, :], ps_lg[tt % 2][:, :],
                                        rscol[:, tt:tt + 1], None, ALU.mult))
                p3["le"].inc(v.nop())
                p3["tp"].inc(v.reduce_max(tstat[:, 0:1], lcol[:, :],
                                          axis=AX, negate=True))
                v.wait_ge(p3["tp"].h, 2 * (tt + 1))
                vbar(v, v.reduce_max(tstat[:, 1:2], ecol[:, :], axis=AX))
                vbar(v, v.tensor_single_scalar(mrcol[:, :], ecol[:, :],
                                               tstat[:, 1:2], ALU.is_lt))
                vbar(v, v.tensor_tensor(mrcol[:, :], mrcol[:, :], ecol[:, :],
                                        ALU.mult))
                vbar(v, v.reduce_max(tstat[:, 2:3], mrcol[:, :], axis=AX))
                vbar(v, v.tensor_single_scalar(indcol[:, :], ecol[:, :],
                                               tstat[:, 2:3], ALU.is_ge))
                vbar(v, v.tensor_tensor(indcol[:, :], indcol[:, :], ecol[:, :],
                                        ALU.mult))
                vbar(v, v.reduce_sum(tstat[:, 3:4], indcol[:, :], axis=AX))
                vbar(v, v.reciprocal(tstat[:, 4:5], tstat[:, 3:4]))
                vbar(v, v.tensor_scalar(cmb[:, tt * 8:(tt + 1) * 8],
                                        indcol[:, :], tstat[:, 4:5], None,
                                        ALU.mult))
            for tt in range(8):
                for eg in range(2):
                    aidx = tt * 2 + eg
                    v.wait_ge(p3["am"].h, aidx + 1)
                    for j in range(4):
                        e = eg * 4 + j
                        ts_ = v.tensor_scalar(
                            as_sb[:, tt * 1024 + e * 128:tt * 1024 + (e + 1) * 128],
                            ps_a[aidx % 2][:, j * 128:(j + 1) * 128],
                            cmb[:, tt * 8 + e: tt * 8 + e + 1], None, ALU.mult)
                    p3["ae"].inc(ts_)

        @block.scalar
        def _(sc):
            for tt in range(8):
                sc.wait_ge(p3["tp"].h, 2 * tt + 1)
                p3["tp"].inc(sc.activation(ecol[:, :], lcol[:, :], AF.Exp,
                                           bias=tstat[:, 0:1]))
            for e in range(8):
                for tg in range(2):
                    tidx = e * 2 + tg
                    sc.wait_ge(p3["at"].h, tidx + 1)
                    p3["ate"].inc(sc.copy(
                        at_sb[:, e * 1024 + tg * 512: e * 1024 + (tg + 1) * 512],
                        ps_t4[:, :]))

        es_ps34.close()

    es_p34.close()

    es_p5 = contextlib.ExitStack()
    x_own = sbuf(es_p5, "x_own", [128, 8 * D], F32)
    gscr = [sbuf(es_p5, f"gscr{i}", [128, 512], F32) for i in range(2)]
    sgscr = [sbuf(es_p5, f"sgscr{i}", [128, 512], F32) for i in range(2)]
    oscr = [sbuf(es_p5, f"oscr{i}", [128, 512], F32) for i in range(2)]
    p5 = {n: S("p5_" + n) for n in ["s", "y", "sil", "c"]}
    odone = {}

    with nc.Block("p5") as block:
        es_ps5 = contextlib.ExitStack()
        ps_s5 = [psum(es_ps5, f"ps_s5{i}", [128, 512], F32) for i in range(2)]
        ps_y5 = [psum(es_ps5, f"ps_y5{i}", [128, 512], F32) for i in range(2)]

        @block.gpsimd
        def _(g):
            pid = g.partition_id()
            xb3 = din["xb"][:, :].rearrange("(a r) c -> a r c", a=2)
            dsem["xo"].inc(g.dma_start(
                out=x_own[:, :].rearrange("p (tt c) -> p tt c", tt=8),
                in_=xb3[pid % 2].rearrange("(tt p) c -> p tt c", p=128)), 16)
            for os in range(2, 4):
                g.wait_ge(p5["s"].h, 8 * (os - 1))
                g.wait_ge(p5["y"].h, 8 * (os - 1))
                dsem["wo%d" % (os % 2)].inc(g.dma_start(
                    out=wout_sb[os % 2][:, :].rearrange("p (gi c) -> p gi c", gi=16),
                    in_=din["wout"][os].rearrange("(gi p) c -> p gi c", p=128)), 16)
                dsem["wo%d" % (os % 2)].inc(g.dma_start(
                    out=wb_sb[os % 2][:, :].rearrange("p (e c) -> p e c", e=8),
                    in_=din["wb"][os].rearrange("e p c -> p e c")), 16)
            g.wait_ge(dsem["xo"].h, 16)

        @block.tensor
        def _(t):
            t.wait_ge(dsem["ful"].h, 32)
            t.wait_ge(p3["ate"].h, 16)
            for os in range(4):
                t.wait_ge(dsem["wo%d" % (os % 2)].h, 32 * (os // 2 + 1))
                for tt in range(8):
                    idx = os * 8 + tt
                    if idx >= 2:
                        t.wait_ge(p5["c"].h, idx - 1)
                        t.wait_ge(p5["sil"].h, idx - 1)
                    for gi in range(16):
                        mm = t.matmul(ps_s5[idx % 2][:, :],
                                      attn_full[:, gi * 1024 + tt * 128:
                                                gi * 1024 + (tt + 1) * 128],
                                      wout_sb[os % 2][:, gi * 512:(gi + 1) * 512],
                                      start=(gi == 0), stop=(gi == 15))
                    p5["s"].inc(mm)
                    for e in range(8):
                        mm = t.matmul(ps_y5[idx % 2][:, :],
                                      at_sb[:, e * 1024 + tt * 128:
                                            e * 1024 + (tt + 1) * 128],
                                      wb_sb[os % 2][:, e * 512:(e + 1) * 512],
                                      start=(e == 0), stop=(e == 7))
                    p5["y"].inc(mm)

        @block.scalar
        def _(sc):
            for os in range(4):
                for tt in range(8):
                    idx = os * 8 + tt
                    sc.wait_ge(p5["y"].h, idx + 1)
                    if idx >= 2:
                        sc.wait_ge(p5["c"].h, idx - 1)
                    p5["sil"].inc(sc.activation(gscr[idx % 2][:, :],
                                                ps_y5[idx % 2][:, :], AF.Silu))

        @block.sync
        def _(s):
            for os in range(4):
                for tt in range(8):
                    idx = os * 8 + tt
                    s.wait_ge(p5["c"].h, idx + 1)
                    dsem["out%d" % (idx % 2)].inc(s.dma_start(
                        out=out_d[tt * 128:(tt + 1) * 128, os * 512:(os + 1) * 512],
                        in_=oscr[idx % 2][:, :]), 16)
            s.wait_ge(dsem["out0"].h, 16 * 16)
            s.wait_ge(dsem["out1"].h, 16 * 16)

        @block.vector
        def _(v):
            v.wait_ge(dsem["xo"].h, 16)
            for os in range(4):
                for tt in range(8):
                    idx = os * 8 + tt
                    v.wait_ge(p5["s"].h, idx + 1)
                    v.wait_ge(p5["sil"].h, idx + 1)
                    if idx >= 2:
                        v.wait_ge(dsem["out%d" % (idx % 2)].h,
                                  16 * ((idx - 2) // 2 + 1))
                    vbar(v, v.scalar_tensor_tensor(sgscr[idx % 2][:, :],
                                                   ps_s5[idx % 2][:, :], 1.0,
                                                   gscr[idx % 2][:, :],
                                                   ALU.mult, ALU.mult))
                    p5["c"].inc(v.tensor_tensor(
                        oscr[idx % 2][:, :], sgscr[idx % 2][:, :],
                        x_own[:, tt * D + os * 512: tt * D + (os + 1) * 512],
                        ALU.add))

        if DEBUG:
            @block.gpsimd
            def _(g):
                g.wait_ge(sd.h, sd.v)
                sg.inc(g.dma_start(out=dbg["attn"].ap(), in_=attn_dram.ap()), 16)
                sg.inc(g.dma_start(out=dbg["brows"].ap(), in_=brows_dram.ap()), 16)
                sg.inc(g.dma_start(out=dbg["xn"].ap(), in_=xn_dram.ap()), 16)
                sg.inc(g.dma_start(out=dbg["ccout"].ap(), in_=cc_out.ap()), 16)
                g.wait_ge(sg.h, sg.v)

        es_ps5.close()
    es_p5.close()
    es_p5l.close()
    return nc


def _host_inputs(x, w_qkv, w_out, router_w, Wa, Wb):
    import jax
    import jax.numpy as jnp
    x = np.asarray(x, dtype=np.float32)
    w_qkv = np.asarray(w_qkv, dtype=np.float32)
    w_out = np.asarray(w_out, dtype=np.float32)
    router_w = np.asarray(router_w, dtype=np.float32)
    Wa = np.asarray(Wa, dtype=np.float32)
    Wb = np.asarray(Wb, dtype=np.float32)

    with jax.default_device(jax.devices("cpu")[0]):
        inv = jnp.asarray(1.0) / jnp.asarray(10000.0) ** (
            jnp.arange(0, HD, 2, dtype=jnp.float32) / HD)
        tv = jnp.arange(T, dtype=jnp.float32)
        freqs = jnp.einsum("i,j->ij", tv, inv)
        emb = jnp.concatenate([freqs, freqs], axis=-1)
        cosf = np.ascontiguousarray(np.asarray(jnp.cos(emb)).T)
        sinf = np.ascontiguousarray(np.asarray(jnp.sin(emb)).T)

    R = np.zeros((128, 128), dtype=np.float32)
    half = np.arange(64)
    R[half, half + 64] = -1.0
    R[half + 64, half] = 1.0
    rmat = np.ascontiguousarray(R.T)

    ii, jj = np.indices((128, 128))
    maskS = np.where(jj <= ii, 0.0, NEG).astype(np.float32)
    maskST = np.where(ii <= jj, 1.0, 0.0).astype(np.float32)
    identf = np.eye(128, dtype=np.float32)

    wa_arr = np.ascontiguousarray(Wa.transpose(1, 0, 2).reshape(D, 1024))
    wb_arr = np.ascontiguousarray(Wb.reshape(8, 128, 4, 512).transpose(2, 0, 1, 3))
    router_arr = np.ascontiguousarray(router_w.reshape(16, 128, 8))
    woutT = w_out.T

    in_maps = []
    for c in range(8):
        b, hg = c // 2, c % 2
        hs = slice(hg * 1024, (hg + 1) * 1024)
        os_ = slice((1 - hg) * 1024, (2 - hg) * 1024)
        wout_local = np.concatenate([woutT[hs], woutT[os_]], axis=0)
        wout_arr = np.ascontiguousarray(
            wout_local.reshape(D, 4, 512).transpose(1, 0, 2))
        in_maps.append({
            "xb": np.ascontiguousarray(x[b]),
            "wq": np.ascontiguousarray(w_qkv[hs, :].T),
            "wk": np.ascontiguousarray(
                w_qkv[2048 + hg * 1024: 2048 + (hg + 1) * 1024, :].T),
            "wv": np.ascontiguousarray(
                w_qkv[4096 + hg * 1024: 4096 + (hg + 1) * 1024, :].T),
            "wout": wout_arr,
            "router": router_arr,
            "wa": wa_arr,
            "wb": wb_arr,
            "cosf": cosf, "sinf": sinf, "rmat": rmat, "identf": identf,
            "maskS": maskS, "maskST": maskST,
            "ones_row": np.ones((1, 128), np.float32),
            "ones_col": np.ones((128, 1), np.float32),
            "one1": np.ones((1, 1), np.float32),
        })
    return in_maps


def kernel(L0_attn_x, L0_attn_w_qkv, L0_attn_w_out, L0_ffn_router_w,
           L0_ffn_Wa, L0_ffn_Wb):
    in_maps = _host_inputs(L0_attn_x, L0_attn_w_qkv, L0_attn_w_out,
                           L0_ffn_router_w, L0_ffn_Wa, L0_ffn_Wb)
    nc = build_nc()
    res = run_bass_kernel_spmd(nc, in_maps, core_ids=list(range(8)))
    out = np.empty((4, T, D), dtype=np.float32)
    for c in range(8):
        b, hg = c // 2, c % 2
        out[b, hg * 1024:(hg + 1) * 1024] = res.results[c]["out"]
    if DEBUG:
        kernel.last_results = res
    return out


# version 12
# speedup vs baseline: 1.0460x; 1.0460x over previous
import contextlib
import numpy as np
import concourse.bass as bass
import concourse.mybir as mybir
from concourse.bass_utils import run_bass_kernel_spmd

F32 = mybir.dt.float32
BF16 = mybir.dt.bfloat16
AF = mybir.ActivationFunctionType
ALU = mybir.AluOpType
AX = mybir.AxisListType.X

T = 2048
D = 2048
HL = 8
HD = 128
NT = 16
NF = 16
NEG = -1e30
EPS = 1e-5
SCALE = float(HD) ** -0.5

DEBUG = False


class Sem:
    __slots__ = ("h", "v")

    def __init__(self, h):
        self.h = h
        self.v = 0

    def inc(self, inst, n=1):
        self.v += n
        return inst.then_inc(self.h, n)


def build_nc():
    nc = bass.Bass("TRN2", target_bir_lowering=False,
                   detect_race_conditions=False)
    es = contextlib.ExitStack()

    din = {}
    for name, shape in [
        ("xb", [T, D]), ("wqkv", [HL, D, 3 * HD]),
        ("wout", [4, D, 512]), ("router", [16, 128, 8]),
        ("wa", [D, 1024]), ("wb", [4, 8, 128, 512]),
        ("cosf", [128, T]), ("sinf", [128, T]),
        ("rmat", [128, 128]), ("identf", [128, 128]),
        ("maskS", [128, 128]), ("maskST", [128, 128]),
        ("ones_row", [1, 128]), ("ones_col", [128, 1]), ("one1", [1, 1]),
    ]:
        din[name] = nc.dram_tensor(name, shape, F32, kind="ExternalInput")
    out_d = nc.dram_tensor("out", [1024, D], F32, kind="ExternalOutput")

    xn_dram = nc.dram_tensor("xn_dram", [128, NF * T], BF16)
    attn_dram = nc.dram_tensor("attn_dram", [HL, 128, T], BF16)
    brows_dram = nc.dram_tensor("brows_dram", [128, 128], F32)
    cc_in = nc.dram_tensor("cc_in", [HL, 128, 1024], BF16)
    cc_out = nc.dram_tensor("cc_out", [2, HL, 128, 1024], BF16)
    groups = [[0, 1], [2, 3], [4, 5], [6, 7]]

    dbg = {}
    if DEBUG:
        for nm, shp, dt in [("attn", [HL, 128, T], BF16),
                            ("brows", [128, 128], F32),
                            ("xn", [128, NF * T], BF16),
                            ("q", [128, HL * T], BF16),
                            ("k", [128, HL * T], BF16),
                            ("v", [128, HL * T], BF16),
                            ("pscr", [128, T], BF16),
                            ("mneg", [128, 128], F32),
                            ("ssum", [128, 128], F32),
                            ("ccout", [2, HL, 128, 1024], BF16),
                            ("eexp0", [1, 512], BF16),
                            ("expb0", [128, 512], BF16),
                            ("mexpb0", [128, 512], BF16),
                            ("brow0", [1, T], F32),
                            ("abuf0", [128, 512], BF16)]:
            dbg[nm] = nc.dram_tensor("dbg_" + nm, shp, dt, kind="ExternalOutput")

    def sbuf(stack, name, shape, dt, side="left"):
        return stack.enter_context(nc.sbuf_tensor(name, shape, dt, side=side))

    def psum(stack, name, shape, dt):
        return stack.enter_context(nc.psum_tensor(name, shape, dt))

    sems = {}

    def S(name):
        s = Sem(es.enter_context(nc.semaphore(name)))
        sems[name] = s
        return s

    sd = S("sd")
    sg = S("sg")
    dsem = {n: S("dma_" + n) for n in
            ["cst", "cs2", "wq", "wk", "wv", "x0", "x1", "xnb", "brow",
             "at0", "at1", "xno", "wa", "rw", "ful", "xo",
             "wo0", "wo1", "out0", "out1", "dbg2"]}
    vsf = S("vsf")
    ssf = S("ssf")

    def vbar(v, inst):
        vsf.v += 1
        inst.then_inc(vsf.h, 1)
        v.wait_ge(vsf.h, vsf.v)

    def sbar(sc, inst):
        ssf.v += 1
        inst.then_inc(ssf.h, 1)
        sc.wait_ge(ssf.h, ssf.v)
    ld = {}

    identb = sbuf(es, "identb", [128, 128], BF16)
    onesr_sb = sbuf(es, "onesr_sb", [1, 128], F32)
    onesc_sb = sbuf(es, "onesc_sb", [128, 1], BF16)
    one1_sb = sbuf(es, "one1_sb", [1, 1], F32)

    es_p2c = contextlib.ExitStack()
    xn_fm = sbuf(es_p2c, "xn_fm", [128, NF * T], BF16)
    identf = sbuf(es_p2c, "identf_sb", [128, 128], F32)
    maskS_sb = sbuf(es_p2c, "maskS_sb", [128, 128], BF16)
    maskST_sb = sbuf(es_p2c, "maskST_sb", [128, 128], BF16)

    es_rope = contextlib.ExitStack()
    cos_sb = sbuf(es_rope, "cos_sb", [128, T], BF16)
    sin_sb = sbuf(es_rope, "sin_sb", [128, T], BF16)
    rmat_sb = sbuf(es_rope, "rmat_sb", [128, 128], BF16)

    es_p1s = contextlib.ExitStack()
    wslab = [sbuf(es_p1s, f"wslab{i}", [128, NF * 3 * HD], BF16) for i in range(2)]
    praw = [sbuf(es_p1s, f"praw{i}", [128, 512], BF16) for i in range(2)]
    t1s = sbuf(es_p1s, "t1s", [128, 512], BF16)
    t2s = sbuf(es_p1s, "t2s", [128, 512], BF16)
    p1stat = sbuf(es_p1s, "p1stat", [128, 64], F32)
    pscr1 = sbuf(es_p1s, "pscr1", [128, 1024], BF16)

    es_mid = contextlib.ExitStack()

    p0 = {n: S("p0_" + n) for n in ["lnv", "lns", "st", "rst", "xn", "tr", "ev"]}
    es_p0 = contextlib.ExitStack()
    xstg = [sbuf(es_p0, f"xstg{i}", [128, D], F32, side="right") for i in range(2)]
    sqscr = [sbuf(es_p0, f"sqscr{i}", [128, D], BF16, side="right") for i in range(2)]
    xn_tok = [sbuf(es_p0, f"xn_tok{i}", [128, D], BF16, side="right") for i in range(2)]
    stat = sbuf(es_p0, "stat", [128, 8 * NT], F32, side="right")
    c_ = lambda j, tt: stat[:, j * NT + tt: j * NT + tt + 1]
    es_ptr = contextlib.ExitStack()
    ps_tr = [psum(es_ptr, f"ps_tr{i}", [128, 512], BF16) for i in range(2)]

    with nc.Block("p0") as block:
        @block.gpsimd
        def _(g):
            dsem["cst"].inc(g.dma_start(out=identb[:, :], in_=din["identf"][:, :]), 16)
            dsem["cst"].inc(g.dma_start(out=cos_sb[:, :], in_=din["cosf"][:, :]), 16)
            dsem["cst"].inc(g.dma_start(out=sin_sb[:, :], in_=din["sinf"][:, :]), 16)
            dsem["cst"].inc(g.dma_start(out=rmat_sb[:, :], in_=din["rmat"][:, :]), 16)
            dsem["cst"].inc(g.dma_start(out=maskS_sb[:, :], in_=din["maskS"][:, :]), 16)
            dsem["cst"].inc(g.dma_start(out=maskST_sb[:, :], in_=din["maskST"][:, :]), 16)
            dsem["cst"].inc(g.dma_start(out=onesc_sb[:, :], in_=din["ones_col"][:, :]), 16)
            for h in range(2):
                dsem["wq" if h % 2 == 0 else "wk"].inc(g.dma_start(
                    out=wslab[h % 2][:, :].rearrange("p (f c) -> p f c", f=NF),
                    in_=din["wqkv"][h].rearrange("(f p) c -> p f c", p=128)), 16)

        @block.sync
        def _(s):
            dsem["cs2"].inc(s.dma_start(out=identf[:, :], in_=din["identf"][:, :]), 16)
            dsem["cs2"].inc(s.dma_start(out=onesr_sb[:, :], in_=din["ones_row"][:, :]), 16)
            dsem["cs2"].inc(s.dma_start(out=one1_sb[:, :], in_=din["one1"][:, :]), 16)
            for tt in range(NT):
                if tt >= 2:
                    s.wait_ge(p0["xn"].h, tt - 1)
                    s.wait_ge(p0["lns"].h, tt - 1)
                dsem["x%d" % (tt % 2)].inc(
                    s.dma_start(out=xstg[tt % 2][:, :],
                                in_=din["xb"][tt * 128:(tt + 1) * 128, :]), 16)

        def xwait(eng, tt):
            eng.wait_ge(dsem["x%d" % (tt % 2)].h, 16 * (tt // 2 + 1))

        @block.vector
        def _(v):
            for tt in range(NT):
                xwait(v, tt)
                vbar(v, v.reduce_sum(c_(0, tt), xstg[tt % 2][:, :], axis=AX))
                v.wait_ge(p0["lns"].h, tt + 1)
                vbar(v, v.tensor_scalar(c_(2, tt), c_(0, tt), 1.0 / D, None,
                                        ALU.mult))
                vbar(v, v.tensor_tensor(c_(3, tt), c_(2, tt), c_(2, tt), ALU.mult))
                vbar(v, v.tensor_scalar(c_(4, tt), c_(1, tt), 1.0 / D, c_(3, tt),
                                        ALU.mult, ALU.subtract))
                vbar(v, v.tensor_scalar(c_(5, tt), c_(4, tt), EPS, None, ALU.add))
                p0["st"].inc(v.reciprocal(c_(6, tt), c_(5, tt)))
                v.wait_ge(p0["rst"].h, tt + 1)
                if tt >= 2:
                    v.wait_ge(p0["tr"].h, 4 * (tt - 1))
                p0["xn"].inc(
                    v.tensor_scalar(xn_tok[tt % 2][:, :], xstg[tt % 2][:, :],
                                    c_(2, tt), c_(7, tt), ALU.subtract, ALU.mult))

        @block.tensor
        def _(t):
            t.wait_ge(dsem["cst"].h, 112)
            for tt in range(NT):
                t.wait_ge(p0["xn"].h, tt + 1)
                for g4 in range(4):
                    idx = tt * 4 + g4
                    if idx >= 2:
                        t.wait_ge(p0["ev"].h, idx - 1)
                    for j in range(4):
                        fc = g4 * 4 + j
                        tr = t.transpose(ps_tr[idx % 2][:, j * 128:(j + 1) * 128],
                                         xn_tok[tt % 2][:, fc * 128:(fc + 1) * 128],
                                         identb[:, :])
                    p0["tr"].inc(tr)

        @block.scalar
        def _(sc):
            for tt in range(NT):
                xwait(sc, tt)
                p0["lns"].inc(
                    sc.activation(sqscr[tt % 2][:, :], xstg[tt % 2][:, :],
                                  AF.Square, accum_out=c_(1, tt)))
                sc.wait_ge(p0["st"].h, tt + 1)
                p0["rst"].inc(sc.activation(c_(7, tt), c_(6, tt), AF.Sqrt))
                if tt >= 1:
                    for g4 in range(4):
                        idx = (tt - 1) * 4 + g4
                        sc.wait_ge(p0["tr"].h, idx + 1)
                        dst = xn_fm[:, :].rearrange("p (f t) -> p f t", f=NF)[
                            :, g4 * 4:(g4 + 1) * 4, (tt - 1) * 128: tt * 128]
                        p0["ev"].inc(sc.copy(
                            dst, ps_tr[idx % 2][:, :].rearrange("p (f t) -> p f t", f=4)))
            for g4 in range(4):
                idx = (NT - 1) * 4 + g4
                sc.wait_ge(p0["tr"].h, idx + 1)
                dst = xn_fm[:, :].rearrange("p (f t) -> p f t", f=NF)[
                    :, g4 * 4:(g4 + 1) * 4, (NT - 1) * 128: NT * 128]
                p0["ev"].inc(sc.copy(
                    dst, ps_tr[idx % 2][:, :].rearrange("p (f t) -> p f t", f=4)))

    es_p0.close()
    es_ptr.close()

    q_sb = sbuf(es_mid, "q_sb", [128, HL * T], BF16, side="right")
    k_sb = sbuf(es_mid, "k_sb", [128, HL * T], BF16, side="right")
    v_sb = sbuf(es_mid, "v_sb", [128, HL * T], BF16, side="right")
    mneg = sbuf(es_mid, "mneg", [128, 128], F32, side="right")
    ssum = sbuf(es_mid, "ssum", [128, 128], F32, side="right")
    es_pq = contextlib.ExitStack()
    ps_q = [psum(es_pq, f"ps_q{i}", [128, 512], F32) for i in range(3)]
    ps_r = psum(es_pq, "ps_r", [128, 512], F32)
    ps_s = [psum(es_pq, f"ps_s{i}", [128, 1024], F32) for i in range(2)]

    CH = [(h, m, ts) for h in range(HL) for m in range(3) for ts in range(4)]
    CIDX = {c: i for i, c in enumerate(CH)}
    RCH = [(h, m, ts) for h in range(HL) for m in range(2) for ts in range(4)]
    RIDX = {c: i for i, c in enumerate(RCH)}
    P1CH = []
    for h in range(HL):
        for qt in range(NT):
            kend = (qt + 1) * 128
            if kend <= 1024:
                P1CH.append((h, qt, 0, 0, kend, True))
            else:
                P1CH.append((h, qt, 0, 0, 1024, False))
                P1CH.append((h, qt, 1, 1024, kend - 1024, False))
    P1OF = {}
    for i, (h, qt, ci, k0, w, sg_) in enumerate(P1CH):
        P1OF[(h, qt, ci)] = i
    P1H = {h: [i for i, c in enumerate(P1CH) if c[0] == h] for h in range(HL)}
    QKV_SB = [q_sb, k_sb, v_sb]

    p1 = {n: S("p1_" + n) for n in ["m", "e", "r", "v", "sm", "vm", "x",
                                    "hb", "he"]}

    with nc.Block("p1s") as block:
        @block.gpsimd
        def _(g):
            for h in range(2, HL):
                g.wait_ge(p1["m"].h, 12 * (h - 1))
                dsem["wq" if h % 2 == 0 else "wk"].inc(g.dma_start(
                    out=wslab[h % 2][:, :].rearrange("p (f c) -> p f c", f=NF),
                    in_=din["wqkv"][h].rearrange("(f p) c -> p f c", p=128)), 16)

        @block.tensor
        def _(t):
            t.wait_ge(dsem["cst"].h, 112)

            def emit_p1(i):
                h, qt, ci, k0, w, single = P1CH[i]
                if i >= 2:
                    t.wait_ge(p1["x"].h, i - 1)
                for j in range((w + 511) // 512):
                    n = min(512, w - j * 512)
                    mm = t.matmul(
                        ps_s[i % 2][:, j * 512: j * 512 + n],
                        q_sb[:, h * T + qt * 128: h * T + (qt + 1) * 128],
                        k_sb[:, h * T + k0 + j * 512: h * T + k0 + j * 512 + n],
                        start=True, stop=True)
                p1["sm"].inc(mm)

            def emit_rot(ri):
                h, m, ts = RCH[ri]
                cq = CIDX[(h, m, ts)]
                t.wait_ge(p1["e"].h, cq + 1)
                if ri >= 1:
                    t.wait_ge(p1["v"].h, ri)
                p1["r"].inc(t.matmul(ps_r[:, :], rmat_sb[:, :],
                                     praw[ri % 2][:, :], start=True, stop=True))

            for cq, (h, m, ts) in enumerate(CH):
                t.wait_ge(p0["ev"].h, 16 * (ts + 1))
                t.wait_ge(dsem["wq" if h % 2 == 0 else "wk"].h,
                          16 * (h // 2 + 1))
                if cq >= 3:
                    t.wait_ge(p1["e"].h, cq - 2)
                for fc in range(NF):
                    mm = t.matmul(
                        ps_q[cq % 3][:, :],
                        wslab[h % 2][:, fc * 3 * HD + m * HD:
                                     fc * 3 * HD + (m + 1) * HD],
                        xn_fm[:, fc * T + ts * 512: fc * T + (ts + 1) * 512],
                        start=(fc == 0), stop=(fc == NF - 1))
                p1["m"].inc(mm)
                if m < 2:
                    ri = RIDX[(h, m, ts)]
                    if ri >= 1 and ri % 8 != 0:
                        emit_rot(ri - 1)
                if m == 2 and ts == 0:
                    emit_rot(RIDX[(h, 1, 3)])
                if h >= 1 and m == 2:
                    t.wait_ge(p1["v"].h, 8 * h)
                    per = len(P1H[h - 1])
                    lo = ts * per // 4
                    hi = (ts + 1) * per // 4
                    for i in P1H[h - 1][lo:hi]:
                        emit_p1(i)
            t.wait_ge(p1["v"].h, 8 * HL)
            for i in P1H[HL - 1]:
                emit_p1(i)

        @block.scalar
        def _(sc):
            nev = [0]

            def emit_exp(i):
                h, qt, ci, k0, w, single = P1CH[i]
                sc.wait_ge(p1["vm"].h, i + 1)
                j = qt - 8
                if single:
                    bias = mneg[:, h * 16 + qt: h * 16 + qt + 1]
                    acc = ssum[:, h * 16 + qt: h * 16 + qt + 1]
                elif ci == 0:
                    bias = p1stat[:, j: j + 1]
                    acc = p1stat[:, 16 + j: 16 + j + 1]
                else:
                    bias = mneg[:, h * 16 + qt: h * 16 + qt + 1]
                    acc = p1stat[:, 24 + j: 24 + j + 1]
                p1["x"].inc(sc.activation(pscr1[:, 0:w], ps_s[i % 2][:, 0:w],
                                          AF.Exp, bias=bias, accum_out=acc))

            def emit_head_batch(h):
                sc.wait_ge(p1["hb"].h, h + 1)
                p1["he"].inc(sc.activation(p1stat[:, 40:48], p1stat[:, 32:40],
                                           AF.Exp))

            for cq, (h, m, ts) in enumerate(CH):
                sc.wait_ge(p1["m"].h, cq + 1)
                if m < 2:
                    ri = RIDX[(h, m, ts)]
                    if ri >= 2:
                        sc.wait_ge(p1["v"].h, ri - 1)
                    p1["e"].inc(sc.activation(praw[ri % 2][:, :],
                                              ps_q[cq % 3][:, :], AF.Copy,
                                              scale=SCALE if m == 0 else 1.0))
                else:
                    p1["e"].inc(sc.activation(
                        v_sb[:, h * T + ts * 512: h * T + (ts + 1) * 512],
                        ps_q[cq % 3][:, :], AF.Copy))
                if h >= 1 and m == 2:
                    per = len(P1H[h - 1])
                    lo = ts * per // 4
                    hi = (ts + 1) * per // 4
                    for i in P1H[h - 1][lo:hi]:
                        emit_exp(i)
                    if ts == 3:
                        emit_head_batch(h - 1)
            for i in P1H[HL - 1]:
                emit_exp(i)
            emit_head_batch(HL - 1)

        @block.vector
        def _(v):
            v.wait_ge(dsem["cst"].h, 112)
            nrope = [0]

            def emit_rope_v(ri):
                h, m, ts = RCH[ri]
                v.wait_ge(p1["r"].h, ri + 1)
                v.tensor_tensor(t2s[:, :], ps_r[:, :],
                                sin_sb[:, ts * 512:(ts + 1) * 512], ALU.mult)
                vbar(v, v.tensor_tensor(t1s[:, :], praw[ri % 2][:, :],
                                        cos_sb[:, ts * 512:(ts + 1) * 512],
                                        ALU.mult))
                p1["v"].inc(v.tensor_tensor(
                    QKV_SB[m][:, h * T + ts * 512: h * T + (ts + 1) * 512],
                    t1s[:, :], t2s[:, :], ALU.add))

            def emit_stats(i):
                h, qt, ci, k0, w, single = P1CH[i]
                v.wait_ge(p1["sm"].h, i + 1)
                j = qt - 8
                last = single or ci == 1
                if last:
                    vbar(v, v.tensor_tensor(ps_s[i % 2][:, w - 128:w],
                                            ps_s[i % 2][:, w - 128:w],
                                            maskS_sb[:, :], ALU.add))
                if single:
                    dst = mneg[:, h * 16 + qt: h * 16 + qt + 1]
                    p1["vm"].inc(v.reduce_max(dst, ps_s[i % 2][:, 0:w],
                                              axis=AX, negate=True))
                elif ci == 0:
                    dst = p1stat[:, j: j + 1]
                    p1["vm"].inc(v.reduce_max(dst, ps_s[i % 2][:, 0:w],
                                              axis=AX, negate=True))
                else:
                    vbar(v, v.reduce_max(p1stat[:, 8 + j: 8 + j + 1],
                                         ps_s[i % 2][:, 0:w], axis=AX,
                                         negate=True))
                    p1["vm"].inc(v.tensor_tensor(
                        mneg[:, h * 16 + qt: h * 16 + qt + 1],
                        p1stat[:, j: j + 1], p1stat[:, 8 + j: 8 + j + 1],
                        ALU.min))

            def emit_head_batch_v(h):
                v.wait_ge(p1["vm"].h, p1["vm"].v)
                vbar(v, v.tensor_tensor(p1stat[:, 32:40],
                                        mneg[:, h * 16 + 8: h * 16 + 16],
                                        p1stat[:, 0:8], ALU.subtract))
                p1["hb"].inc(v.nop())
                v.wait_ge(p1["he"].h, h + 1)
                vbar(v, v.tensor_tensor(p1stat[:, 48:56], p1stat[:, 16:24],
                                        p1stat[:, 40:48], ALU.mult))
                vbar(v, v.tensor_tensor(ssum[:, h * 16 + 8: h * 16 + 16],
                                        p1stat[:, 48:56], p1stat[:, 24:32],
                                        ALU.add))

            for cq, (h, m, ts) in enumerate(CH):
                if m < 2:
                    ri = RIDX[(h, m, ts)]
                    emit_rope_v(ri)
                if h >= 1 and m == 2:
                    per = len(P1H[h - 1])
                    lo = ts * per // 4
                    hi = (ts + 1) * per // 4
                    for i in P1H[h - 1][lo:hi]:
                        emit_stats(i)
                    if ts == 3:
                        emit_head_batch_v(h - 1)
            for i in P1H[HL - 1]:
                emit_stats(i)
            emit_head_batch_v(HL - 1)

    es_pq.close()
    es_p1s.close()
    es_rope.close()

    abuf = [sbuf(es_mid, f"abuf{i}", [128, 512], BF16, side="right")
            for i in range(2)]
    pscr = sbuf(es_mid, "pscr", [128, T], BF16, side="right")
    lns = sbuf(es_mid, "lns", [128, 128], F32, side="right")
    bh = sbuf(es_mid, "bh", [128, 128], F32, side="right")
    brow = sbuf(es_mid, "brow", [1, T], F32, side="right")
    eexp = [sbuf(es_mid, f"eexp{i}", [1, 512], BF16, side="right") for i in range(2)]
    onesr_bf = sbuf(es_mid, "onesr_bf", [1, 128], BF16, side="right")
    expb = [sbuf(es_mid, f"expb{i}", [128, 512], BF16, side="right") for i in range(2)]
    mexpb = [sbuf(es_mid, f"mexpb{i}", [128, 512], BF16, side="right") for i in range(2)]

    ccsem = S("ccsem")
    p2 = {n: S("p2_" + n) for n in ["s1", "msk", "exp", "st", "tB", "eB",
                                    "dg", "pt", "avd", "ae", "eb", "bc",
                                    "ebe", "pb", "vt", "vte"]}

    def kchunks(qt):
        kend = (qt + 1) * 128
        return [(kc * 512, min(512, kend - kc * 512))
                for kc in range((kend + 511) // 512)]

    EV2 = [(h, qs, kt) for h in range(HL) for qs in range(4)
           for kt in range(4 * qs + 4)]
    EIDX = {e: i for i, e in enumerate(EV2)}
    LASTE = {(h, qs): EIDX[(h, qs, 4 * qs + 3)] for h in range(HL)
             for qs in range(4)}
    DGC = np.cumsum([1 if kt >= 4 * qs else 0 for (h, qs, kt) in EV2]).tolist()
    brow_done = {}
    adma_done = {}

    with nc.Block("p2") as block:
        es_ps2 = contextlib.ExitStack()
        ps_b = psum(es_ps2, "ps_b", [128, 128], F32)
        ps_vt = psum(es_ps2, "ps_vt", [128, 512], BF16)

        @block.sync
        def _(s):
            s.wait_ge(p0["ev"].h, 64)
            dsem["xnb"].inc(s.dma_start(out=xn_dram[:, :], in_=xn_fm[:, :]), 16)

        @block.scalar
        def _(sc):
            p2["st"].inc(sc.activation(lns[:, :], ssum[:, :], AF.Ln))

        @block.vector
        def _(v):
            v.wait_ge(p2["st"].h, 1)
            p2["st"].inc(v.tensor_tensor(bh[:, :], mneg[:, :], lns[:, :],
                                         ALU.subtract))

        @block.tensor
        def _(t):
            t.wait_ge(p2["st"].h, 2)
            t.wait_ge(dsem["cs2"].h, 48)
            p2["tB"].inc(t.transpose(ps_b[:, :], bh[:, :], identf[:, :]))
            for h in range(HL):
                for g4 in range(4):
                    gi = h * 4 + g4
                    if gi >= 1:
                        t.wait_ge(p2["vte"].h, gi)
                    for j in range(4):
                        kt = g4 * 4 + j
                        tr = t.transpose(ps_vt[:, j * 128:(j + 1) * 128],
                                         v_sb[:, h * T + kt * 128:
                                              h * T + (kt + 1) * 128],
                                         identb[:, :])
                    p2["vt"].inc(tr)

        @block.scalar
        def _(sc):
            for gi in range(32):
                h, g4 = gi // 4, gi % 4
                sc.wait_ge(p2["vt"].h, gi + 1)
                p2["vte"].inc(sc.copy(
                    v_sb[:, h * T + g4 * 512: h * T + (g4 + 1) * 512],
                    ps_vt[:, :]))

        @block.vector
        def _(v):
            v.wait_ge(p2["tB"].h, 1)
            p2["eB"].inc(v.tensor_copy(bh[:, :], ps_b[:, :]))

        @block.sync
        def _(s):
            s.wait_ge(p2["eB"].h, 1)
            dsem["brow"].inc(s.dma_start(out=brows_dram[:, :], in_=bh[:, :]), 16)
            s.wait_ge(dsem["brow"].h, 16)

            def emit_chunk(hh, qs):
                gidx = hh * 4 + qs
                s.wait_ge(p2["ae"].h, gidx + 1)
                dsem["at%d" % (gidx % 2)].inc(s.dma_start(
                    out=attn_dram[hh, :, qs * 512:(qs + 1) * 512],
                    in_=abuf[gidx % 2][:, :]), 16)

            for h in range(HL):
                if h >= 1:
                    for qs in range(4):
                        emit_chunk(h - 1, qs)
                    s.wait_ge(p2["dg"].h, LASTE[(h - 1, 3)] + 1)
                    if DEBUG and h == 1:
                        s.wait_ge(dsem["dbg2"].h, 64)
                dsem["brow"].inc(s.dma_start(
                    out=brow[:, :],
                    in_=brows_dram[:, :].rearrange("(h r) c -> h (r c)", h=HL)[
                        h:h + 1, :]), 16)
            for qs in range(4):
                emit_chunk(HL - 1, qs)

        @block.gpsimd
        def _(g):
            dsem["cst"].inc(g.dma_start(out=onesr_bf[:, :], in_=din["ones_row"][:, :]), 16)
            if DEBUG:
                g.wait_ge(p2["pb"].h, 1)
                dsem["dbg2"].inc(g.dma_start(out=dbg["eexp0"].ap(), in_=eexp[0][:, :]), 16)
                dsem["dbg2"].inc(g.dma_start(out=dbg["expb0"].ap(), in_=expb[0][:, :]), 16)
                dsem["dbg2"].inc(g.dma_start(out=dbg["mexpb0"].ap(), in_=mexpb[0][:, :]), 16)
                dsem["dbg2"].inc(g.dma_start(out=dbg["brow0"].ap(), in_=brow[:, :]), 16)
                g.wait_ge(p2["ae"].h, 1)
                dsem["dbg2"].inc(g.dma_start(out=dbg["abuf0"].ap(), in_=abuf[0][:, :]), 16)
            pid = g.partition_id()
            oth = (1 - pid % 2) * 1024
            g.wait_ge(dsem["at0"].h, 16 * 16)
            g.wait_ge(dsem["at1"].h, 16 * 16)
            sg.inc(g.dma_start(out=cc_in[:, :, :],
                               in_=attn_dram[:, :, bass.ds(oth, 1024)]), 16)
            g.wait_ge(sg.h, sg.v)
            ccsem.inc(g.collective_compute(
                "AllGather", ALU.bypass, replica_groups=groups,
                ins=[cc_in.ap().opt()], outs=[cc_out.ap().opt()]))
            if DEBUG:
                sg.inc(g.dma_start(out=dbg["mneg"].ap(), in_=mneg[:, :]), 16)
                sg.inc(g.dma_start(out=dbg["ssum"].ap(), in_=ssum[:, :]), 16)
                sg.inc(g.dma_start(out=dbg["q"].ap(), in_=q_sb[:, :]), 16)
                sg.inc(g.dma_start(out=dbg["k"].ap(), in_=k_sb[:, :]), 16)
                sg.inc(g.dma_start(out=dbg["v"].ap(), in_=v_sb[:, :]), 16)
                sg.inc(g.dma_start(out=dbg["pscr"].ap(), in_=pscr[:, :]), 16)
                g.wait_ge(sg.h, sg.v)

        es_ps2b = contextlib.ExitStack()
        ps_st = [psum(es_ps2b, f"ps_st{i}", [128, 512], F32) for i in range(3)]
        ps_av = [psum(es_ps2b, f"ps_av{i}", [128, 512], F32) for i in range(2)]
        ps_bc = [psum(es_ps2b, "ps_bc0", [128, 512], F32)] * 2

        @block.tensor
        def _(t):
            def OFF(qs_, kt_):
                return max(0, kt_ - 4 * qs_) * 128

            t.wait_ge(dsem["cst"].h, 128)
            next_av = [0]

            def emit_av():
                j = next_av[0]
                jh, jqs, jkt = EV2[j]
                jo = OFF(jqs, jkt)
                t.wait_ge(p2["vte"].h, 4 * (jh + 1))
                t.wait_ge(p2["pb"].h, j + 1)
                p2["avd"].inc(t.matmul(
                    ps_av[(jh * 4 + jqs) % 2][:, jo:512],
                    v_sb[:, jh * T + jkt * 128: jh * T + (jkt + 1) * 128],
                    pscr[:, (j % 4) * 512 + jo:(j % 4 + 1) * 512],
                    start=(jkt == 0), stop=(jkt == 4 * jqs + 3)))
                next_av[0] += 1

            for i, (h, qs, kt) in enumerate(EV2):
                if kt == 0:
                    while next_av[0] < i:
                        emit_av()
                    gidx = h * 4 + qs
                    if gidx >= 2:
                        t.wait_ge(p2["ae"].h, gidx - 1)
                    t.wait_ge(p2["eb"].h, gidx + 1)
                    if gidx >= 1:
                        t.wait_ge(p2["ebe"].h, gidx)
                    p2["bc"].inc(t.matmul(ps_bc[gidx % 2][:, :], onesr_bf[:, :],
                                          eexp[gidx % 2][:, :],
                                          start=True, stop=True))
                if i >= 3:
                    t.wait_ge(p2["pt"].h, i - 2)
                off = OFF(qs, kt)
                p2["dg"].inc(t.matmul(
                    ps_st[i % 3][:, off:512],
                    k_sb[:, h * T + kt * 128: h * T + (kt + 1) * 128],
                    q_sb[:, h * T + qs * 512 + off: h * T + (qs + 1) * 512],
                    start=True, stop=True))
                while i - next_av[0] >= 3:
                    emit_av()
            while next_av[0] < len(EV2):
                emit_av()

        @block.vector
        def _(v):
            def OFF(qs_, kt_):
                return max(0, kt_ - 4 * qs_) * 128

            tri_b = bass.AP(maskST_sb[:, :].tensor, 0,
                            [[128, 128], [0, 4], [1, 128]])

            def biasmul(i):
                jh, jqs, jkt = EV2[i]
                jo = OFF(jqs, jkt)
                jg = jh * 4 + jqs
                v.wait_ge(p2["pt"].h, i + 1)
                qsl = slice((i % 4) * 512 + jo, (i % 4 + 1) * 512)
                if jkt >= 4 * jqs:
                    mtt = v.tensor_tensor(
                        pscr[:, (i % 4) * 512 + jo:(i % 4) * 512 + jo + 128],
                        pscr[:, (i % 4) * 512 + jo:(i % 4) * 512 + jo + 128],
                        mexpb[jg % 2][:, jo:jo + 128], ALU.mult)
                    if jo + 128 < 512:
                        p2["pb"].inc(v.tensor_tensor(
                            pscr[:, (i % 4) * 512 + jo + 128:(i % 4 + 1) * 512],
                            pscr[:, (i % 4) * 512 + jo + 128:(i % 4 + 1) * 512],
                            expb[jg % 2][:, jo + 128:512], ALU.mult))
                    else:
                        p2["pb"].inc(mtt)
                else:
                    p2["pb"].inc(v.tensor_tensor(
                        pscr[:, qsl], pscr[:, qsl],
                        expb[jg % 2][:, jo:512], ALU.mult))

            for i, (h, qs, kt) in enumerate(EV2):
                if i >= 1:
                    biasmul(i - 1)
                if kt == 0:
                    gidx = h * 4 + qs
                    v.wait_ge(p2["ebe"].h, gidx + 1)
                    v.tensor_tensor(
                        mexpb[gidx % 2][:, :].rearrange("p (a b) -> p a b", a=4),
                        expb[gidx % 2][:, :].rearrange("p (a b) -> p a b", a=4),
                        tri_b, ALU.mult)
            biasmul(len(EV2) - 1)

        @block.scalar
        def _(sc):
            for i, (h, qs, kt) in enumerate(EV2):
                gidx = h * 4 + qs
                if kt == 0:
                    sc.wait_ge(dsem["brow"].h, 16 * (h + 2))
                    if DEBUG and gidx == 2:
                        sc.wait_ge(dsem["dbg2"].h, 80)
                    if gidx >= 2:
                        g2 = gidx - 2
                        sc.wait_ge(p2["pb"].h, LASTE[(g2 // 4, g2 % 4)] + 1)
                        sc.wait_ge(p2["bc"].h, gidx - 1)
                    p2["eb"].inc(sc.activation(eexp[gidx % 2][:, :],
                                               brow[0:1, qs * 512:(qs + 1) * 512],
                                               AF.Exp))
                    sc.wait_ge(p2["bc"].h, gidx + 1)
                    p2["ebe"].inc(sc.copy(expb[gidx % 2][:, :],
                                          ps_bc[gidx % 2][:, :]))
                sc.wait_ge(p2["dg"].h, i + 1)
                if i >= 4:
                    sc.wait_ge(p2["avd"].h, i - 3)
                off = max(0, kt - 4 * qs) * 128
                p2["pt"].inc(sc.activation(
                    pscr[:, (i % 4) * 512 + off:(i % 4 + 1) * 512],
                    ps_st[i % 3][:, off:512], AF.Exp))
                if kt == 4 * qs + 3:
                    sc.wait_ge(p2["avd"].h, i + 1)
                    if gidx >= 2:
                        sc.wait_ge(dsem["at%d" % (gidx % 2)].h,
                                   16 * ((gidx - 2) // 2 + 1))
                    p2["ae"].inc(sc.copy(abuf[gidx % 2][:, :],
                                         ps_av[gidx % 2][:, :]))

        es_ps2b.close()
        es_ps2.close()

    es_mid.close()
    es_p2c.close()


    es_p5l = contextlib.ExitStack()
    attn_full = sbuf(es_p5l, "attn_full", [128, 16 * 1024], BF16)
    at_sb = sbuf(es_p5l, "at_sb", [128, 8 * 1024], BF16)
    wout_sb = [sbuf(es_p5l, f"wout_sb{i}", [128, 16 * 512], BF16) for i in range(2)]
    wb_sb = [sbuf(es_p5l, f"wb_sb{i}", [128, 8 * 512], BF16) for i in range(2)]
    wdone = {}
    es_p34 = contextlib.ExitStack()
    xn_own = sbuf(es_p34, "xn_own", [128, NF * 1024], BF16)
    wa_sb = sbuf(es_p34, "wa_sb", [128, NF * 1024], BF16)
    rw_sb = sbuf(es_p34, "rw_sb", [128, 128], BF16)
    as_sb = sbuf(es_p34, "as_sb", [128, 8 * 1024], BF16)
    sqb = [sbuf(es_p34, f"sqb{i}", [128, 1024], BF16) for i in range(2)]
    ssrow = sbuf(es_p34, "ssrow", [1, 1024], F32)
    nrow = sbuf(es_p34, "nrow", [1, 1024], F32)
    rsrow = sbuf(es_p34, "rsrow", [1, 1024], F32)
    rscol = sbuf(es_p34, "rscol", [128, 8], F32)
    lcol = sbuf(es_p34, "lcol", [128, 8], F32)
    ecol = sbuf(es_p34, "ecol", [128, 8], F32)
    mrcol = sbuf(es_p34, "mrcol", [128, 8], F32)
    indcol = sbuf(es_p34, "indcol", [128, 8], F32)
    cmb = sbuf(es_p34, "cmb", [128, 64], F32)
    tstat = sbuf(es_p34, "tstat", [128, 8], F32)

    p3 = {n: S("p3_" + n) for n in ["sq", "ss", "se", "rs", "rc", "lg", "le",
                                    "tp", "am", "ae", "at", "ate"]}

    with nc.Block("p34") as block:
        @block.gpsimd
        def _(g):
            pid = g.partition_id()
            own = (pid % 2) * 1024
            partner = 1 - (pid % 2)
            g.wait_ge(dsem["xnb"].h, 16)
            dsem["xno"].inc(g.dma_start(
                out=xn_own[:, :].rearrange("p (f t) -> p f t", f=NF),
                in_=xn_dram[:, :].rearrange("p (f t) -> p f t", f=NF)[
                    :, :, bass.ds(own, 1024)]), 16)
            dsem["wa"].inc(g.dma_start(
                out=wa_sb[:, :].rearrange("p (f c) -> p f c", f=NF),
                in_=din["wa"][:, :].rearrange("(f p) c -> p f c", p=128)), 16)
            dsem["rw"].inc(g.dma_start(
                out=rw_sb[:, 0:NF * 8].rearrange("p (f c) -> p f c", f=NF),
                in_=din["router"][:, :, :].rearrange("f p c -> p f c")), 16)
            g.wait_ge(dsem["at0"].h, 16 * 16)
            g.wait_ge(dsem["at1"].h, 16 * 16)
            dsem["ful"].inc(g.dma_start(
                out=attn_full[:, 0:HL * 1024].rearrange("p (h t) -> p h t", h=HL),
                in_=attn_dram[:, :, bass.ds(own, 1024)].rearrange(
                    "h p t -> p h t")), 16)
            for os in range(2):
                dsem["wo%d" % os].inc(g.dma_start(
                    out=wout_sb[os][:, :].rearrange("p (gi c) -> p gi c", gi=16),
                    in_=din["wout"][os].rearrange("(gi p) c -> p gi c", p=128)), 16)
                dsem["wo%d" % os].inc(g.dma_start(
                    out=wb_sb[os][:, :].rearrange("p (e c) -> p e c", e=8),
                    in_=din["wb"][os].rearrange("e p c -> p e c")), 16)
            g.wait_ge(ccsem.h, 1)
            dsem["ful"].inc(g.dma_start(
                out=attn_full[:, HL * 1024:].rearrange("p (h t) -> p h t", h=HL),
                in_=cc_out[partner].rearrange("h p t -> p h t")), 16)

        @block.scalar
        def _(sc):
            sc.wait_ge(dsem["xno"].h, 16)
            for fc in range(NF):
                if fc >= 2:
                    sc.wait_ge(p3["ss"].h, fc - 1)
                p3["sq"].inc(sc.activation(sqb[fc % 2][:, :],
                                           xn_own[:, fc * 1024:(fc + 1) * 1024],
                                           AF.Square))

        es_ps34 = contextlib.ExitStack()
        ps_ss = [psum(es_ps34, f"ps_ss{i}", [1, 512], F32) for i in range(2)]
        ps_rc = psum(es_ps34, "ps_rc", [128, 8], F32)
        ps_lg = [psum(es_ps34, f"ps_lg{i}", [128, 8], F32) for i in range(2)]
        ps_a = [psum(es_ps34, f"ps_a{i}", [128, 512], F32) for i in range(2)]
        ps_t4 = psum(es_ps34, "ps_t4", [128, 512], BF16)

        @block.tensor
        def _(t):
            for fc in range(NF):
                t.wait_ge(p3["sq"].h, fc + 1)
                for th in range(2):
                    mm = t.matmul(ps_ss[th][:, :], onesc_sb[:, :],
                                  sqb[fc % 2][:, th * 512:(th + 1) * 512],
                                  start=(fc == 0), stop=(fc == NF - 1))
                p3["ss"].inc(mm)
            t.wait_ge(p3["rs"].h, 1)
            for tt in range(8):
                mm = t.matmul(ps_rc[:, tt:tt + 1],
                              rsrow[0:1, tt * 128:(tt + 1) * 128],
                              one1_sb[:, :], start=(tt == 0), stop=(tt == 7))
            p3["rc"].inc(mm)
            t.wait_ge(dsem["rw"].h, 16)
            for tt in range(8):
                if tt >= 2:
                    t.wait_ge(p3["le"].h, tt - 1)
                for fc in range(NF):
                    mm = t.matmul(ps_lg[tt % 2][:, :],
                                  xn_own[:, fc * 1024 + tt * 128:
                                         fc * 1024 + (tt + 1) * 128],
                                  rw_sb[:, fc * 8:(fc + 1) * 8],
                                  start=(fc == 0), stop=(fc == NF - 1))
                p3["lg"].inc(mm)
            t.wait_ge(dsem["wa"].h, 16)
            for tt in range(8):
                for eg in range(2):
                    aidx = tt * 2 + eg
                    if aidx >= 2:
                        t.wait_ge(p3["ae"].h, aidx - 1)
                    for fc in range(NF):
                        mm = t.matmul(ps_a[aidx % 2][:, :],
                                      xn_own[:, fc * 1024 + tt * 128:
                                             fc * 1024 + (tt + 1) * 128],
                                      wa_sb[:, fc * 1024 + eg * 512:
                                            fc * 1024 + (eg + 1) * 512],
                                      start=(fc == 0), stop=(fc == NF - 1))
                    p3["am"].inc(mm)
            for e in range(8):
                for tg in range(2):
                    tidx = e * 2 + tg
                    need_aidx = (tg * 4 + 3) * 2 + (e // 4)
                    t.wait_ge(p3["ae"].h, need_aidx + 1)
                    if tidx >= 1:
                        t.wait_ge(p3["ate"].h, tidx)
                    for j in range(4):
                        tt = tg * 4 + j
                        tr = t.transpose(ps_t4[:, j * 128:(j + 1) * 128],
                                         as_sb[:, tt * 1024 + e * 128:
                                               tt * 1024 + (e + 1) * 128],
                                         identb[:, :])
                    p3["at"].inc(tr)

        @block.vector
        def _(v):
            v.wait_ge(p3["ss"].h, NF)
            v.tensor_copy(ssrow[:, 0:512], ps_ss[0][:, :])
            p3["se"].inc(v.tensor_copy(ssrow[:, 512:1024], ps_ss[1][:, :]))

        @block.scalar
        def _(sc):
            sc.wait_ge(p3["se"].h, 1)
            p3["se"].inc(sc.activation(nrow[:, :], ssrow[:, :], AF.Sqrt))

        @block.vector
        def _(v):
            v.wait_ge(p3["se"].h, 2)
            vbar(v, v.tensor_scalar(ssrow[:, :], nrow[:, :], 1e-12, None,
                                    ALU.max))
            p3["rs"].inc(v.reciprocal(rsrow[:, :], ssrow[:, :]))
            v.wait_ge(p3["rc"].h, 1)
            for tt in range(8):
                cp = v.tensor_copy(rscol[:, tt:tt + 1], ps_rc[:, tt:tt + 1])
            vbar(v, cp)
            for tt in range(8):
                v.wait_ge(p3["lg"].h, tt + 1)
                vbar(v, v.tensor_scalar(lcol[:, :], ps_lg[tt % 2][:, :],
                                        rscol[:, tt:tt + 1], None, ALU.mult))
                p3["le"].inc(v.nop())
                p3["tp"].inc(v.reduce_max(tstat[:, 0:1], lcol[:, :],
                                          axis=AX, negate=True))
                v.wait_ge(p3["tp"].h, 2 * (tt + 1))
                vbar(v, v.reduce_max(tstat[:, 1:2], ecol[:, :], axis=AX))
                vbar(v, v.tensor_single_scalar(mrcol[:, :], ecol[:, :],
                                               tstat[:, 1:2], ALU.is_lt))
                vbar(v, v.tensor_tensor(mrcol[:, :], mrcol[:, :], ecol[:, :],
                                        ALU.mult))
                vbar(v, v.reduce_max(tstat[:, 2:3], mrcol[:, :], axis=AX))
                vbar(v, v.tensor_single_scalar(indcol[:, :], ecol[:, :],
                                               tstat[:, 2:3], ALU.is_ge))
                vbar(v, v.tensor_tensor(indcol[:, :], indcol[:, :], ecol[:, :],
                                        ALU.mult))
                vbar(v, v.reduce_sum(tstat[:, 3:4], indcol[:, :], axis=AX))
                vbar(v, v.reciprocal(tstat[:, 4:5], tstat[:, 3:4]))
                vbar(v, v.tensor_scalar(cmb[:, tt * 8:(tt + 1) * 8],
                                        indcol[:, :], tstat[:, 4:5], None,
                                        ALU.mult))
            for tt in range(8):
                for eg in range(2):
                    aidx = tt * 2 + eg
                    v.wait_ge(p3["am"].h, aidx + 1)
                    for j in range(4):
                        e = eg * 4 + j
                        ts_ = v.tensor_scalar(
                            as_sb[:, tt * 1024 + e * 128:tt * 1024 + (e + 1) * 128],
                            ps_a[aidx % 2][:, j * 128:(j + 1) * 128],
                            cmb[:, tt * 8 + e: tt * 8 + e + 1], None, ALU.mult)
                    p3["ae"].inc(ts_)

        @block.scalar
        def _(sc):
            for tt in range(8):
                sc.wait_ge(p3["tp"].h, 2 * tt + 1)
                p3["tp"].inc(sc.activation(ecol[:, :], lcol[:, :], AF.Exp,
                                           bias=tstat[:, 0:1]))
            for e in range(8):
                for tg in range(2):
                    tidx = e * 2 + tg
                    sc.wait_ge(p3["at"].h, tidx + 1)
                    p3["ate"].inc(sc.copy(
                        at_sb[:, e * 1024 + tg * 512: e * 1024 + (tg + 1) * 512],
                        ps_t4[:, :]))

        es_ps34.close()

    es_p34.close()

    es_p5 = contextlib.ExitStack()
    x_own = sbuf(es_p5, "x_own", [128, 8 * D], F32)
    gscr = [sbuf(es_p5, f"gscr{i}", [128, 512], F32) for i in range(2)]
    sgscr = [sbuf(es_p5, f"sgscr{i}", [128, 512], F32) for i in range(2)]
    oscr = [sbuf(es_p5, f"oscr{i}", [128, 512], F32) for i in range(2)]
    p5 = {n: S("p5_" + n) for n in ["s", "y", "sil", "c"]}
    odone = {}

    with nc.Block("p5") as block:
        es_ps5 = contextlib.ExitStack()
        ps_s5 = [psum(es_ps5, f"ps_s5{i}", [128, 512], F32) for i in range(2)]
        ps_y5 = [psum(es_ps5, f"ps_y5{i}", [128, 512], F32) for i in range(2)]

        @block.gpsimd
        def _(g):
            pid = g.partition_id()
            xb3 = din["xb"][:, :].rearrange("(a r) c -> a r c", a=2)
            dsem["xo"].inc(g.dma_start(
                out=x_own[:, :].rearrange("p (tt c) -> p tt c", tt=8),
                in_=xb3[pid % 2].rearrange("(tt p) c -> p tt c", p=128)), 16)
            for os in range(2, 4):
                g.wait_ge(p5["s"].h, 8 * (os - 1))
                g.wait_ge(p5["y"].h, 8 * (os - 1))
                dsem["wo%d" % (os % 2)].inc(g.dma_start(
                    out=wout_sb[os % 2][:, :].rearrange("p (gi c) -> p gi c", gi=16),
                    in_=din["wout"][os].rearrange("(gi p) c -> p gi c", p=128)), 16)
                dsem["wo%d" % (os % 2)].inc(g.dma_start(
                    out=wb_sb[os % 2][:, :].rearrange("p (e c) -> p e c", e=8),
                    in_=din["wb"][os].rearrange("e p c -> p e c")), 16)
            g.wait_ge(dsem["xo"].h, 16)

        @block.tensor
        def _(t):
            t.wait_ge(dsem["ful"].h, 32)
            t.wait_ge(p3["ate"].h, 16)
            for os in range(4):
                t.wait_ge(dsem["wo%d" % (os % 2)].h, 32 * (os // 2 + 1))
                for tt in range(8):
                    idx = os * 8 + tt
                    if idx >= 2:
                        t.wait_ge(p5["c"].h, idx - 1)
                        t.wait_ge(p5["sil"].h, idx - 1)
                    for gi in range(16):
                        mm = t.matmul(ps_s5[idx % 2][:, :],
                                      attn_full[:, gi * 1024 + tt * 128:
                                                gi * 1024 + (tt + 1) * 128],
                                      wout_sb[os % 2][:, gi * 512:(gi + 1) * 512],
                                      start=(gi == 0), stop=(gi == 15))
                    p5["s"].inc(mm)
                    for e in range(8):
                        mm = t.matmul(ps_y5[idx % 2][:, :],
                                      at_sb[:, e * 1024 + tt * 128:
                                            e * 1024 + (tt + 1) * 128],
                                      wb_sb[os % 2][:, e * 512:(e + 1) * 512],
                                      start=(e == 0), stop=(e == 7))
                    p5["y"].inc(mm)

        @block.scalar
        def _(sc):
            for os in range(4):
                for tt in range(8):
                    idx = os * 8 + tt
                    sc.wait_ge(p5["y"].h, idx + 1)
                    if idx >= 2:
                        sc.wait_ge(p5["c"].h, idx - 1)
                    p5["sil"].inc(sc.activation(gscr[idx % 2][:, :],
                                                ps_y5[idx % 2][:, :], AF.Silu))

        @block.sync
        def _(s):
            for os in range(4):
                for tt in range(8):
                    idx = os * 8 + tt
                    s.wait_ge(p5["c"].h, idx + 1)
                    dsem["out%d" % (idx % 2)].inc(s.dma_start(
                        out=out_d[tt * 128:(tt + 1) * 128, os * 512:(os + 1) * 512],
                        in_=oscr[idx % 2][:, :]), 16)
            s.wait_ge(dsem["out0"].h, 16 * 16)
            s.wait_ge(dsem["out1"].h, 16 * 16)

        @block.vector
        def _(v):
            v.wait_ge(dsem["xo"].h, 16)
            for os in range(4):
                for tt in range(8):
                    idx = os * 8 + tt
                    v.wait_ge(p5["s"].h, idx + 1)
                    v.wait_ge(p5["sil"].h, idx + 1)
                    if idx >= 2:
                        v.wait_ge(dsem["out%d" % (idx % 2)].h,
                                  16 * ((idx - 2) // 2 + 1))
                    vbar(v, v.scalar_tensor_tensor(sgscr[idx % 2][:, :],
                                                   ps_s5[idx % 2][:, :], 1.0,
                                                   gscr[idx % 2][:, :],
                                                   ALU.mult, ALU.mult))
                    p5["c"].inc(v.tensor_tensor(
                        oscr[idx % 2][:, :], sgscr[idx % 2][:, :],
                        x_own[:, tt * D + os * 512: tt * D + (os + 1) * 512],
                        ALU.add))

        if DEBUG:
            @block.gpsimd
            def _(g):
                g.wait_ge(sd.h, sd.v)
                sg.inc(g.dma_start(out=dbg["attn"].ap(), in_=attn_dram.ap()), 16)
                sg.inc(g.dma_start(out=dbg["brows"].ap(), in_=brows_dram.ap()), 16)
                sg.inc(g.dma_start(out=dbg["xn"].ap(), in_=xn_dram.ap()), 16)
                sg.inc(g.dma_start(out=dbg["ccout"].ap(), in_=cc_out.ap()), 16)
                g.wait_ge(sg.h, sg.v)

        es_ps5.close()
    es_p5.close()
    es_p5l.close()
    return nc


def _host_inputs(x, w_qkv, w_out, router_w, Wa, Wb):
    import jax
    import jax.numpy as jnp
    x = np.asarray(x, dtype=np.float32)
    w_qkv = np.asarray(w_qkv, dtype=np.float32)
    w_out = np.asarray(w_out, dtype=np.float32)
    router_w = np.asarray(router_w, dtype=np.float32)
    Wa = np.asarray(Wa, dtype=np.float32)
    Wb = np.asarray(Wb, dtype=np.float32)

    with jax.default_device(jax.devices("cpu")[0]):
        inv = jnp.asarray(1.0) / jnp.asarray(10000.0) ** (
            jnp.arange(0, HD, 2, dtype=jnp.float32) / HD)
        tv = jnp.arange(T, dtype=jnp.float32)
        freqs = jnp.einsum("i,j->ij", tv, inv)
        emb = jnp.concatenate([freqs, freqs], axis=-1)
        cosf = np.ascontiguousarray(np.asarray(jnp.cos(emb)).T)
        sinf = np.ascontiguousarray(np.asarray(jnp.sin(emb)).T)

    R = np.zeros((128, 128), dtype=np.float32)
    half = np.arange(64)
    R[half, half + 64] = -1.0
    R[half + 64, half] = 1.0
    rmat = np.ascontiguousarray(R.T)

    ii, jj = np.indices((128, 128))
    maskS = np.where(jj <= ii, 0.0, NEG).astype(np.float32)
    maskST = np.where(ii <= jj, 1.0, 0.0).astype(np.float32)
    identf = np.eye(128, dtype=np.float32)

    wa_arr = np.ascontiguousarray(Wa.transpose(1, 0, 2).reshape(D, 1024))
    wb_arr = np.ascontiguousarray(Wb.reshape(8, 128, 4, 512).transpose(2, 0, 1, 3))
    router_arr = np.ascontiguousarray(router_w.reshape(16, 128, 8))
    woutT = w_out.T

    in_maps = []
    for c in range(8):
        b, hg = c // 2, c % 2
        hs = slice(hg * 1024, (hg + 1) * 1024)
        wq_l = w_qkv[hs, :].reshape(HL, HD, D)
        wk_l = w_qkv[2048 + hg * 1024: 2048 + (hg + 1) * 1024, :].reshape(HL, HD, D)
        wv_l = w_qkv[4096 + hg * 1024: 4096 + (hg + 1) * 1024, :].reshape(HL, HD, D)
        wqkv_arr = np.ascontiguousarray(
            np.concatenate([wq_l.transpose(0, 2, 1), wk_l.transpose(0, 2, 1),
                            wv_l.transpose(0, 2, 1)], axis=2))
        os_ = slice((1 - hg) * 1024, (2 - hg) * 1024)
        wout_local = np.concatenate([woutT[hs], woutT[os_]], axis=0)
        wout_arr = np.ascontiguousarray(
            wout_local.reshape(D, 4, 512).transpose(1, 0, 2))
        in_maps.append({
            "xb": np.ascontiguousarray(x[b]),
            "wqkv": wqkv_arr,
            "wout": wout_arr,
            "router": router_arr,
            "wa": wa_arr,
            "wb": wb_arr,
            "cosf": cosf, "sinf": sinf, "rmat": rmat, "identf": identf,
            "maskS": maskS, "maskST": maskST,
            "ones_row": np.ones((1, 128), np.float32),
            "ones_col": np.ones((128, 1), np.float32),
            "one1": np.ones((1, 1), np.float32),
        })
    return in_maps


def kernel(L0_attn_x, L0_attn_w_qkv, L0_attn_w_out, L0_ffn_router_w,
           L0_ffn_Wa, L0_ffn_Wb):
    in_maps = _host_inputs(L0_attn_x, L0_attn_w_qkv, L0_attn_w_out,
                           L0_ffn_router_w, L0_ffn_Wa, L0_ffn_Wb)
    nc = build_nc()
    res = run_bass_kernel_spmd(nc, in_maps, core_ids=list(range(8)))
    out = np.empty((4, T, D), dtype=np.float32)
    for c in range(8):
        b, hg = c // 2, c % 2
        out[b, hg * 1024:(hg + 1) * 1024] = res.results[c]["out"]
    if DEBUG:
        kernel.last_results = res
    return out


# version 13
# speedup vs baseline: 1.0973x; 1.0490x over previous
import contextlib
import numpy as np
import concourse.bass as bass
import concourse.mybir as mybir
from concourse.bass_utils import run_bass_kernel_spmd

F32 = mybir.dt.float32
BF16 = mybir.dt.bfloat16
AF = mybir.ActivationFunctionType
ALU = mybir.AluOpType
AX = mybir.AxisListType.X

T = 2048
D = 2048
HL = 8
HD = 128
NT = 16
NF = 16
NEG = -1e30
EPS = 1e-5
SCALE = float(HD) ** -0.5

DEBUG = False


class Sem:
    __slots__ = ("h", "v")

    def __init__(self, h):
        self.h = h
        self.v = 0

    def inc(self, inst, n=1):
        self.v += n
        return inst.then_inc(self.h, n)


def build_nc():
    nc = bass.Bass("TRN2", target_bir_lowering=False,
                   detect_race_conditions=False)
    es = contextlib.ExitStack()

    din = {}
    for name, shape in [
        ("xb", [T, D]), ("wqkv", [HL, D, 3 * HD]),
        ("wout", [4, D, 512]), ("router", [16, 128, 8]),
        ("wa", [D, 1024]), ("wb", [4, 8, 128, 512]),
        ("cosf", [128, T]), ("sinf", [128, T]),
        ("rmat", [128, 128]), ("identf", [128, 128]),
        ("maskS", [128, 128]), ("maskST", [128, 128]),
        ("ones_row", [1, 128]), ("ones_col", [128, 1]), ("one1", [1, 1]),
    ]:
        din[name] = nc.dram_tensor(name, shape, F32, kind="ExternalInput")
    out_d = nc.dram_tensor("out", [1024, D], F32, kind="ExternalOutput")

    xn_dram = nc.dram_tensor("xn_dram", [128, NF * T], BF16)
    attn_dram = nc.dram_tensor("attn_dram", [HL, 128, T], BF16)
    brows_dram = nc.dram_tensor("brows_dram", [128, 128], F32)
    cc_in = nc.dram_tensor("cc_in", [HL, 128, 1024], BF16)
    cc_out = nc.dram_tensor("cc_out", [2, HL, 128, 1024], BF16)
    groups = [[0, 1], [2, 3], [4, 5], [6, 7]]

    dbg = {}
    if DEBUG:
        for nm, shp, dt in [("attn", [HL, 128, T], BF16),
                            ("brows", [128, 128], F32),
                            ("xn", [128, NF * T], BF16),
                            ("q", [128, HL * T], BF16),
                            ("k", [128, HL * T], BF16),
                            ("v", [128, HL * T], BF16),
                            ("pscr", [128, T], BF16),
                            ("ssum", [128, 128], F32),
                            ("ccout", [2, HL, 128, 1024], BF16),
                            ("eexp0", [1, 512], BF16),
                            ("expb0", [128, 512], BF16),
                            ("mexpb0", [128, 512], BF16),
                            ("brow0", [1, T], F32),
                            ("abuf0", [128, 512], BF16)]:
            dbg[nm] = nc.dram_tensor("dbg_" + nm, shp, dt, kind="ExternalOutput")

    def sbuf(stack, name, shape, dt, side="left"):
        return stack.enter_context(nc.sbuf_tensor(name, shape, dt, side=side))

    def psum(stack, name, shape, dt):
        return stack.enter_context(nc.psum_tensor(name, shape, dt))

    sems = {}

    def S(name):
        s = Sem(es.enter_context(nc.semaphore(name)))
        sems[name] = s
        return s

    sd = S("sd")
    sg = S("sg")
    dsem = {n: S("dma_" + n) for n in
            ["cst", "cs2", "wq", "wk", "wv", "x0", "x1", "xnb", "brow",
             "at0", "at1", "xno", "wa", "rw", "ful", "xo",
             "wo0", "wo1", "out0", "out1", "dbg2"]}
    vsf = S("vsf")
    ssf = S("ssf")

    def vbar(v, inst):
        vsf.v += 1
        inst.then_inc(vsf.h, 1)
        v.wait_ge(vsf.h, vsf.v)

    def sbar(sc, inst):
        ssf.v += 1
        inst.then_inc(ssf.h, 1)
        sc.wait_ge(ssf.h, ssf.v)
    ld = {}

    identb = sbuf(es, "identb", [128, 128], BF16)
    onesr_sb = sbuf(es, "onesr_sb", [1, 128], F32)
    onesc_sb = sbuf(es, "onesc_sb", [128, 1], BF16)
    one1_sb = sbuf(es, "one1_sb", [1, 1], F32)

    es_p2c = contextlib.ExitStack()
    xn_fm = sbuf(es_p2c, "xn_fm", [128, NF * T], BF16)
    identf = sbuf(es_p2c, "identf_sb", [128, 128], F32)
    maskS_sb = sbuf(es_p2c, "maskS_sb", [128, 128], BF16)
    maskST_sb = sbuf(es_p2c, "maskST_sb", [128, 128], BF16)

    es_rope = contextlib.ExitStack()
    cos_sb = sbuf(es_rope, "cos_sb", [128, T], BF16)
    sin_sb = sbuf(es_rope, "sin_sb", [128, T], BF16)
    rmat_sb = sbuf(es_rope, "rmat_sb", [128, 128], BF16)

    es_p1s = contextlib.ExitStack()
    wslab = [sbuf(es_p1s, f"wslab{i}", [128, NF * 3 * HD], BF16) for i in range(2)]
    praw = [sbuf(es_p1s, f"praw{i}", [128, 512], BF16) for i in range(2)]
    t1s = sbuf(es_p1s, "t1s", [128, 512], BF16)
    t2s = sbuf(es_p1s, "t2s", [128, 512], BF16)
    p1stat = sbuf(es_p1s, "p1stat", [128, 64], F32)
    pscr1 = sbuf(es_p1s, "pscr1", [128, 1024], BF16)

    es_mid = contextlib.ExitStack()

    p0 = {n: S("p0_" + n) for n in ["lnv", "lns", "st", "rst", "xn", "tr", "ev"]}
    es_p0 = contextlib.ExitStack()
    xstg = [sbuf(es_p0, f"xstg{i}", [128, D], F32, side="right") for i in range(2)]
    sqscr = [sbuf(es_p0, f"sqscr{i}", [128, D], BF16, side="right") for i in range(2)]
    xn_tok = [sbuf(es_p0, f"xn_tok{i}", [128, D], BF16, side="right") for i in range(2)]
    stat = sbuf(es_p0, "stat", [128, 8 * NT], F32, side="right")
    c_ = lambda j, tt: stat[:, j * NT + tt: j * NT + tt + 1]
    es_ptr = contextlib.ExitStack()
    ps_tr = [psum(es_ptr, f"ps_tr{i}", [128, 512], BF16) for i in range(2)]

    with nc.Block("p0") as block:
        @block.gpsimd
        def _(g):
            dsem["cst"].inc(g.dma_start(out=identb[:, :], in_=din["identf"][:, :]), 16)
            dsem["cst"].inc(g.dma_start(out=cos_sb[:, :], in_=din["cosf"][:, :]), 16)
            dsem["cst"].inc(g.dma_start(out=sin_sb[:, :], in_=din["sinf"][:, :]), 16)
            dsem["cst"].inc(g.dma_start(out=rmat_sb[:, :], in_=din["rmat"][:, :]), 16)
            dsem["cst"].inc(g.dma_start(out=maskS_sb[:, :], in_=din["maskS"][:, :]), 16)
            dsem["cst"].inc(g.dma_start(out=maskST_sb[:, :], in_=din["maskST"][:, :]), 16)
            dsem["cst"].inc(g.dma_start(out=onesc_sb[:, :], in_=din["ones_col"][:, :]), 16)
            for h in range(2):
                dsem["wq" if h % 2 == 0 else "wk"].inc(g.dma_start(
                    out=wslab[h % 2][:, :].rearrange("p (f c) -> p f c", f=NF),
                    in_=din["wqkv"][h].rearrange("(f p) c -> p f c", p=128)), 16)

        @block.sync
        def _(s):
            dsem["cs2"].inc(s.dma_start(out=identf[:, :], in_=din["identf"][:, :]), 16)
            dsem["cs2"].inc(s.dma_start(out=onesr_sb[:, :], in_=din["ones_row"][:, :]), 16)
            dsem["cs2"].inc(s.dma_start(out=one1_sb[:, :], in_=din["one1"][:, :]), 16)
            for tt in range(NT):
                if tt >= 2:
                    s.wait_ge(p0["xn"].h, tt - 1)
                    s.wait_ge(p0["lns"].h, tt - 1)
                dsem["x%d" % (tt % 2)].inc(
                    s.dma_start(out=xstg[tt % 2][:, :],
                                in_=din["xb"][tt * 128:(tt + 1) * 128, :]), 16)

        def xwait(eng, tt):
            eng.wait_ge(dsem["x%d" % (tt % 2)].h, 16 * (tt // 2 + 1))

        @block.vector
        def _(v):
            for tt in range(NT):
                xwait(v, tt)
                vbar(v, v.reduce_sum(c_(0, tt), xstg[tt % 2][:, :], axis=AX))
                v.wait_ge(p0["lns"].h, tt + 1)
                vbar(v, v.tensor_scalar(c_(2, tt), c_(0, tt), 1.0 / D, None,
                                        ALU.mult))
                vbar(v, v.tensor_tensor(c_(3, tt), c_(2, tt), c_(2, tt), ALU.mult))
                vbar(v, v.tensor_scalar(c_(4, tt), c_(1, tt), 1.0 / D, c_(3, tt),
                                        ALU.mult, ALU.subtract))
                vbar(v, v.tensor_scalar(c_(5, tt), c_(4, tt), EPS, None, ALU.add))
                p0["st"].inc(v.reciprocal(c_(6, tt), c_(5, tt)))
                v.wait_ge(p0["rst"].h, tt + 1)
                if tt >= 2:
                    v.wait_ge(p0["tr"].h, 4 * (tt - 1))
                p0["xn"].inc(
                    v.tensor_scalar(xn_tok[tt % 2][:, :], xstg[tt % 2][:, :],
                                    c_(2, tt), c_(7, tt), ALU.subtract, ALU.mult))

        @block.tensor
        def _(t):
            t.wait_ge(dsem["cst"].h, 112)
            for tt in range(NT):
                t.wait_ge(p0["xn"].h, tt + 1)
                for g4 in range(4):
                    idx = tt * 4 + g4
                    if idx >= 2:
                        t.wait_ge(p0["ev"].h, idx - 1)
                    for j in range(4):
                        fc = g4 * 4 + j
                        tr = t.transpose(ps_tr[idx % 2][:, j * 128:(j + 1) * 128],
                                         xn_tok[tt % 2][:, fc * 128:(fc + 1) * 128],
                                         identb[:, :])
                    p0["tr"].inc(tr)

        @block.scalar
        def _(sc):
            for tt in range(NT):
                xwait(sc, tt)
                p0["lns"].inc(
                    sc.activation(sqscr[tt % 2][:, :], xstg[tt % 2][:, :],
                                  AF.Square, accum_out=c_(1, tt)))
                sc.wait_ge(p0["st"].h, tt + 1)
                p0["rst"].inc(sc.activation(c_(7, tt), c_(6, tt), AF.Sqrt))
                if tt >= 1:
                    for g4 in range(4):
                        idx = (tt - 1) * 4 + g4
                        sc.wait_ge(p0["tr"].h, idx + 1)
                        dst = xn_fm[:, :].rearrange("p (f t) -> p f t", f=NF)[
                            :, g4 * 4:(g4 + 1) * 4, (tt - 1) * 128: tt * 128]
                        p0["ev"].inc(sc.copy(
                            dst, ps_tr[idx % 2][:, :].rearrange("p (f t) -> p f t", f=4)))
            for g4 in range(4):
                idx = (NT - 1) * 4 + g4
                sc.wait_ge(p0["tr"].h, idx + 1)
                dst = xn_fm[:, :].rearrange("p (f t) -> p f t", f=NF)[
                    :, g4 * 4:(g4 + 1) * 4, (NT - 1) * 128: NT * 128]
                p0["ev"].inc(sc.copy(
                    dst, ps_tr[idx % 2][:, :].rearrange("p (f t) -> p f t", f=4)))

    es_p0.close()
    es_ptr.close()

    q_sb = sbuf(es_mid, "q_sb", [128, HL * T], BF16, side="right")
    k_sb = sbuf(es_mid, "k_sb", [128, HL * T], BF16, side="right")
    v_sb = sbuf(es_mid, "v_sb", [128, HL * T], BF16, side="right")
    ssum = sbuf(es_mid, "ssum", [128, 128], F32, side="right")
    es_pq = contextlib.ExitStack()
    ps_q = [psum(es_pq, f"ps_q{i}", [128, 512], F32) for i in range(3)]
    ps_r = psum(es_pq, "ps_r", [128, 512], F32)
    ps_s = [psum(es_pq, f"ps_s{i}", [128, 1024], F32) for i in range(2)]

    CH = [(h, m, ts) for h in range(HL) for m in range(3) for ts in range(4)]
    CIDX = {c: i for i, c in enumerate(CH)}
    RCH = [(h, m, ts) for h in range(HL) for m in range(2) for ts in range(4)]
    RIDX = {c: i for i, c in enumerate(RCH)}
    P1CH = []
    for h in range(HL):
        for qt in range(NT):
            kend = (qt + 1) * 128
            if kend <= 1024:
                P1CH.append((h, qt, 0, 0, kend, True))
            else:
                P1CH.append((h, qt, 0, 0, 1024, False))
                P1CH.append((h, qt, 1, 1024, kend - 1024, False))
    P1OF = {}
    for i, (h, qt, ci, k0, w, sg_) in enumerate(P1CH):
        P1OF[(h, qt, ci)] = i
    P1H = {h: [i for i, c in enumerate(P1CH) if c[0] == h] for h in range(HL)}
    QKV_SB = [q_sb, k_sb, v_sb]

    p1 = {n: S("p1_" + n) for n in ["m", "e", "r", "v", "sm", "x", "he"]}

    with nc.Block("p1s") as block:
        @block.gpsimd
        def _(g):
            for h in range(2, HL):
                g.wait_ge(p1["m"].h, 12 * (h - 1))
                dsem["wq" if h % 2 == 0 else "wk"].inc(g.dma_start(
                    out=wslab[h % 2][:, :].rearrange("p (f c) -> p f c", f=NF),
                    in_=din["wqkv"][h].rearrange("(f p) c -> p f c", p=128)), 16)

        @block.tensor
        def _(t):
            t.wait_ge(dsem["cst"].h, 112)

            def emit_p1(i):
                h, qt, ci, k0, w, single = P1CH[i]
                if i >= 2:
                    t.wait_ge(p1["x"].h, i - 1)
                last = single or ci == 1
                qtile = q_sb[:, h * T + qt * 128: h * T + (qt + 1) * 128]
                hi = w - 128 if last else w
                for j in range((hi + 511) // 512):
                    lo = j * 512
                    n = min(512, hi - lo)
                    mm = t.matmul(
                        ps_s[i % 2][:, lo: lo + n], qtile,
                        k_sb[:, h * T + k0 + lo: h * T + k0 + lo + n],
                        start=True, stop=True)
                if last:
                    t.matmul(ps_s[i % 2][:, w - 128:w], identb[:, :],
                             maskS_sb[:, :], start=True, stop=False)
                    mm = t.matmul(
                        ps_s[i % 2][:, w - 128: w], qtile,
                        k_sb[:, h * T + k0 + w - 128: h * T + k0 + w],
                        start=False, stop=True)
                p1["sm"].inc(mm)

            def emit_rot(ri):
                h, m, ts = RCH[ri]
                cq = CIDX[(h, m, ts)]
                t.wait_ge(p1["e"].h, cq + 1)
                if ri >= 1:
                    t.wait_ge(p1["v"].h, ri)
                p1["r"].inc(t.matmul(ps_r[:, :], rmat_sb[:, :],
                                     praw[ri % 2][:, :], start=True, stop=True))

            for cq, (h, m, ts) in enumerate(CH):
                t.wait_ge(p0["ev"].h, 16 * (ts + 1))
                t.wait_ge(dsem["wq" if h % 2 == 0 else "wk"].h,
                          16 * (h // 2 + 1))
                if cq >= 3:
                    t.wait_ge(p1["e"].h, cq - 2)
                for fc in range(NF):
                    mm = t.matmul(
                        ps_q[cq % 3][:, :],
                        wslab[h % 2][:, fc * 3 * HD + m * HD:
                                     fc * 3 * HD + (m + 1) * HD],
                        xn_fm[:, fc * T + ts * 512: fc * T + (ts + 1) * 512],
                        start=(fc == 0), stop=(fc == NF - 1))
                p1["m"].inc(mm)
                if m < 2:
                    ri = RIDX[(h, m, ts)]
                    if ri >= 1 and ri % 8 != 0:
                        emit_rot(ri - 1)
                if m == 2 and ts == 0:
                    emit_rot(RIDX[(h, 1, 3)])
                if h >= 1 and m == 2:
                    t.wait_ge(p1["v"].h, 8 * h)
                    per = len(P1H[h - 1])
                    lo = ts * per // 4
                    hi = (ts + 1) * per // 4
                    for i in P1H[h - 1][lo:hi]:
                        emit_p1(i)
            t.wait_ge(p1["v"].h, 8 * HL)
            for i in P1H[HL - 1]:
                emit_p1(i)

        @block.scalar
        def _(sc):
            nev = [0]

            def emit_exp(i):
                h, qt, ci, k0, w, single = P1CH[i]
                sc.wait_ge(p1["sm"].h, i + 1)
                j = qt - 8
                par = (h % 2) * 8
                if single:
                    acc = ssum[:, h * 16 + qt: h * 16 + qt + 1]
                elif ci == 0:
                    if j == 0 and h >= 2:
                        sc.wait_ge(p1["he"].h, h - 1)
                    acc = ssum[:, h * 16 + qt: h * 16 + qt + 1]
                else:
                    acc = p1stat[:, par + j: par + j + 1]
                p1["x"].inc(sc.activation(pscr1[:, 0:w], ps_s[i % 2][:, 0:w],
                                          AF.Exp, accum_out=acc))

            for cq, (h, m, ts) in enumerate(CH):
                sc.wait_ge(p1["m"].h, cq + 1)
                if m < 2:
                    ri = RIDX[(h, m, ts)]
                    if ri >= 2:
                        sc.wait_ge(p1["v"].h, ri - 1)
                    p1["e"].inc(sc.activation(praw[ri % 2][:, :],
                                              ps_q[cq % 3][:, :], AF.Copy,
                                              scale=SCALE if m == 0 else 1.0))
                else:
                    p1["e"].inc(sc.activation(
                        v_sb[:, h * T + ts * 512: h * T + (ts + 1) * 512],
                        ps_q[cq % 3][:, :], AF.Copy))
                if h >= 1 and m == 2:
                    per = len(P1H[h - 1])
                    lo = ts * per // 4
                    hi = (ts + 1) * per // 4
                    for i in P1H[h - 1][lo:hi]:
                        emit_exp(i)
            for i in P1H[HL - 1]:
                emit_exp(i)

        @block.vector
        def _(v):
            v.wait_ge(dsem["cst"].h, 112)
            nrope = [0]

            def emit_rope_v(ri):
                h, m, ts = RCH[ri]
                v.wait_ge(p1["r"].h, ri + 1)
                v.tensor_tensor(t2s[:, :], ps_r[:, :],
                                sin_sb[:, ts * 512:(ts + 1) * 512], ALU.mult)
                vbar(v, v.tensor_tensor(t1s[:, :], praw[ri % 2][:, :],
                                        cos_sb[:, ts * 512:(ts + 1) * 512],
                                        ALU.mult))
                p1["v"].inc(v.tensor_tensor(
                    QKV_SB[m][:, h * T + ts * 512: h * T + (ts + 1) * 512],
                    t1s[:, :], t2s[:, :], ALU.add))

            def emit_combine(h):
                par = (h % 2) * 8
                v.wait_ge(p1["x"].h, P1OF[(h, NT - 1, 1)] + 1)
                p1["he"].inc(v.tensor_tensor(
                    ssum[:, h * 16 + 8: h * 16 + 16],
                    ssum[:, h * 16 + 8: h * 16 + 16],
                    p1stat[:, par: par + 8], ALU.add))

            for cq, (h, m, ts) in enumerate(CH):
                if m < 2:
                    ri = RIDX[(h, m, ts)]
                    emit_rope_v(ri)
                if h >= 2 and m == 2 and ts == 3:
                    emit_combine(h - 2)
            emit_combine(HL - 2)
            emit_combine(HL - 1)

    es_pq.close()
    es_p1s.close()
    es_rope.close()

    abuf = [sbuf(es_mid, f"abuf{i}", [128, 512], BF16, side="right")
            for i in range(2)]
    pscr = sbuf(es_mid, "pscr", [128, T], BF16, side="right")
    bh = sbuf(es_mid, "bh", [128, 128], F32, side="right")
    brow = sbuf(es_mid, "brow", [1, T], F32, side="right")
    eexp = [sbuf(es_mid, f"eexp{i}", [1, 512], BF16, side="right") for i in range(2)]
    onesr_bf = sbuf(es_mid, "onesr_bf", [1, 128], BF16, side="right")
    expb = [sbuf(es_mid, f"expb{i}", [128, 512], BF16, side="right") for i in range(2)]
    mexpb = [sbuf(es_mid, f"mexpb{i}", [128, 512], BF16, side="right") for i in range(2)]

    ccsem = S("ccsem")
    p2 = {n: S("p2_" + n) for n in ["s1", "msk", "exp", "st", "tB", "eB",
                                    "dg", "pt", "avd", "ae", "eb", "bc",
                                    "ebe", "pb", "vt", "vte"]}

    def kchunks(qt):
        kend = (qt + 1) * 128
        return [(kc * 512, min(512, kend - kc * 512))
                for kc in range((kend + 511) // 512)]

    EV2 = [(h, qs, kt) for h in range(HL) for qs in range(4)
           for kt in range(4 * qs + 4)]
    EIDX = {e: i for i, e in enumerate(EV2)}
    LASTE = {(h, qs): EIDX[(h, qs, 4 * qs + 3)] for h in range(HL)
             for qs in range(4)}
    DGC = np.cumsum([1 if kt >= 4 * qs else 0 for (h, qs, kt) in EV2]).tolist()
    brow_done = {}
    adma_done = {}

    with nc.Block("p2") as block:
        es_ps2 = contextlib.ExitStack()
        ps_b = psum(es_ps2, "ps_b", [128, 128], F32)
        ps_vt = psum(es_ps2, "ps_vt", [128, 512], BF16)

        @block.sync
        def _(s):
            s.wait_ge(p0["ev"].h, 64)
            dsem["xnb"].inc(s.dma_start(out=xn_dram[:, :], in_=xn_fm[:, :]), 16)

        @block.tensor
        def _(t):
            t.wait_ge(dsem["cs2"].h, 48)
            p2["tB"].inc(t.transpose(ps_b[:, :], ssum[:, :], identf[:, :]))
            for h in range(HL):
                for g4 in range(4):
                    gi = h * 4 + g4
                    if gi >= 1:
                        t.wait_ge(p2["vte"].h, gi)
                    for j in range(4):
                        kt = g4 * 4 + j
                        tr = t.transpose(ps_vt[:, j * 128:(j + 1) * 128],
                                         v_sb[:, h * T + kt * 128:
                                              h * T + (kt + 1) * 128],
                                         identb[:, :])
                    p2["vt"].inc(tr)

        @block.scalar
        def _(sc):
            for gi in range(32):
                h, g4 = gi // 4, gi % 4
                sc.wait_ge(p2["vt"].h, gi + 1)
                p2["vte"].inc(sc.copy(
                    v_sb[:, h * T + g4 * 512: h * T + (g4 + 1) * 512],
                    ps_vt[:, :]))

        @block.vector
        def _(v):
            v.wait_ge(p2["tB"].h, 1)
            p2["eB"].inc(v.tensor_copy(bh[:, :], ps_b[:, :]))

        @block.sync
        def _(s):
            s.wait_ge(p2["eB"].h, 1)
            dsem["brow"].inc(s.dma_start(out=brows_dram[:, :], in_=bh[:, :]), 16)
            s.wait_ge(dsem["brow"].h, 16)

            def emit_chunk(hh, qs):
                gidx = hh * 4 + qs
                s.wait_ge(p2["ae"].h, gidx + 1)
                dsem["at%d" % (gidx % 2)].inc(s.dma_start(
                    out=attn_dram[hh, :, qs * 512:(qs + 1) * 512],
                    in_=abuf[gidx % 2][:, :]), 16)

            for h in range(HL):
                if h >= 1:
                    for qs in range(4):
                        emit_chunk(h - 1, qs)
                    s.wait_ge(p2["dg"].h, LASTE[(h - 1, 3)] + 1)
                    if DEBUG and h == 1:
                        s.wait_ge(dsem["dbg2"].h, 64)
                dsem["brow"].inc(s.dma_start(
                    out=brow[:, :],
                    in_=brows_dram[:, :].rearrange("(h r) c -> h (r c)", h=HL)[
                        h:h + 1, :]), 16)
            for qs in range(4):
                emit_chunk(HL - 1, qs)

        @block.gpsimd
        def _(g):
            dsem["cst"].inc(g.dma_start(out=onesr_bf[:, :], in_=din["ones_row"][:, :]), 16)
            if DEBUG:
                g.wait_ge(p2["pb"].h, 1)
                dsem["dbg2"].inc(g.dma_start(out=dbg["eexp0"].ap(), in_=eexp[0][:, :]), 16)
                dsem["dbg2"].inc(g.dma_start(out=dbg["expb0"].ap(), in_=expb[0][:, :]), 16)
                dsem["dbg2"].inc(g.dma_start(out=dbg["mexpb0"].ap(), in_=mexpb[0][:, :]), 16)
                dsem["dbg2"].inc(g.dma_start(out=dbg["brow0"].ap(), in_=brow[:, :]), 16)
                g.wait_ge(p2["ae"].h, 1)
                dsem["dbg2"].inc(g.dma_start(out=dbg["abuf0"].ap(), in_=abuf[0][:, :]), 16)
            pid = g.partition_id()
            oth = (1 - pid % 2) * 1024
            g.wait_ge(dsem["at0"].h, 16 * 16)
            g.wait_ge(dsem["at1"].h, 16 * 16)
            sg.inc(g.dma_start(out=cc_in[:, :, :],
                               in_=attn_dram[:, :, bass.ds(oth, 1024)]), 16)
            g.wait_ge(sg.h, sg.v)
            ccsem.inc(g.collective_compute(
                "AllGather", ALU.bypass, replica_groups=groups,
                ins=[cc_in.ap().opt()], outs=[cc_out.ap().opt()]))
            if DEBUG:
                sg.inc(g.dma_start(out=dbg["ssum"].ap(), in_=ssum[:, :]), 16)
                sg.inc(g.dma_start(out=dbg["q"].ap(), in_=q_sb[:, :]), 16)
                sg.inc(g.dma_start(out=dbg["k"].ap(), in_=k_sb[:, :]), 16)
                sg.inc(g.dma_start(out=dbg["v"].ap(), in_=v_sb[:, :]), 16)
                sg.inc(g.dma_start(out=dbg["pscr"].ap(), in_=pscr[:, :]), 16)
                g.wait_ge(sg.h, sg.v)

        es_ps2b = contextlib.ExitStack()
        ps_st = [psum(es_ps2b, f"ps_st{i}", [128, 512], F32) for i in range(3)]
        ps_av = [psum(es_ps2b, f"ps_av{i}", [128, 512], F32) for i in range(2)]
        ps_bc = [psum(es_ps2b, "ps_bc0", [128, 512], F32)] * 2

        @block.tensor
        def _(t):
            def OFF(qs_, kt_):
                return max(0, kt_ - 4 * qs_) * 128

            t.wait_ge(dsem["cst"].h, 128)
            next_av = [0]

            def emit_av():
                j = next_av[0]
                jh, jqs, jkt = EV2[j]
                jo = OFF(jqs, jkt)
                t.wait_ge(p2["vte"].h, 4 * (jh + 1))
                t.wait_ge(p2["pb"].h, j + 1)
                p2["avd"].inc(t.matmul(
                    ps_av[(jh * 4 + jqs) % 2][:, jo:512],
                    v_sb[:, jh * T + jkt * 128: jh * T + (jkt + 1) * 128],
                    pscr[:, (j % 4) * 512 + jo:(j % 4 + 1) * 512],
                    start=(jkt == 0), stop=(jkt == 4 * jqs + 3)))
                next_av[0] += 1

            for i, (h, qs, kt) in enumerate(EV2):
                if kt == 0:
                    while next_av[0] < i:
                        emit_av()
                    gidx = h * 4 + qs
                    if gidx >= 2:
                        t.wait_ge(p2["ae"].h, gidx - 1)
                    t.wait_ge(p2["eb"].h, gidx + 1)
                    if gidx >= 1:
                        t.wait_ge(p2["ebe"].h, gidx)
                    p2["bc"].inc(t.matmul(ps_bc[gidx % 2][:, :], onesr_bf[:, :],
                                          eexp[gidx % 2][:, :],
                                          start=True, stop=True))
                if i >= 3:
                    t.wait_ge(p2["pt"].h, i - 2)
                off = OFF(qs, kt)
                p2["dg"].inc(t.matmul(
                    ps_st[i % 3][:, off:512],
                    k_sb[:, h * T + kt * 128: h * T + (kt + 1) * 128],
                    q_sb[:, h * T + qs * 512 + off: h * T + (qs + 1) * 512],
                    start=True, stop=True))
                while i - next_av[0] >= 3:
                    emit_av()
            while next_av[0] < len(EV2):
                emit_av()

        @block.vector
        def _(v):
            def OFF(qs_, kt_):
                return max(0, kt_ - 4 * qs_) * 128

            tri_b = bass.AP(maskST_sb[:, :].tensor, 0,
                            [[128, 128], [0, 4], [1, 128]])

            def biasmul(i):
                jh, jqs, jkt = EV2[i]
                jo = OFF(jqs, jkt)
                jg = jh * 4 + jqs
                v.wait_ge(p2["pt"].h, i + 1)
                qsl = slice((i % 4) * 512 + jo, (i % 4 + 1) * 512)
                if jkt >= 4 * jqs:
                    mtt = v.tensor_tensor(
                        pscr[:, (i % 4) * 512 + jo:(i % 4) * 512 + jo + 128],
                        pscr[:, (i % 4) * 512 + jo:(i % 4) * 512 + jo + 128],
                        mexpb[jg % 2][:, jo:jo + 128], ALU.mult)
                    if jo + 128 < 512:
                        p2["pb"].inc(v.tensor_tensor(
                            pscr[:, (i % 4) * 512 + jo + 128:(i % 4 + 1) * 512],
                            pscr[:, (i % 4) * 512 + jo + 128:(i % 4 + 1) * 512],
                            expb[jg % 2][:, jo + 128:512], ALU.mult))
                    else:
                        p2["pb"].inc(mtt)
                else:
                    p2["pb"].inc(v.tensor_tensor(
                        pscr[:, qsl], pscr[:, qsl],
                        expb[jg % 2][:, jo:512], ALU.mult))

            for i, (h, qs, kt) in enumerate(EV2):
                if i >= 1:
                    biasmul(i - 1)
                if kt == 0:
                    gidx = h * 4 + qs
                    v.wait_ge(dsem["brow"].h, 16 * (h + 2))
                    if DEBUG and gidx == 2:
                        v.wait_ge(dsem["dbg2"].h, 80)
                    if gidx >= 2:
                        v.wait_ge(p2["bc"].h, gidx - 1)
                    with nc.allow_low_precision(reason="bf16 matmul operand"):
                        p2["eb"].inc(v.reciprocal(
                            eexp[gidx % 2][:, :],
                            brow[0:1, qs * 512:(qs + 1) * 512]))
                    v.wait_ge(p2["bc"].h, gidx + 1)
                    p2["ebe"].inc(v.tensor_copy(expb[gidx % 2][:, :],
                                                ps_bc[gidx % 2][:, :]))
                    v.wait_ge(p2["ebe"].h, gidx + 1)
                    v.tensor_tensor(
                        mexpb[gidx % 2][:, :].rearrange("p (a b) -> p a b", a=4),
                        expb[gidx % 2][:, :].rearrange("p (a b) -> p a b", a=4),
                        tri_b, ALU.mult)
            biasmul(len(EV2) - 1)

        @block.scalar
        def _(sc):
            for i, (h, qs, kt) in enumerate(EV2):
                gidx = h * 4 + qs
                sc.wait_ge(p2["dg"].h, i + 1)
                if i >= 4:
                    sc.wait_ge(p2["avd"].h, i - 3)
                off = max(0, kt - 4 * qs) * 128
                p2["pt"].inc(sc.activation(
                    pscr[:, (i % 4) * 512 + off:(i % 4 + 1) * 512],
                    ps_st[i % 3][:, off:512], AF.Exp))
                if kt == 4 * qs + 3:
                    sc.wait_ge(p2["avd"].h, i + 1)
                    if gidx >= 2:
                        sc.wait_ge(dsem["at%d" % (gidx % 2)].h,
                                   16 * ((gidx - 2) // 2 + 1))
                    p2["ae"].inc(sc.copy(abuf[gidx % 2][:, :],
                                         ps_av[gidx % 2][:, :]))

        es_ps2b.close()
        es_ps2.close()

    es_mid.close()
    es_p2c.close()


    es_p5l = contextlib.ExitStack()
    attn_full = sbuf(es_p5l, "attn_full", [128, 16 * 1024], BF16)
    at_sb = sbuf(es_p5l, "at_sb", [128, 8 * 1024], BF16)
    wout_sb = [sbuf(es_p5l, f"wout_sb{i}", [128, 16 * 512], BF16) for i in range(2)]
    wb_sb = [sbuf(es_p5l, f"wb_sb{i}", [128, 8 * 512], BF16) for i in range(2)]
    wdone = {}
    es_p34 = contextlib.ExitStack()
    xn_own = sbuf(es_p34, "xn_own", [128, NF * 1024], BF16)
    wa_sb = sbuf(es_p34, "wa_sb", [128, NF * 1024], BF16)
    rw_sb = sbuf(es_p34, "rw_sb", [128, 128], BF16)
    as_sb = sbuf(es_p34, "as_sb", [128, 8 * 1024], BF16)
    sqb = [sbuf(es_p34, f"sqb{i}", [128, 1024], BF16) for i in range(2)]
    ssrow = sbuf(es_p34, "ssrow", [1, 1024], F32)
    nrow = sbuf(es_p34, "nrow", [1, 1024], F32)
    rsrow = sbuf(es_p34, "rsrow", [1, 1024], F32)
    rscol = sbuf(es_p34, "rscol", [128, 8], F32)
    lcol = sbuf(es_p34, "lcol", [128, 8], F32)
    ecol = sbuf(es_p34, "ecol", [128, 8], F32)
    mrcol = sbuf(es_p34, "mrcol", [128, 8], F32)
    indcol = sbuf(es_p34, "indcol", [128, 8], F32)
    cmb = sbuf(es_p34, "cmb", [128, 64], F32)
    tstat = sbuf(es_p34, "tstat", [128, 8], F32)

    p3 = {n: S("p3_" + n) for n in ["sq", "ss", "se", "rs", "rc", "lg", "le",
                                    "tp", "am", "ae", "at", "ate"]}

    with nc.Block("p34") as block:
        @block.gpsimd
        def _(g):
            pid = g.partition_id()
            own = (pid % 2) * 1024
            partner = 1 - (pid % 2)
            g.wait_ge(dsem["xnb"].h, 16)
            dsem["xno"].inc(g.dma_start(
                out=xn_own[:, :].rearrange("p (f t) -> p f t", f=NF),
                in_=xn_dram[:, :].rearrange("p (f t) -> p f t", f=NF)[
                    :, :, bass.ds(own, 1024)]), 16)
            dsem["wa"].inc(g.dma_start(
                out=wa_sb[:, :].rearrange("p (f c) -> p f c", f=NF),
                in_=din["wa"][:, :].rearrange("(f p) c -> p f c", p=128)), 16)
            dsem["rw"].inc(g.dma_start(
                out=rw_sb[:, 0:NF * 8].rearrange("p (f c) -> p f c", f=NF),
                in_=din["router"][:, :, :].rearrange("f p c -> p f c")), 16)
            g.wait_ge(dsem["at0"].h, 16 * 16)
            g.wait_ge(dsem["at1"].h, 16 * 16)
            dsem["ful"].inc(g.dma_start(
                out=attn_full[:, 0:HL * 1024].rearrange("p (h t) -> p h t", h=HL),
                in_=attn_dram[:, :, bass.ds(own, 1024)].rearrange(
                    "h p t -> p h t")), 16)
            for os in range(2):
                dsem["wo%d" % os].inc(g.dma_start(
                    out=wout_sb[os][:, :].rearrange("p (gi c) -> p gi c", gi=16),
                    in_=din["wout"][os].rearrange("(gi p) c -> p gi c", p=128)), 16)
                dsem["wo%d" % os].inc(g.dma_start(
                    out=wb_sb[os][:, :].rearrange("p (e c) -> p e c", e=8),
                    in_=din["wb"][os].rearrange("e p c -> p e c")), 16)
            g.wait_ge(ccsem.h, 1)
            dsem["ful"].inc(g.dma_start(
                out=attn_full[:, HL * 1024:].rearrange("p (h t) -> p h t", h=HL),
                in_=cc_out[partner].rearrange("h p t -> p h t")), 16)

        @block.scalar
        def _(sc):
            sc.wait_ge(dsem["xno"].h, 16)
            for fc in range(NF):
                if fc >= 2:
                    sc.wait_ge(p3["ss"].h, fc - 1)
                p3["sq"].inc(sc.activation(sqb[fc % 2][:, :],
                                           xn_own[:, fc * 1024:(fc + 1) * 1024],
                                           AF.Square))

        es_ps34 = contextlib.ExitStack()
        ps_ss = [psum(es_ps34, f"ps_ss{i}", [1, 512], F32) for i in range(2)]
        ps_rc = psum(es_ps34, "ps_rc", [128, 8], F32)
        ps_lg = [psum(es_ps34, f"ps_lg{i}", [128, 8], F32) for i in range(2)]
        ps_a = [psum(es_ps34, f"ps_a{i}", [128, 512], F32) for i in range(2)]
        ps_t4 = psum(es_ps34, "ps_t4", [128, 512], BF16)

        @block.tensor
        def _(t):
            for fc in range(NF):
                t.wait_ge(p3["sq"].h, fc + 1)
                for th in range(2):
                    mm = t.matmul(ps_ss[th][:, :], onesc_sb[:, :],
                                  sqb[fc % 2][:, th * 512:(th + 1) * 512],
                                  start=(fc == 0), stop=(fc == NF - 1))
                p3["ss"].inc(mm)
            t.wait_ge(p3["rs"].h, 1)
            for tt in range(8):
                mm = t.matmul(ps_rc[:, tt:tt + 1],
                              rsrow[0:1, tt * 128:(tt + 1) * 128],
                              one1_sb[:, :], start=(tt == 0), stop=(tt == 7))
            p3["rc"].inc(mm)
            t.wait_ge(dsem["rw"].h, 16)
            for tt in range(8):
                if tt >= 2:
                    t.wait_ge(p3["le"].h, tt - 1)
                for fc in range(NF):
                    mm = t.matmul(ps_lg[tt % 2][:, :],
                                  xn_own[:, fc * 1024 + tt * 128:
                                         fc * 1024 + (tt + 1) * 128],
                                  rw_sb[:, fc * 8:(fc + 1) * 8],
                                  start=(fc == 0), stop=(fc == NF - 1))
                p3["lg"].inc(mm)
            t.wait_ge(dsem["wa"].h, 16)
            for tt in range(8):
                for eg in range(2):
                    aidx = tt * 2 + eg
                    if aidx >= 2:
                        t.wait_ge(p3["ae"].h, aidx - 1)
                    for fc in range(NF):
                        mm = t.matmul(ps_a[aidx % 2][:, :],
                                      xn_own[:, fc * 1024 + tt * 128:
                                             fc * 1024 + (tt + 1) * 128],
                                      wa_sb[:, fc * 1024 + eg * 512:
                                            fc * 1024 + (eg + 1) * 512],
                                      start=(fc == 0), stop=(fc == NF - 1))
                    p3["am"].inc(mm)
            for e in range(8):
                for tg in range(2):
                    tidx = e * 2 + tg
                    need_aidx = (tg * 4 + 3) * 2 + (e // 4)
                    t.wait_ge(p3["ae"].h, need_aidx + 1)
                    if tidx >= 1:
                        t.wait_ge(p3["ate"].h, tidx)
                    for j in range(4):
                        tt = tg * 4 + j
                        tr = t.transpose(ps_t4[:, j * 128:(j + 1) * 128],
                                         as_sb[:, tt * 1024 + e * 128:
                                               tt * 1024 + (e + 1) * 128],
                                         identb[:, :])
                    p3["at"].inc(tr)

        @block.vector
        def _(v):
            v.wait_ge(p3["ss"].h, NF)
            v.tensor_copy(ssrow[:, 0:512], ps_ss[0][:, :])
            p3["se"].inc(v.tensor_copy(ssrow[:, 512:1024], ps_ss[1][:, :]))

        @block.scalar
        def _(sc):
            sc.wait_ge(p3["se"].h, 1)
            p3["se"].inc(sc.activation(nrow[:, :], ssrow[:, :], AF.Sqrt))

        @block.vector
        def _(v):
            v.wait_ge(p3["se"].h, 2)
            vbar(v, v.tensor_scalar(ssrow[:, :], nrow[:, :], 1e-12, None,
                                    ALU.max))
            p3["rs"].inc(v.reciprocal(rsrow[:, :], ssrow[:, :]))
            v.wait_ge(p3["rc"].h, 1)
            for tt in range(8):
                cp = v.tensor_copy(rscol[:, tt:tt + 1], ps_rc[:, tt:tt + 1])
            vbar(v, cp)
            for tt in range(8):
                v.wait_ge(p3["lg"].h, tt + 1)
                vbar(v, v.tensor_scalar(lcol[:, :], ps_lg[tt % 2][:, :],
                                        rscol[:, tt:tt + 1], None, ALU.mult))
                p3["le"].inc(v.nop())
                p3["tp"].inc(v.reduce_max(tstat[:, 0:1], lcol[:, :],
                                          axis=AX, negate=True))
                v.wait_ge(p3["tp"].h, 2 * (tt + 1))
                vbar(v, v.reduce_max(tstat[:, 1:2], ecol[:, :], axis=AX))
                vbar(v, v.tensor_single_scalar(mrcol[:, :], ecol[:, :],
                                               tstat[:, 1:2], ALU.is_lt))
                vbar(v, v.tensor_tensor(mrcol[:, :], mrcol[:, :], ecol[:, :],
                                        ALU.mult))
                vbar(v, v.reduce_max(tstat[:, 2:3], mrcol[:, :], axis=AX))
                vbar(v, v.tensor_single_scalar(indcol[:, :], ecol[:, :],
                                               tstat[:, 2:3], ALU.is_ge))
                vbar(v, v.tensor_tensor(indcol[:, :], indcol[:, :], ecol[:, :],
                                        ALU.mult))
                vbar(v, v.reduce_sum(tstat[:, 3:4], indcol[:, :], axis=AX))
                vbar(v, v.reciprocal(tstat[:, 4:5], tstat[:, 3:4]))
                vbar(v, v.tensor_scalar(cmb[:, tt * 8:(tt + 1) * 8],
                                        indcol[:, :], tstat[:, 4:5], None,
                                        ALU.mult))
            for tt in range(8):
                for eg in range(2):
                    aidx = tt * 2 + eg
                    v.wait_ge(p3["am"].h, aidx + 1)
                    for j in range(4):
                        e = eg * 4 + j
                        ts_ = v.tensor_scalar(
                            as_sb[:, tt * 1024 + e * 128:tt * 1024 + (e + 1) * 128],
                            ps_a[aidx % 2][:, j * 128:(j + 1) * 128],
                            cmb[:, tt * 8 + e: tt * 8 + e + 1], None, ALU.mult)
                    p3["ae"].inc(ts_)

        @block.scalar
        def _(sc):
            for tt in range(8):
                sc.wait_ge(p3["tp"].h, 2 * tt + 1)
                p3["tp"].inc(sc.activation(ecol[:, :], lcol[:, :], AF.Exp,
                                           bias=tstat[:, 0:1]))
            for e in range(8):
                for tg in range(2):
                    tidx = e * 2 + tg
                    sc.wait_ge(p3["at"].h, tidx + 1)
                    p3["ate"].inc(sc.copy(
                        at_sb[:, e * 1024 + tg * 512: e * 1024 + (tg + 1) * 512],
                        ps_t4[:, :]))

        es_ps34.close()

    es_p34.close()

    es_p5 = contextlib.ExitStack()
    x_own = sbuf(es_p5, "x_own", [128, 8 * D], F32)
    gscr = [sbuf(es_p5, f"gscr{i}", [128, 512], F32) for i in range(2)]
    sgscr = [sbuf(es_p5, f"sgscr{i}", [128, 512], F32) for i in range(2)]
    oscr = [sbuf(es_p5, f"oscr{i}", [128, 512], F32) for i in range(2)]
    p5 = {n: S("p5_" + n) for n in ["s", "y", "sil", "c"]}
    odone = {}

    with nc.Block("p5") as block:
        es_ps5 = contextlib.ExitStack()
        ps_s5 = [psum(es_ps5, f"ps_s5{i}", [128, 512], F32) for i in range(2)]
        ps_y5 = [psum(es_ps5, f"ps_y5{i}", [128, 512], F32) for i in range(2)]

        @block.gpsimd
        def _(g):
            pid = g.partition_id()
            xb3 = din["xb"][:, :].rearrange("(a r) c -> a r c", a=2)
            dsem["xo"].inc(g.dma_start(
                out=x_own[:, :].rearrange("p (tt c) -> p tt c", tt=8),
                in_=xb3[pid % 2].rearrange("(tt p) c -> p tt c", p=128)), 16)
            for os in range(2, 4):
                g.wait_ge(p5["s"].h, 8 * (os - 1))
                g.wait_ge(p5["y"].h, 8 * (os - 1))
                dsem["wo%d" % (os % 2)].inc(g.dma_start(
                    out=wout_sb[os % 2][:, :].rearrange("p (gi c) -> p gi c", gi=16),
                    in_=din["wout"][os].rearrange("(gi p) c -> p gi c", p=128)), 16)
                dsem["wo%d" % (os % 2)].inc(g.dma_start(
                    out=wb_sb[os % 2][:, :].rearrange("p (e c) -> p e c", e=8),
                    in_=din["wb"][os].rearrange("e p c -> p e c")), 16)
            g.wait_ge(dsem["xo"].h, 16)

        @block.tensor
        def _(t):
            t.wait_ge(dsem["ful"].h, 32)
            t.wait_ge(p3["ate"].h, 16)
            for os in range(4):
                t.wait_ge(dsem["wo%d" % (os % 2)].h, 32 * (os // 2 + 1))
                for tt in range(8):
                    idx = os * 8 + tt
                    if idx >= 2:
                        t.wait_ge(p5["c"].h, idx - 1)
                        t.wait_ge(p5["sil"].h, idx - 1)
                    for gi in range(16):
                        mm = t.matmul(ps_s5[idx % 2][:, :],
                                      attn_full[:, gi * 1024 + tt * 128:
                                                gi * 1024 + (tt + 1) * 128],
                                      wout_sb[os % 2][:, gi * 512:(gi + 1) * 512],
                                      start=(gi == 0), stop=(gi == 15))
                    p5["s"].inc(mm)
                    for e in range(8):
                        mm = t.matmul(ps_y5[idx % 2][:, :],
                                      at_sb[:, e * 1024 + tt * 128:
                                            e * 1024 + (tt + 1) * 128],
                                      wb_sb[os % 2][:, e * 512:(e + 1) * 512],
                                      start=(e == 0), stop=(e == 7))
                    p5["y"].inc(mm)

        @block.scalar
        def _(sc):
            for os in range(4):
                for tt in range(8):
                    idx = os * 8 + tt
                    sc.wait_ge(p5["y"].h, idx + 1)
                    if idx >= 2:
                        sc.wait_ge(p5["c"].h, idx - 1)
                    p5["sil"].inc(sc.activation(gscr[idx % 2][:, :],
                                                ps_y5[idx % 2][:, :], AF.Silu))

        @block.sync
        def _(s):
            for os in range(4):
                for tt in range(8):
                    idx = os * 8 + tt
                    s.wait_ge(p5["c"].h, idx + 1)
                    dsem["out%d" % (idx % 2)].inc(s.dma_start(
                        out=out_d[tt * 128:(tt + 1) * 128, os * 512:(os + 1) * 512],
                        in_=oscr[idx % 2][:, :]), 16)
            s.wait_ge(dsem["out0"].h, 16 * 16)
            s.wait_ge(dsem["out1"].h, 16 * 16)

        @block.vector
        def _(v):
            v.wait_ge(dsem["xo"].h, 16)
            for os in range(4):
                for tt in range(8):
                    idx = os * 8 + tt
                    v.wait_ge(p5["s"].h, idx + 1)
                    v.wait_ge(p5["sil"].h, idx + 1)
                    if idx >= 2:
                        v.wait_ge(dsem["out%d" % (idx % 2)].h,
                                  16 * ((idx - 2) // 2 + 1))
                    vbar(v, v.scalar_tensor_tensor(sgscr[idx % 2][:, :],
                                                   ps_s5[idx % 2][:, :], 1.0,
                                                   gscr[idx % 2][:, :],
                                                   ALU.mult, ALU.mult))
                    p5["c"].inc(v.tensor_tensor(
                        oscr[idx % 2][:, :], sgscr[idx % 2][:, :],
                        x_own[:, tt * D + os * 512: tt * D + (os + 1) * 512],
                        ALU.add))

        if DEBUG:
            @block.gpsimd
            def _(g):
                g.wait_ge(sd.h, sd.v)
                sg.inc(g.dma_start(out=dbg["attn"].ap(), in_=attn_dram.ap()), 16)
                sg.inc(g.dma_start(out=dbg["brows"].ap(), in_=brows_dram.ap()), 16)
                sg.inc(g.dma_start(out=dbg["xn"].ap(), in_=xn_dram.ap()), 16)
                sg.inc(g.dma_start(out=dbg["ccout"].ap(), in_=cc_out.ap()), 16)
                g.wait_ge(sg.h, sg.v)

        es_ps5.close()
    es_p5.close()
    es_p5l.close()
    return nc


def _host_inputs(x, w_qkv, w_out, router_w, Wa, Wb):
    import jax
    import jax.numpy as jnp
    x = np.asarray(x, dtype=np.float32)
    w_qkv = np.asarray(w_qkv, dtype=np.float32)
    w_out = np.asarray(w_out, dtype=np.float32)
    router_w = np.asarray(router_w, dtype=np.float32)
    Wa = np.asarray(Wa, dtype=np.float32)
    Wb = np.asarray(Wb, dtype=np.float32)

    with jax.default_device(jax.devices("cpu")[0]):
        inv = jnp.asarray(1.0) / jnp.asarray(10000.0) ** (
            jnp.arange(0, HD, 2, dtype=jnp.float32) / HD)
        tv = jnp.arange(T, dtype=jnp.float32)
        freqs = jnp.einsum("i,j->ij", tv, inv)
        emb = jnp.concatenate([freqs, freqs], axis=-1)
        cosf = np.ascontiguousarray(np.asarray(jnp.cos(emb)).T)
        sinf = np.ascontiguousarray(np.asarray(jnp.sin(emb)).T)

    R = np.zeros((128, 128), dtype=np.float32)
    half = np.arange(64)
    R[half, half + 64] = -1.0
    R[half + 64, half] = 1.0
    rmat = np.ascontiguousarray(R.T)

    ii, jj = np.indices((128, 128))
    maskS = np.where(jj <= ii, 0.0, NEG).astype(np.float32)
    maskST = np.where(ii <= jj, 1.0, 0.0).astype(np.float32)
    identf = np.eye(128, dtype=np.float32)

    wa_arr = np.ascontiguousarray(Wa.transpose(1, 0, 2).reshape(D, 1024))
    wb_arr = np.ascontiguousarray(Wb.reshape(8, 128, 4, 512).transpose(2, 0, 1, 3))
    router_arr = np.ascontiguousarray(router_w.reshape(16, 128, 8))
    woutT = w_out.T

    in_maps = []
    for c in range(8):
        b, hg = c // 2, c % 2
        hs = slice(hg * 1024, (hg + 1) * 1024)
        wq_l = w_qkv[hs, :].reshape(HL, HD, D)
        wk_l = w_qkv[2048 + hg * 1024: 2048 + (hg + 1) * 1024, :].reshape(HL, HD, D)
        wv_l = w_qkv[4096 + hg * 1024: 4096 + (hg + 1) * 1024, :].reshape(HL, HD, D)
        wqkv_arr = np.ascontiguousarray(
            np.concatenate([wq_l.transpose(0, 2, 1), wk_l.transpose(0, 2, 1),
                            wv_l.transpose(0, 2, 1)], axis=2))
        os_ = slice((1 - hg) * 1024, (2 - hg) * 1024)
        wout_local = np.concatenate([woutT[hs], woutT[os_]], axis=0)
        wout_arr = np.ascontiguousarray(
            wout_local.reshape(D, 4, 512).transpose(1, 0, 2))
        in_maps.append({
            "xb": np.ascontiguousarray(x[b]),
            "wqkv": wqkv_arr,
            "wout": wout_arr,
            "router": router_arr,
            "wa": wa_arr,
            "wb": wb_arr,
            "cosf": cosf, "sinf": sinf, "rmat": rmat, "identf": identf,
            "maskS": maskS, "maskST": maskST,
            "ones_row": np.ones((1, 128), np.float32),
            "ones_col": np.ones((128, 1), np.float32),
            "one1": np.ones((1, 1), np.float32),
        })
    return in_maps


def kernel(L0_attn_x, L0_attn_w_qkv, L0_attn_w_out, L0_ffn_router_w,
           L0_ffn_Wa, L0_ffn_Wb):
    in_maps = _host_inputs(L0_attn_x, L0_attn_w_qkv, L0_attn_w_out,
                           L0_ffn_router_w, L0_ffn_Wa, L0_ffn_Wb)
    nc = build_nc()
    res = run_bass_kernel_spmd(nc, in_maps, core_ids=list(range(8)))
    out = np.empty((4, T, D), dtype=np.float32)
    for c in range(8):
        b, hg = c // 2, c % 2
        out[b, hg * 1024:(hg + 1) * 1024] = res.results[c]["out"]
    if DEBUG:
        kernel.last_results = res
    return out


# version 14
# speedup vs baseline: 1.1874x; 1.0821x over previous
import contextlib
import numpy as np
import concourse.bass as bass
import concourse.mybir as mybir
from concourse.bass_utils import run_bass_kernel_spmd

F32 = mybir.dt.float32
BF16 = mybir.dt.bfloat16
AF = mybir.ActivationFunctionType
ALU = mybir.AluOpType
AX = mybir.AxisListType.X

T = 2048
D = 2048
HL = 8
HD = 128
NT = 16
NF = 16
NEG = -1e30
EPS = 1e-5
SCALE = float(HD) ** -0.5

DEBUG = False


class Sem:
    __slots__ = ("h", "v")

    def __init__(self, h):
        self.h = h
        self.v = 0

    def inc(self, inst, n=1):
        self.v += n
        return inst.then_inc(self.h, n)


def build_nc():
    nc = bass.Bass("TRN2", target_bir_lowering=False,
                   detect_race_conditions=False)
    es = contextlib.ExitStack()

    din = {}
    for name, shape in [
        ("xb", [T, D]), ("wqkv", [HL, D, 3 * HD]),
        ("wout", [4, D, 512]), ("router", [16, 128, 8]),
        ("wa", [D, 1024]), ("wb", [4, 8, 128, 512]),
        ("cosf", [128, T]), ("sinf", [128, T]),
        ("rmat", [128, 128]), ("identf", [128, 128]),
        ("maskS", [128, 128]), ("maskST", [128, 128]),
        ("ones_row", [1, 128]), ("ones_col", [128, 1]), ("one1", [1, 1]),
    ]:
        din[name] = nc.dram_tensor(name, shape, F32, kind="ExternalInput")
    out_d = nc.dram_tensor("out", [1024, D], F32, kind="ExternalOutput")

    xn_dram = nc.dram_tensor("xn_dram", [128, NF * T], BF16)
    attn_dram = nc.dram_tensor("attn_dram", [HL, 128, T], BF16)
    brows_dram = nc.dram_tensor("brows_dram", [128, 128], F32)
    cc_in = nc.dram_tensor("cc_in", [HL, 128, 1024], BF16)
    cc_out = nc.dram_tensor("cc_out", [2, HL, 128, 1024], BF16)
    groups = [[0, 1], [2, 3], [4, 5], [6, 7]]

    dbg = {}
    if DEBUG:
        for nm, shp, dt in [("attn", [HL, 128, T], BF16),
                            ("brows", [128, 128], F32),
                            ("xn", [128, NF * T], BF16),
                            ("q", [128, HL * T], BF16),
                            ("k", [128, HL * T], BF16),
                            ("v", [128, HL * T], BF16),
                            ("pscr", [128, T], BF16),
                            ("ssum", [128, 128], F32),
                            ("ccout", [2, HL, 128, 1024], BF16),
                            ("expb0", [128, 512], BF16),
                            ("mexpb0", [128, 512], BF16),
                            ("brow0", [1, T], F32),
                            ("abuf0", [128, 512], BF16)]:
            dbg[nm] = nc.dram_tensor("dbg_" + nm, shp, dt, kind="ExternalOutput")

    def sbuf(stack, name, shape, dt, side="left"):
        return stack.enter_context(nc.sbuf_tensor(name, shape, dt, side=side))

    def psum(stack, name, shape, dt):
        return stack.enter_context(nc.psum_tensor(name, shape, dt))

    sems = {}

    def S(name):
        s = Sem(es.enter_context(nc.semaphore(name)))
        sems[name] = s
        return s

    sd = S("sd")
    sg = S("sg")
    dsem = {n: S("dma_" + n) for n in
            ["cst", "cs2", "wq", "wk", "wv", "x0", "x1", "xnb", "brow",
             "at0", "at1", "xno", "wa", "rw", "ful", "xo",
             "wo0", "wo1", "out0", "out1", "dbg2", "brr"]}
    vsf = S("vsf")
    ssf = S("ssf")

    def vbar(v, inst):
        vsf.v += 1
        inst.then_inc(vsf.h, 1)
        v.wait_ge(vsf.h, vsf.v)

    def sbar(sc, inst):
        ssf.v += 1
        inst.then_inc(ssf.h, 1)
        sc.wait_ge(ssf.h, ssf.v)
    ld = {}

    identb = sbuf(es, "identb", [128, 128], BF16)
    onesr_sb = sbuf(es, "onesr_sb", [1, 128], F32)
    onesc_sb = sbuf(es, "onesc_sb", [128, 1], BF16)
    one1_sb = sbuf(es, "one1_sb", [1, 1], F32)

    es_p2c = contextlib.ExitStack()
    xn_fm = sbuf(es_p2c, "xn_fm", [128, NF * T], BF16)
    identf = sbuf(es_p2c, "identf_sb", [128, 128], F32)
    maskS_sb = sbuf(es_p2c, "maskS_sb", [128, 128], BF16)
    maskST_sb = sbuf(es_p2c, "maskST_sb", [128, 128], BF16)

    es_rope = contextlib.ExitStack()
    cos_sb = sbuf(es_rope, "cos_sb", [128, T], BF16)
    sin_sb = sbuf(es_rope, "sin_sb", [128, T], BF16)
    rmat_sb = sbuf(es_rope, "rmat_sb", [128, 128], BF16)

    es_p1s = contextlib.ExitStack()
    wslab = [sbuf(es_p1s, f"wslab{i}", [128, NF * 3 * HD], BF16) for i in range(2)]
    praw = [sbuf(es_p1s, f"praw{i}", [128, 512], BF16) for i in range(2)]
    t1s = sbuf(es_p1s, "t1s", [128, 512], BF16)
    t2s = sbuf(es_p1s, "t2s", [128, 512], BF16)
    p1stat = sbuf(es_p1s, "p1stat", [128, 64], F32)
    pscr1 = sbuf(es_p1s, "pscr1", [128, 1024], BF16)

    es_mid = contextlib.ExitStack()

    p0 = {n: S("p0_" + n) for n in ["lnv", "lns", "st", "rst", "xn", "tr", "ev"]}
    es_p0 = contextlib.ExitStack()
    xstg = [sbuf(es_p0, f"xstg{i}", [128, D], F32, side="right") for i in range(2)]
    sqscr = [sbuf(es_p0, f"sqscr{i}", [128, D], BF16, side="right") for i in range(2)]
    xn_tok = [sbuf(es_p0, f"xn_tok{i}", [128, D], BF16, side="right") for i in range(2)]
    stat = sbuf(es_p0, "stat", [128, 8 * NT], F32, side="right")
    c_ = lambda j, tt: stat[:, j * NT + tt: j * NT + tt + 1]
    es_ptr = contextlib.ExitStack()
    ps_tr = [psum(es_ptr, f"ps_tr{i}", [128, 512], BF16) for i in range(2)]

    with nc.Block("p0") as block:
        @block.gpsimd
        def _(g):
            dsem["cst"].inc(g.dma_start(out=identb[:, :], in_=din["identf"][:, :]), 16)
            dsem["cst"].inc(g.dma_start(out=cos_sb[:, :], in_=din["cosf"][:, :]), 16)
            dsem["cst"].inc(g.dma_start(out=sin_sb[:, :], in_=din["sinf"][:, :]), 16)
            dsem["cst"].inc(g.dma_start(out=rmat_sb[:, :], in_=din["rmat"][:, :]), 16)
            dsem["cst"].inc(g.dma_start(out=maskS_sb[:, :], in_=din["maskS"][:, :]), 16)
            dsem["cst"].inc(g.dma_start(out=maskST_sb[:, :], in_=din["maskST"][:, :]), 16)
            dsem["cst"].inc(g.dma_start(out=onesc_sb[:, :], in_=din["ones_col"][:, :]), 16)
            for h in range(2):
                dsem["wq" if h % 2 == 0 else "wk"].inc(g.dma_start(
                    out=wslab[h % 2][:, :].rearrange("p (f c) -> p f c", f=NF),
                    in_=din["wqkv"][h].rearrange("(f p) c -> p f c", p=128)), 16)

        @block.sync
        def _(s):
            dsem["cs2"].inc(s.dma_start(out=identf[:, :], in_=din["identf"][:, :]), 16)
            dsem["cs2"].inc(s.dma_start(out=onesr_sb[:, :], in_=din["ones_row"][:, :]), 16)
            dsem["cs2"].inc(s.dma_start(out=one1_sb[:, :], in_=din["one1"][:, :]), 16)
            for tt in range(NT):
                if tt >= 2:
                    s.wait_ge(p0["xn"].h, tt - 1)
                    s.wait_ge(p0["lns"].h, tt - 1)
                dsem["x%d" % (tt % 2)].inc(
                    s.dma_start(out=xstg[tt % 2][:, :],
                                in_=din["xb"][tt * 128:(tt + 1) * 128, :]), 16)

        def xwait(eng, tt):
            eng.wait_ge(dsem["x%d" % (tt % 2)].h, 16 * (tt // 2 + 1))

        @block.vector
        def _(v):
            for tt in range(NT):
                xwait(v, tt)
                vbar(v, v.reduce_sum(c_(0, tt), xstg[tt % 2][:, :], axis=AX))
                v.wait_ge(p0["lns"].h, tt + 1)
                vbar(v, v.tensor_scalar(c_(2, tt), c_(0, tt), 1.0 / D, None,
                                        ALU.mult))
                vbar(v, v.tensor_tensor(c_(3, tt), c_(2, tt), c_(2, tt), ALU.mult))
                vbar(v, v.tensor_scalar(c_(4, tt), c_(1, tt), 1.0 / D, c_(3, tt),
                                        ALU.mult, ALU.subtract))
                vbar(v, v.tensor_scalar(c_(5, tt), c_(4, tt), EPS, None, ALU.add))
                p0["st"].inc(v.reciprocal(c_(6, tt), c_(5, tt)))
                v.wait_ge(p0["rst"].h, tt + 1)
                if tt >= 2:
                    v.wait_ge(p0["tr"].h, 4 * (tt - 1))
                p0["xn"].inc(
                    v.tensor_scalar(xn_tok[tt % 2][:, :], xstg[tt % 2][:, :],
                                    c_(2, tt), c_(7, tt), ALU.subtract, ALU.mult))

        @block.tensor
        def _(t):
            t.wait_ge(dsem["cst"].h, 112)
            for tt in range(NT):
                t.wait_ge(p0["xn"].h, tt + 1)
                for g4 in range(4):
                    idx = tt * 4 + g4
                    if idx >= 2:
                        t.wait_ge(p0["ev"].h, idx - 1)
                    for j in range(4):
                        fc = g4 * 4 + j
                        tr = t.transpose(ps_tr[idx % 2][:, j * 128:(j + 1) * 128],
                                         xn_tok[tt % 2][:, fc * 128:(fc + 1) * 128],
                                         identb[:, :])
                    p0["tr"].inc(tr)

        @block.scalar
        def _(sc):
            for tt in range(NT):
                xwait(sc, tt)
                p0["lns"].inc(
                    sc.activation(sqscr[tt % 2][:, :], xstg[tt % 2][:, :],
                                  AF.Square, accum_out=c_(1, tt)))
                sc.wait_ge(p0["st"].h, tt + 1)
                p0["rst"].inc(sc.activation(c_(7, tt), c_(6, tt), AF.Sqrt))
                if tt >= 1:
                    for g4 in range(4):
                        idx = (tt - 1) * 4 + g4
                        sc.wait_ge(p0["tr"].h, idx + 1)
                        dst = xn_fm[:, :].rearrange("p (f t) -> p f t", f=NF)[
                            :, g4 * 4:(g4 + 1) * 4, (tt - 1) * 128: tt * 128]
                        p0["ev"].inc(sc.copy(
                            dst, ps_tr[idx % 2][:, :].rearrange("p (f t) -> p f t", f=4)))
            for g4 in range(4):
                idx = (NT - 1) * 4 + g4
                sc.wait_ge(p0["tr"].h, idx + 1)
                dst = xn_fm[:, :].rearrange("p (f t) -> p f t", f=NF)[
                    :, g4 * 4:(g4 + 1) * 4, (NT - 1) * 128: NT * 128]
                p0["ev"].inc(sc.copy(
                    dst, ps_tr[idx % 2][:, :].rearrange("p (f t) -> p f t", f=4)))

    es_p0.close()
    es_ptr.close()

    q_sb = sbuf(es_mid, "q_sb", [128, HL * T], BF16, side="right")
    k_sb = sbuf(es_mid, "k_sb", [128, HL * T], BF16, side="right")
    v_sb = sbuf(es_mid, "v_sb", [128, HL * T], BF16, side="right")
    ssum = sbuf(es_mid, "ssum", [128, 128], F32, side="right")
    es_pq = contextlib.ExitStack()
    ps_q = [psum(es_pq, f"ps_q{i}", [128, 512], F32) for i in range(3)]
    ps_r = psum(es_pq, "ps_r", [128, 512], F32)
    ps_s = [psum(es_pq, f"ps_s{i}", [128, 1024], F32) for i in range(2)]

    CH = [(h, m, ts) for h in range(HL) for m in range(3) for ts in range(4)]
    CIDX = {c: i for i, c in enumerate(CH)}
    RCH = [(h, m, ts) for h in range(HL) for m in range(2) for ts in range(4)]
    RIDX = {c: i for i, c in enumerate(RCH)}
    P1CH = []
    for h in range(HL):
        for qt in range(NT):
            kend = (qt + 1) * 128
            if kend <= 1024:
                P1CH.append((h, qt, 0, 0, kend, True))
            else:
                P1CH.append((h, qt, 0, 0, 1024, False))
                P1CH.append((h, qt, 1, 1024, kend - 1024, False))
    P1OF = {}
    for i, (h, qt, ci, k0, w, sg_) in enumerate(P1CH):
        P1OF[(h, qt, ci)] = i
    P1H = {h: [i for i, c in enumerate(P1CH) if c[0] == h] for h in range(HL)}
    QKV_SB = [q_sb, k_sb, v_sb]

    p1 = {n: S("p1_" + n) for n in ["m", "e", "r", "v", "sm", "x", "he"]}

    with nc.Block("p1s") as block:
        @block.gpsimd
        def _(g):
            for h in range(2, HL):
                g.wait_ge(p1["m"].h, 12 * (h - 1))
                dsem["wq" if h % 2 == 0 else "wk"].inc(g.dma_start(
                    out=wslab[h % 2][:, :].rearrange("p (f c) -> p f c", f=NF),
                    in_=din["wqkv"][h].rearrange("(f p) c -> p f c", p=128)), 16)

        @block.tensor
        def _(t):
            t.wait_ge(dsem["cst"].h, 112)

            def emit_p1(i):
                h, qt, ci, k0, w, single = P1CH[i]
                if i >= 2:
                    t.wait_ge(p1["x"].h, i - 1)
                last = single or ci == 1
                qtile = q_sb[:, h * T + qt * 128: h * T + (qt + 1) * 128]
                hi = w - 128 if last else w
                for j in range((hi + 511) // 512):
                    lo = j * 512
                    n = min(512, hi - lo)
                    mm = t.matmul(
                        ps_s[i % 2][:, lo: lo + n], qtile,
                        k_sb[:, h * T + k0 + lo: h * T + k0 + lo + n],
                        start=True, stop=True)
                if last:
                    t.matmul(ps_s[i % 2][:, w - 128:w], identb[:, :],
                             maskS_sb[:, :], start=True, stop=False)
                    mm = t.matmul(
                        ps_s[i % 2][:, w - 128: w], qtile,
                        k_sb[:, h * T + k0 + w - 128: h * T + k0 + w],
                        start=False, stop=True)
                p1["sm"].inc(mm)

            def emit_rot(ri):
                h, m, ts = RCH[ri]
                cq = CIDX[(h, m, ts)]
                t.wait_ge(p1["e"].h, cq + 1)
                if ri >= 1:
                    t.wait_ge(p1["v"].h, ri)
                p1["r"].inc(t.matmul(ps_r[:, :], rmat_sb[:, :],
                                     praw[ri % 2][:, :], start=True, stop=True))

            for cq, (h, m, ts) in enumerate(CH):
                t.wait_ge(p0["ev"].h, 16 * (ts + 1))
                t.wait_ge(dsem["wq" if h % 2 == 0 else "wk"].h,
                          16 * (h // 2 + 1))
                if cq >= 3:
                    t.wait_ge(p1["e"].h, cq - 2)
                for fc in range(NF):
                    mm = t.matmul(
                        ps_q[cq % 3][:, :],
                        wslab[h % 2][:, fc * 3 * HD + m * HD:
                                     fc * 3 * HD + (m + 1) * HD],
                        xn_fm[:, fc * T + ts * 512: fc * T + (ts + 1) * 512],
                        start=(fc == 0), stop=(fc == NF - 1))
                p1["m"].inc(mm)
                if m < 2:
                    ri = RIDX[(h, m, ts)]
                    if ri >= 1 and ri % 8 != 0:
                        emit_rot(ri - 1)
                if m == 2 and ts == 0:
                    emit_rot(RIDX[(h, 1, 3)])
                if h >= 1:
                    c12 = m * 4 + ts
                    if c12 == 0:
                        t.wait_ge(p1["v"].h, 8 * h)
                    per = len(P1H[h - 1])
                    lo = c12 * per // 12
                    hi = (c12 + 1) * per // 12
                    for i in P1H[h - 1][lo:hi]:
                        emit_p1(i)
            t.wait_ge(p1["v"].h, 8 * HL)
            for i in P1H[HL - 1]:
                emit_p1(i)

        @block.scalar
        def _(sc):
            nev = [0]

            def emit_exp(i):
                h, qt, ci, k0, w, single = P1CH[i]
                sc.wait_ge(p1["sm"].h, i + 1)
                j = qt - 8
                par = (h % 2) * 8
                if single:
                    acc = ssum[:, h * 16 + qt: h * 16 + qt + 1]
                elif ci == 0:
                    if j == 0 and h >= 2:
                        sc.wait_ge(p1["he"].h, h - 1)
                    acc = ssum[:, h * 16 + qt: h * 16 + qt + 1]
                else:
                    acc = p1stat[:, par + j: par + j + 1]
                p1["x"].inc(sc.activation(pscr1[:, 0:w], ps_s[i % 2][:, 0:w],
                                          AF.Exp, accum_out=acc))

            for cq, (h, m, ts) in enumerate(CH):
                sc.wait_ge(p1["m"].h, cq + 1)
                if m < 2:
                    ri = RIDX[(h, m, ts)]
                    if ri >= 2:
                        sc.wait_ge(p1["v"].h, ri - 1)
                    p1["e"].inc(sc.activation(praw[ri % 2][:, :],
                                              ps_q[cq % 3][:, :], AF.Copy,
                                              scale=SCALE if m == 0 else 1.0))
                else:
                    p1["e"].inc(sc.activation(
                        v_sb[:, h * T + ts * 512: h * T + (ts + 1) * 512],
                        ps_q[cq % 3][:, :], AF.Copy))
                if h >= 1:
                    c12 = m * 4 + ts
                    per = len(P1H[h - 1])
                    lo = c12 * per // 12
                    hi = (c12 + 1) * per // 12
                    for i in P1H[h - 1][lo:hi]:
                        emit_exp(i)
            for i in P1H[HL - 1]:
                emit_exp(i)

        @block.vector
        def _(v):
            v.wait_ge(dsem["cst"].h, 112)
            nrope = [0]

            def emit_rope_v(ri):
                h, m, ts = RCH[ri]
                v.wait_ge(p1["r"].h, ri + 1)
                v.tensor_tensor(t2s[:, :], ps_r[:, :],
                                sin_sb[:, ts * 512:(ts + 1) * 512], ALU.mult)
                vbar(v, v.tensor_tensor(t1s[:, :], praw[ri % 2][:, :],
                                        cos_sb[:, ts * 512:(ts + 1) * 512],
                                        ALU.mult))
                p1["v"].inc(v.tensor_tensor(
                    QKV_SB[m][:, h * T + ts * 512: h * T + (ts + 1) * 512],
                    t1s[:, :], t2s[:, :], ALU.add))

            def emit_combine(h):
                par = (h % 2) * 8
                v.wait_ge(p1["x"].h, P1OF[(h, NT - 1, 1)] + 1)
                p1["he"].inc(v.tensor_tensor(
                    ssum[:, h * 16 + 8: h * 16 + 16],
                    ssum[:, h * 16 + 8: h * 16 + 16],
                    p1stat[:, par: par + 8], ALU.add))

            for cq, (h, m, ts) in enumerate(CH):
                if m < 2:
                    ri = RIDX[(h, m, ts)]
                    emit_rope_v(ri)
                if h >= 2 and m == 2 and ts == 3:
                    emit_combine(h - 2)
            emit_combine(HL - 2)
            emit_combine(HL - 1)

    es_pq.close()
    es_p1s.close()
    es_rope.close()

    abuf = [sbuf(es_mid, f"abuf{i}", [128, 512], BF16, side="right")
            for i in range(2)]
    pscr = sbuf(es_mid, "pscr", [128, T], BF16, side="right")
    bh = sbuf(es_mid, "bh", [128, 128], F32, side="right")
    rsum = sbuf(es_mid, "rsum", [128, 128], F32, side="right")
    browb = [sbuf(es_mid, f"browb{i}", [1, T], BF16, side="right")
             for i in range(2)]
    onesr_bf = sbuf(es_mid, "onesr_bf", [1, 128], BF16, side="right")
    expb = [sbuf(es_mid, f"expb{i}", [128, 512], BF16, side="right") for i in range(2)]
    mexpb = [sbuf(es_mid, f"mexpb{i}", [128, 512], BF16, side="right") for i in range(2)]

    ccsem = S("ccsem")
    p2 = {n: S("p2_" + n) for n in ["s1", "msk", "exp", "st", "tB", "eB",
                                    "dg", "pt", "avd", "ae", "eb", "bc",
                                    "ebe", "pb", "vt", "vte"]}

    def kchunks(qt):
        kend = (qt + 1) * 128
        return [(kc * 512, min(512, kend - kc * 512))
                for kc in range((kend + 511) // 512)]

    EV2 = [(h, qs, kt) for h in range(HL) for qs in range(4)
           for kt in range(4 * qs + 4)]
    EIDX = {e: i for i, e in enumerate(EV2)}
    LASTE = {(h, qs): EIDX[(h, qs, 4 * qs + 3)] for h in range(HL)
             for qs in range(4)}
    DGC = np.cumsum([1 if kt >= 4 * qs else 0 for (h, qs, kt) in EV2]).tolist()
    brow_done = {}
    adma_done = {}

    with nc.Block("p2") as block:
        es_ps2 = contextlib.ExitStack()
        ps_b = psum(es_ps2, "ps_b", [128, 128], F32)
        ps_vt = psum(es_ps2, "ps_vt", [128, 512], BF16)

        @block.sync
        def _(s):
            s.wait_ge(p0["ev"].h, 64)
            dsem["xnb"].inc(s.dma_start(out=xn_dram[:, :], in_=xn_fm[:, :]), 16)

        @block.tensor
        def _(t):
            t.wait_ge(p2["st"].h, 1)
            t.wait_ge(dsem["cs2"].h, 48)
            p2["tB"].inc(t.transpose(ps_b[:, :], rsum[:, :], identf[:, :]))
            for h in range(HL):
                for g4 in range(4):
                    gi = h * 4 + g4
                    if gi >= 1:
                        t.wait_ge(p2["vte"].h, gi)
                    for j in range(4):
                        kt = g4 * 4 + j
                        tr = t.transpose(ps_vt[:, j * 128:(j + 1) * 128],
                                         v_sb[:, h * T + kt * 128:
                                              h * T + (kt + 1) * 128],
                                         identb[:, :])
                    p2["vt"].inc(tr)

        @block.scalar
        def _(sc):
            for gi in range(32):
                h, g4 = gi // 4, gi % 4
                sc.wait_ge(p2["vt"].h, gi + 1)
                p2["vte"].inc(sc.copy(
                    v_sb[:, h * T + g4 * 512: h * T + (g4 + 1) * 512],
                    ps_vt[:, :]))

        @block.vector
        def _(v):
            p2["st"].inc(v.reciprocal(rsum[:, :], ssum[:, :]))
            v.wait_ge(p2["tB"].h, 1)
            p2["eB"].inc(v.tensor_copy(bh[:, :], ps_b[:, :]))

        @block.sync
        def _(s):
            s.wait_ge(p2["eB"].h, 1)
            dsem["brow"].inc(s.dma_start(out=brows_dram[:, :], in_=bh[:, :]), 16)
            s.wait_ge(dsem["brow"].h, 16)

            def emit_chunk(hh, qs):
                gidx = hh * 4 + qs
                s.wait_ge(p2["ae"].h, gidx + 1)
                dsem["at%d" % (gidx % 2)].inc(s.dma_start(
                    out=attn_dram[hh, :, qs * 512:(qs + 1) * 512],
                    in_=abuf[gidx % 2][:, :]), 16)

            for h in range(1, HL):
                for qs in range(4):
                    emit_chunk(h - 1, qs)
            for qs in range(4):
                emit_chunk(HL - 1, qs)

        @block.gpsimd
        def _(g):
            dsem["cst"].inc(g.dma_start(out=onesr_bf[:, :], in_=din["ones_row"][:, :]), 16)
            g.wait_ge(dsem["brow"].h, 16)
            for h in range(HL):
                if h >= 2:
                    g.wait_ge(p2["bc"].h, 4 * (h - 1))
                if DEBUG and h == 2:
                    dsem["dbg2"].inc(g.dma_start(out=dbg["brow0"].ap(),
                                                 in_=browb[0][:, :]), 16)
                    g.wait_ge(dsem["dbg2"].h, 16)
                dsem["brr"].inc(g.dma_start(
                    out=browb[h % 2][:, :],
                    in_=brows_dram[:, :].rearrange("(h r) c -> h (r c)", h=HL)[
                        h:h + 1, :]), 16)
            if DEBUG:
                g.wait_ge(p2["pb"].h, 1)
                dsem["dbg2"].inc(g.dma_start(out=dbg["expb0"].ap(), in_=expb[0][:, :]), 16)
                dsem["dbg2"].inc(g.dma_start(out=dbg["mexpb0"].ap(), in_=mexpb[0][:, :]), 16)
                g.wait_ge(p2["ae"].h, 1)
                dsem["dbg2"].inc(g.dma_start(out=dbg["abuf0"].ap(), in_=abuf[0][:, :]), 16)
            pid = g.partition_id()
            oth = (1 - pid % 2) * 1024
            g.wait_ge(dsem["at0"].h, 16 * 16)
            g.wait_ge(dsem["at1"].h, 16 * 16)
            sg.inc(g.dma_start(out=cc_in[:, :, :],
                               in_=attn_dram[:, :, bass.ds(oth, 1024)]), 16)
            g.wait_ge(sg.h, sg.v)
            ccsem.inc(g.collective_compute(
                "AllGather", ALU.bypass, replica_groups=groups,
                ins=[cc_in.ap().opt()], outs=[cc_out.ap().opt()]))
            if DEBUG:
                sg.inc(g.dma_start(out=dbg["ssum"].ap(), in_=ssum[:, :]), 16)
                sg.inc(g.dma_start(out=dbg["q"].ap(), in_=q_sb[:, :]), 16)
                sg.inc(g.dma_start(out=dbg["k"].ap(), in_=k_sb[:, :]), 16)
                sg.inc(g.dma_start(out=dbg["v"].ap(), in_=v_sb[:, :]), 16)
                sg.inc(g.dma_start(out=dbg["pscr"].ap(), in_=pscr[:, :]), 16)
                g.wait_ge(sg.h, sg.v)

        es_ps2b = contextlib.ExitStack()
        ps_st = [psum(es_ps2b, f"ps_st{i}", [128, 512], F32) for i in range(3)]
        ps_av = [psum(es_ps2b, f"ps_av{i}", [128, 512], F32) for i in range(2)]
        ps_bc = [psum(es_ps2b, "ps_bc0", [128, 512], F32)] * 2

        @block.tensor
        def _(t):
            def OFF(qs_, kt_):
                return max(0, kt_ - 4 * qs_) * 128

            t.wait_ge(dsem["cst"].h, 128)
            next_av = [0]

            def emit_av():
                j = next_av[0]
                jh, jqs, jkt = EV2[j]
                jo = OFF(jqs, jkt)
                t.wait_ge(p2["vte"].h, 4 * (jh + 1))
                t.wait_ge(p2["pb"].h, j + 1)
                p2["avd"].inc(t.matmul(
                    ps_av[(jh * 4 + jqs) % 2][:, jo:512],
                    v_sb[:, jh * T + jkt * 128: jh * T + (jkt + 1) * 128],
                    pscr[:, (j % 4) * 512 + jo:(j % 4 + 1) * 512],
                    start=(jkt == 0), stop=(jkt == 4 * jqs + 3)))
                next_av[0] += 1

            for i, (h, qs, kt) in enumerate(EV2):
                if kt == 0:
                    while next_av[0] < i:
                        emit_av()
                    gidx = h * 4 + qs
                    if gidx >= 2:
                        t.wait_ge(p2["ae"].h, gidx - 1)
                    t.wait_ge(dsem["brr"].h, 16 * (h + 1))
                    if gidx >= 1:
                        t.wait_ge(p2["ebe"].h, gidx)
                    p2["bc"].inc(t.matmul(
                        ps_bc[gidx % 2][:, :], onesr_bf[:, :],
                        browb[h % 2][0:1, qs * 512:(qs + 1) * 512],
                        start=True, stop=True))
                if i >= 3:
                    t.wait_ge(p2["pt"].h, i - 2)
                off = OFF(qs, kt)
                p2["dg"].inc(t.matmul(
                    ps_st[i % 3][:, off:512],
                    k_sb[:, h * T + kt * 128: h * T + (kt + 1) * 128],
                    q_sb[:, h * T + qs * 512 + off: h * T + (qs + 1) * 512],
                    start=True, stop=True))
                while i - next_av[0] >= 3:
                    emit_av()
            while next_av[0] < len(EV2):
                emit_av()

        @block.vector
        def _(v):
            def OFF(qs_, kt_):
                return max(0, kt_ - 4 * qs_) * 128

            tri_b = bass.AP(maskST_sb[:, :].tensor, 0,
                            [[128, 128], [0, 4], [1, 128]])

            def biasmul(i):
                jh, jqs, jkt = EV2[i]
                jo = OFF(jqs, jkt)
                jg = jh * 4 + jqs
                v.wait_ge(p2["pt"].h, i + 1)
                qsl = slice((i % 4) * 512 + jo, (i % 4 + 1) * 512)
                if jkt >= 4 * jqs:
                    mtt = v.tensor_tensor(
                        pscr[:, (i % 4) * 512 + jo:(i % 4) * 512 + jo + 128],
                        pscr[:, (i % 4) * 512 + jo:(i % 4) * 512 + jo + 128],
                        mexpb[jg % 2][:, jo:jo + 128], ALU.mult)
                    if jo + 128 < 512:
                        p2["pb"].inc(v.tensor_tensor(
                            pscr[:, (i % 4) * 512 + jo + 128:(i % 4 + 1) * 512],
                            pscr[:, (i % 4) * 512 + jo + 128:(i % 4 + 1) * 512],
                            expb[jg % 2][:, jo + 128:512], ALU.mult))
                    else:
                        p2["pb"].inc(mtt)
                else:
                    p2["pb"].inc(v.tensor_tensor(
                        pscr[:, qsl], pscr[:, qsl],
                        expb[jg % 2][:, jo:512], ALU.mult))

            for i, (h, qs, kt) in enumerate(EV2):
                if i >= 1:
                    biasmul(i - 1)
                if kt == 0:
                    gidx = h * 4 + qs
                    if DEBUG and gidx == 2:
                        v.wait_ge(dsem["dbg2"].h, 64)
                    v.wait_ge(p2["bc"].h, gidx + 1)
                    p2["ebe"].inc(v.tensor_copy(expb[gidx % 2][:, :],
                                                ps_bc[gidx % 2][:, :]))
                    v.wait_ge(p2["ebe"].h, gidx + 1)
                    v.tensor_tensor(
                        mexpb[gidx % 2][:, :].rearrange("p (a b) -> p a b", a=4),
                        expb[gidx % 2][:, :].rearrange("p (a b) -> p a b", a=4),
                        tri_b, ALU.mult)
            biasmul(len(EV2) - 1)

        @block.scalar
        def _(sc):
            for i, (h, qs, kt) in enumerate(EV2):
                gidx = h * 4 + qs
                sc.wait_ge(p2["dg"].h, i + 1)
                if i >= 4:
                    sc.wait_ge(p2["avd"].h, i - 3)
                off = max(0, kt - 4 * qs) * 128
                p2["pt"].inc(sc.activation(
                    pscr[:, (i % 4) * 512 + off:(i % 4 + 1) * 512],
                    ps_st[i % 3][:, off:512], AF.Exp))
                if kt == 4 * qs + 3:
                    sc.wait_ge(p2["avd"].h, i + 1)
                    if gidx >= 2:
                        sc.wait_ge(dsem["at%d" % (gidx % 2)].h,
                                   16 * ((gidx - 2) // 2 + 1))
                    p2["ae"].inc(sc.copy(abuf[gidx % 2][:, :],
                                         ps_av[gidx % 2][:, :]))

        es_ps2b.close()
        es_ps2.close()

    es_mid.close()
    es_p2c.close()


    es_p5l = contextlib.ExitStack()
    attn_full = sbuf(es_p5l, "attn_full", [128, 16 * 1024], BF16)
    at_sb = sbuf(es_p5l, "at_sb", [128, 8 * 1024], BF16)
    wout_sb = [sbuf(es_p5l, f"wout_sb{i}", [128, 16 * 512], BF16) for i in range(2)]
    wb_sb = [sbuf(es_p5l, f"wb_sb{i}", [128, 8 * 512], BF16) for i in range(2)]
    wdone = {}
    es_p34 = contextlib.ExitStack()
    xn_own = sbuf(es_p34, "xn_own", [128, NF * 1024], BF16)
    wa_sb = sbuf(es_p34, "wa_sb", [128, NF * 1024], BF16)
    rw_sb = sbuf(es_p34, "rw_sb", [128, 128], BF16)
    as_sb = sbuf(es_p34, "as_sb", [128, 8 * 1024], BF16)
    sqb = [sbuf(es_p34, f"sqb{i}", [128, 1024], BF16) for i in range(2)]
    ssrow = sbuf(es_p34, "ssrow", [1, 1024], F32)
    nrow = sbuf(es_p34, "nrow", [1, 1024], F32)
    rsrow = sbuf(es_p34, "rsrow", [1, 1024], F32)
    rscol = sbuf(es_p34, "rscol", [128, 8], F32)
    lcol = sbuf(es_p34, "lcol", [128, 8], F32)
    ecol = sbuf(es_p34, "ecol", [128, 8], F32)
    mrcol = sbuf(es_p34, "mrcol", [128, 8], F32)
    indcol = sbuf(es_p34, "indcol", [128, 8], F32)
    cmb = sbuf(es_p34, "cmb", [128, 64], F32)
    tstat = sbuf(es_p34, "tstat", [128, 8], F32)

    p3 = {n: S("p3_" + n) for n in ["sq", "ss", "se", "rs", "rc", "lg", "le",
                                    "tp", "am", "ae", "at", "ate"]}

    with nc.Block("p34") as block:
        @block.gpsimd
        def _(g):
            pid = g.partition_id()
            own = (pid % 2) * 1024
            partner = 1 - (pid % 2)
            g.wait_ge(dsem["xnb"].h, 16)
            dsem["xno"].inc(g.dma_start(
                out=xn_own[:, :].rearrange("p (f t) -> p f t", f=NF),
                in_=xn_dram[:, :].rearrange("p (f t) -> p f t", f=NF)[
                    :, :, bass.ds(own, 1024)]), 16)
            dsem["wa"].inc(g.dma_start(
                out=wa_sb[:, :].rearrange("p (f c) -> p f c", f=NF),
                in_=din["wa"][:, :].rearrange("(f p) c -> p f c", p=128)), 16)
            dsem["rw"].inc(g.dma_start(
                out=rw_sb[:, 0:NF * 8].rearrange("p (f c) -> p f c", f=NF),
                in_=din["router"][:, :, :].rearrange("f p c -> p f c")), 16)
            g.wait_ge(dsem["at0"].h, 16 * 16)
            g.wait_ge(dsem["at1"].h, 16 * 16)
            dsem["ful"].inc(g.dma_start(
                out=attn_full[:, 0:HL * 1024].rearrange("p (h t) -> p h t", h=HL),
                in_=attn_dram[:, :, bass.ds(own, 1024)].rearrange(
                    "h p t -> p h t")), 16)
            for os in range(2):
                dsem["wo%d" % os].inc(g.dma_start(
                    out=wout_sb[os][:, :].rearrange("p (gi c) -> p gi c", gi=16),
                    in_=din["wout"][os].rearrange("(gi p) c -> p gi c", p=128)), 16)
                dsem["wo%d" % os].inc(g.dma_start(
                    out=wb_sb[os][:, :].rearrange("p (e c) -> p e c", e=8),
                    in_=din["wb"][os].rearrange("e p c -> p e c")), 16)
            g.wait_ge(ccsem.h, 1)
            dsem["ful"].inc(g.dma_start(
                out=attn_full[:, HL * 1024:].rearrange("p (h t) -> p h t", h=HL),
                in_=cc_out[partner].rearrange("h p t -> p h t")), 16)

        @block.scalar
        def _(sc):
            sc.wait_ge(dsem["xno"].h, 16)
            for fc in range(NF):
                if fc >= 2:
                    sc.wait_ge(p3["ss"].h, fc - 1)
                p3["sq"].inc(sc.activation(sqb[fc % 2][:, :],
                                           xn_own[:, fc * 1024:(fc + 1) * 1024],
                                           AF.Square))

        es_ps34 = contextlib.ExitStack()
        ps_ss = [psum(es_ps34, f"ps_ss{i}", [1, 512], F32) for i in range(2)]
        ps_rc = psum(es_ps34, "ps_rc", [128, 8], F32)
        ps_lg = [psum(es_ps34, f"ps_lg{i}", [128, 8], F32) for i in range(2)]
        ps_a = [psum(es_ps34, f"ps_a{i}", [128, 512], F32) for i in range(2)]
        ps_t4 = psum(es_ps34, "ps_t4", [128, 512], BF16)

        @block.tensor
        def _(t):
            for fc in range(NF):
                t.wait_ge(p3["sq"].h, fc + 1)
                for th in range(2):
                    mm = t.matmul(ps_ss[th][:, :], onesc_sb[:, :],
                                  sqb[fc % 2][:, th * 512:(th + 1) * 512],
                                  start=(fc == 0), stop=(fc == NF - 1))
                p3["ss"].inc(mm)
            t.wait_ge(p3["rs"].h, 1)
            for tt in range(8):
                mm = t.matmul(ps_rc[:, tt:tt + 1],
                              rsrow[0:1, tt * 128:(tt + 1) * 128],
                              one1_sb[:, :], start=(tt == 0), stop=(tt == 7))
            p3["rc"].inc(mm)
            t.wait_ge(dsem["rw"].h, 16)
            for tt in range(8):
                if tt >= 2:
                    t.wait_ge(p3["le"].h, tt - 1)
                for fc in range(NF):
                    mm = t.matmul(ps_lg[tt % 2][:, :],
                                  xn_own[:, fc * 1024 + tt * 128:
                                         fc * 1024 + (tt + 1) * 128],
                                  rw_sb[:, fc * 8:(fc + 1) * 8],
                                  start=(fc == 0), stop=(fc == NF - 1))
                p3["lg"].inc(mm)
            t.wait_ge(dsem["wa"].h, 16)
            for tt in range(8):
                for eg in range(2):
                    aidx = tt * 2 + eg
                    if aidx >= 2:
                        t.wait_ge(p3["ae"].h, aidx - 1)
                    for fc in range(NF):
                        mm = t.matmul(ps_a[aidx % 2][:, :],
                                      xn_own[:, fc * 1024 + tt * 128:
                                             fc * 1024 + (tt + 1) * 128],
                                      wa_sb[:, fc * 1024 + eg * 512:
                                            fc * 1024 + (eg + 1) * 512],
                                      start=(fc == 0), stop=(fc == NF - 1))
                    p3["am"].inc(mm)
            for e in range(8):
                for tg in range(2):
                    tidx = e * 2 + tg
                    need_aidx = (tg * 4 + 3) * 2 + (e // 4)
                    t.wait_ge(p3["ae"].h, need_aidx + 1)
                    if tidx >= 1:
                        t.wait_ge(p3["ate"].h, tidx)
                    for j in range(4):
                        tt = tg * 4 + j
                        tr = t.transpose(ps_t4[:, j * 128:(j + 1) * 128],
                                         as_sb[:, tt * 1024 + e * 128:
                                               tt * 1024 + (e + 1) * 128],
                                         identb[:, :])
                    p3["at"].inc(tr)

        @block.vector
        def _(v):
            v.wait_ge(p3["ss"].h, NF)
            v.tensor_copy(ssrow[:, 0:512], ps_ss[0][:, :])
            p3["se"].inc(v.tensor_copy(ssrow[:, 512:1024], ps_ss[1][:, :]))

        @block.scalar
        def _(sc):
            sc.wait_ge(p3["se"].h, 1)
            p3["se"].inc(sc.activation(nrow[:, :], ssrow[:, :], AF.Sqrt))

        @block.vector
        def _(v):
            v.wait_ge(p3["se"].h, 2)
            vbar(v, v.tensor_scalar(ssrow[:, :], nrow[:, :], 1e-12, None,
                                    ALU.max))
            p3["rs"].inc(v.reciprocal(rsrow[:, :], ssrow[:, :]))
            v.wait_ge(p3["rc"].h, 1)
            for tt in range(8):
                cp = v.tensor_copy(rscol[:, tt:tt + 1], ps_rc[:, tt:tt + 1])
            vbar(v, cp)
            for tt in range(8):
                v.wait_ge(p3["lg"].h, tt + 1)
                vbar(v, v.tensor_scalar(lcol[:, :], ps_lg[tt % 2][:, :],
                                        rscol[:, tt:tt + 1], None, ALU.mult))
                p3["le"].inc(v.nop())
                p3["tp"].inc(v.reduce_max(tstat[:, 0:1], lcol[:, :],
                                          axis=AX, negate=True))
                v.wait_ge(p3["tp"].h, 2 * (tt + 1))
                vbar(v, v.reduce_max(tstat[:, 1:2], ecol[:, :], axis=AX))
                vbar(v, v.tensor_single_scalar(mrcol[:, :], ecol[:, :],
                                               tstat[:, 1:2], ALU.is_lt))
                vbar(v, v.tensor_tensor(mrcol[:, :], mrcol[:, :], ecol[:, :],
                                        ALU.mult))
                vbar(v, v.reduce_max(tstat[:, 2:3], mrcol[:, :], axis=AX))
                vbar(v, v.tensor_single_scalar(indcol[:, :], ecol[:, :],
                                               tstat[:, 2:3], ALU.is_ge))
                vbar(v, v.tensor_tensor(indcol[:, :], indcol[:, :], ecol[:, :],
                                        ALU.mult))
                vbar(v, v.reduce_sum(tstat[:, 3:4], indcol[:, :], axis=AX))
                vbar(v, v.reciprocal(tstat[:, 4:5], tstat[:, 3:4]))
                vbar(v, v.tensor_scalar(cmb[:, tt * 8:(tt + 1) * 8],
                                        indcol[:, :], tstat[:, 4:5], None,
                                        ALU.mult))
            for tt in range(8):
                for eg in range(2):
                    aidx = tt * 2 + eg
                    v.wait_ge(p3["am"].h, aidx + 1)
                    for j in range(4):
                        e = eg * 4 + j
                        ts_ = v.tensor_scalar(
                            as_sb[:, tt * 1024 + e * 128:tt * 1024 + (e + 1) * 128],
                            ps_a[aidx % 2][:, j * 128:(j + 1) * 128],
                            cmb[:, tt * 8 + e: tt * 8 + e + 1], None, ALU.mult)
                    p3["ae"].inc(ts_)

        @block.scalar
        def _(sc):
            for tt in range(8):
                sc.wait_ge(p3["tp"].h, 2 * tt + 1)
                p3["tp"].inc(sc.activation(ecol[:, :], lcol[:, :], AF.Exp,
                                           bias=tstat[:, 0:1]))
            for e in range(8):
                for tg in range(2):
                    tidx = e * 2 + tg
                    sc.wait_ge(p3["at"].h, tidx + 1)
                    p3["ate"].inc(sc.copy(
                        at_sb[:, e * 1024 + tg * 512: e * 1024 + (tg + 1) * 512],
                        ps_t4[:, :]))

        es_ps34.close()

    es_p34.close()

    es_p5 = contextlib.ExitStack()
    x_own = sbuf(es_p5, "x_own", [128, 8 * D], F32)
    gscr = [sbuf(es_p5, f"gscr{i}", [128, 512], F32) for i in range(2)]
    sgscr = [sbuf(es_p5, f"sgscr{i}", [128, 512], F32) for i in range(2)]
    oscr = [sbuf(es_p5, f"oscr{i}", [128, 512], F32) for i in range(2)]
    p5 = {n: S("p5_" + n) for n in ["s", "y", "sil", "c"]}
    odone = {}

    with nc.Block("p5") as block:
        es_ps5 = contextlib.ExitStack()
        ps_s5 = [psum(es_ps5, f"ps_s5{i}", [128, 512], F32) for i in range(2)]
        ps_y5 = [psum(es_ps5, f"ps_y5{i}", [128, 512], F32) for i in range(2)]

        @block.gpsimd
        def _(g):
            pid = g.partition_id()
            xb3 = din["xb"][:, :].rearrange("(a r) c -> a r c", a=2)
            dsem["xo"].inc(g.dma_start(
                out=x_own[:, :].rearrange("p (tt c) -> p tt c", tt=8),
                in_=xb3[pid % 2].rearrange("(tt p) c -> p tt c", p=128)), 16)
            for os in range(2, 4):
                g.wait_ge(p5["s"].h, 8 * (os - 1))
                g.wait_ge(p5["y"].h, 8 * (os - 1))
                dsem["wo%d" % (os % 2)].inc(g.dma_start(
                    out=wout_sb[os % 2][:, :].rearrange("p (gi c) -> p gi c", gi=16),
                    in_=din["wout"][os].rearrange("(gi p) c -> p gi c", p=128)), 16)
                dsem["wo%d" % (os % 2)].inc(g.dma_start(
                    out=wb_sb[os % 2][:, :].rearrange("p (e c) -> p e c", e=8),
                    in_=din["wb"][os].rearrange("e p c -> p e c")), 16)
            g.wait_ge(dsem["xo"].h, 16)

        @block.tensor
        def _(t):
            t.wait_ge(dsem["ful"].h, 32)
            t.wait_ge(p3["ate"].h, 16)
            for os in range(4):
                t.wait_ge(dsem["wo%d" % (os % 2)].h, 32 * (os // 2 + 1))
                for tt in range(8):
                    idx = os * 8 + tt
                    if idx >= 2:
                        t.wait_ge(p5["c"].h, idx - 1)
                        t.wait_ge(p5["sil"].h, idx - 1)
                    for gi in range(16):
                        mm = t.matmul(ps_s5[idx % 2][:, :],
                                      attn_full[:, gi * 1024 + tt * 128:
                                                gi * 1024 + (tt + 1) * 128],
                                      wout_sb[os % 2][:, gi * 512:(gi + 1) * 512],
                                      start=(gi == 0), stop=(gi == 15))
                    p5["s"].inc(mm)
                    for e in range(8):
                        mm = t.matmul(ps_y5[idx % 2][:, :],
                                      at_sb[:, e * 1024 + tt * 128:
                                            e * 1024 + (tt + 1) * 128],
                                      wb_sb[os % 2][:, e * 512:(e + 1) * 512],
                                      start=(e == 0), stop=(e == 7))
                    p5["y"].inc(mm)

        @block.scalar
        def _(sc):
            for os in range(4):
                for tt in range(8):
                    idx = os * 8 + tt
                    sc.wait_ge(p5["y"].h, idx + 1)
                    if idx >= 2:
                        sc.wait_ge(p5["c"].h, idx - 1)
                    p5["sil"].inc(sc.activation(gscr[idx % 2][:, :],
                                                ps_y5[idx % 2][:, :], AF.Silu))

        @block.sync
        def _(s):
            for os in range(4):
                for tt in range(8):
                    idx = os * 8 + tt
                    s.wait_ge(p5["c"].h, idx + 1)
                    dsem["out%d" % (idx % 2)].inc(s.dma_start(
                        out=out_d[tt * 128:(tt + 1) * 128, os * 512:(os + 1) * 512],
                        in_=oscr[idx % 2][:, :]), 16)
            s.wait_ge(dsem["out0"].h, 16 * 16)
            s.wait_ge(dsem["out1"].h, 16 * 16)

        @block.vector
        def _(v):
            v.wait_ge(dsem["xo"].h, 16)
            for os in range(4):
                for tt in range(8):
                    idx = os * 8 + tt
                    v.wait_ge(p5["s"].h, idx + 1)
                    v.wait_ge(p5["sil"].h, idx + 1)
                    if idx >= 2:
                        v.wait_ge(dsem["out%d" % (idx % 2)].h,
                                  16 * ((idx - 2) // 2 + 1))
                    vbar(v, v.scalar_tensor_tensor(sgscr[idx % 2][:, :],
                                                   ps_s5[idx % 2][:, :], 1.0,
                                                   gscr[idx % 2][:, :],
                                                   ALU.mult, ALU.mult))
                    p5["c"].inc(v.tensor_tensor(
                        oscr[idx % 2][:, :], sgscr[idx % 2][:, :],
                        x_own[:, tt * D + os * 512: tt * D + (os + 1) * 512],
                        ALU.add))

        if DEBUG:
            @block.gpsimd
            def _(g):
                g.wait_ge(sd.h, sd.v)
                sg.inc(g.dma_start(out=dbg["attn"].ap(), in_=attn_dram.ap()), 16)
                sg.inc(g.dma_start(out=dbg["brows"].ap(), in_=brows_dram.ap()), 16)
                sg.inc(g.dma_start(out=dbg["xn"].ap(), in_=xn_dram.ap()), 16)
                sg.inc(g.dma_start(out=dbg["ccout"].ap(), in_=cc_out.ap()), 16)
                g.wait_ge(sg.h, sg.v)

        es_ps5.close()
    es_p5.close()
    es_p5l.close()
    return nc


def _host_inputs(x, w_qkv, w_out, router_w, Wa, Wb):
    import jax
    import jax.numpy as jnp
    x = np.asarray(x, dtype=np.float32)
    w_qkv = np.asarray(w_qkv, dtype=np.float32)
    w_out = np.asarray(w_out, dtype=np.float32)
    router_w = np.asarray(router_w, dtype=np.float32)
    Wa = np.asarray(Wa, dtype=np.float32)
    Wb = np.asarray(Wb, dtype=np.float32)

    with jax.default_device(jax.devices("cpu")[0]):
        inv = jnp.asarray(1.0) / jnp.asarray(10000.0) ** (
            jnp.arange(0, HD, 2, dtype=jnp.float32) / HD)
        tv = jnp.arange(T, dtype=jnp.float32)
        freqs = jnp.einsum("i,j->ij", tv, inv)
        emb = jnp.concatenate([freqs, freqs], axis=-1)
        cosf = np.ascontiguousarray(np.asarray(jnp.cos(emb)).T)
        sinf = np.ascontiguousarray(np.asarray(jnp.sin(emb)).T)

    R = np.zeros((128, 128), dtype=np.float32)
    half = np.arange(64)
    R[half, half + 64] = -1.0
    R[half + 64, half] = 1.0
    rmat = np.ascontiguousarray(R.T)

    ii, jj = np.indices((128, 128))
    maskS = np.where(jj <= ii, 0.0, NEG).astype(np.float32)
    maskST = np.where(ii <= jj, 1.0, 0.0).astype(np.float32)
    identf = np.eye(128, dtype=np.float32)

    wa_arr = np.ascontiguousarray(Wa.transpose(1, 0, 2).reshape(D, 1024))
    wb_arr = np.ascontiguousarray(Wb.reshape(8, 128, 4, 512).transpose(2, 0, 1, 3))
    router_arr = np.ascontiguousarray(router_w.reshape(16, 128, 8))
    woutT = w_out.T

    in_maps = []
    for c in range(8):
        b, hg = c // 2, c % 2
        hs = slice(hg * 1024, (hg + 1) * 1024)
        wq_l = w_qkv[hs, :].reshape(HL, HD, D)
        wk_l = w_qkv[2048 + hg * 1024: 2048 + (hg + 1) * 1024, :].reshape(HL, HD, D)
        wv_l = w_qkv[4096 + hg * 1024: 4096 + (hg + 1) * 1024, :].reshape(HL, HD, D)
        wqkv_arr = np.ascontiguousarray(
            np.concatenate([wq_l.transpose(0, 2, 1), wk_l.transpose(0, 2, 1),
                            wv_l.transpose(0, 2, 1)], axis=2))
        os_ = slice((1 - hg) * 1024, (2 - hg) * 1024)
        wout_local = np.concatenate([woutT[hs], woutT[os_]], axis=0)
        wout_arr = np.ascontiguousarray(
            wout_local.reshape(D, 4, 512).transpose(1, 0, 2))
        in_maps.append({
            "xb": np.ascontiguousarray(x[b]),
            "wqkv": wqkv_arr,
            "wout": wout_arr,
            "router": router_arr,
            "wa": wa_arr,
            "wb": wb_arr,
            "cosf": cosf, "sinf": sinf, "rmat": rmat, "identf": identf,
            "maskS": maskS, "maskST": maskST,
            "ones_row": np.ones((1, 128), np.float32),
            "ones_col": np.ones((128, 1), np.float32),
            "one1": np.ones((1, 1), np.float32),
        })
    return in_maps


def kernel(L0_attn_x, L0_attn_w_qkv, L0_attn_w_out, L0_ffn_router_w,
           L0_ffn_Wa, L0_ffn_Wb):
    in_maps = _host_inputs(L0_attn_x, L0_attn_w_qkv, L0_attn_w_out,
                           L0_ffn_router_w, L0_ffn_Wa, L0_ffn_Wb)
    nc = build_nc()
    res = run_bass_kernel_spmd(nc, in_maps, core_ids=list(range(8)))
    out = np.empty((4, T, D), dtype=np.float32)
    for c in range(8):
        b, hg = c // 2, c % 2
        out[b, hg * 1024:(hg + 1) * 1024] = res.results[c]["out"]
    if DEBUG:
        kernel.last_results = res
    return out


# version 15
# speedup vs baseline: 1.2560x; 1.0577x over previous
import contextlib
import numpy as np
import concourse.bass as bass
import concourse.mybir as mybir
from concourse.bass_utils import run_bass_kernel_spmd

F32 = mybir.dt.float32
BF16 = mybir.dt.bfloat16
AF = mybir.ActivationFunctionType
ALU = mybir.AluOpType
AX = mybir.AxisListType.X

T = 2048
D = 2048
HL = 8
HD = 128
NT = 16
NF = 16
NEG = -1e30
EPS = 1e-5
SCALE = float(HD) ** -0.5

DEBUG = False


class Sem:
    __slots__ = ("h", "v")

    def __init__(self, h):
        self.h = h
        self.v = 0

    def inc(self, inst, n=1):
        self.v += n
        return inst.then_inc(self.h, n)


def build_nc():
    nc = bass.Bass("TRN2", target_bir_lowering=False,
                   detect_race_conditions=False)
    es = contextlib.ExitStack()

    din = {}
    for name, shape in [
        ("xb", [T, D]), ("wqkv", [HL, D, 3 * HD]),
        ("wout", [4, D, 512]), ("router", [16, 128, 8]),
        ("wa", [D, 1024]), ("wb", [4, 8, 128, 512]),
        ("cosf", [128, T]), ("sinf", [128, T]),
        ("rmat", [128, 128]), ("identf", [128, 128]),
        ("maskS", [128, 128]), ("maskST", [128, 128]),
        ("ones_row", [1, 128]), ("ones_col", [128, 1]), ("one1", [1, 1]),
    ]:
        din[name] = nc.dram_tensor(name, shape, F32, kind="ExternalInput")
    out_d = nc.dram_tensor("out", [1024, D], F32, kind="ExternalOutput")

    xn_dram = nc.dram_tensor("xn_dram", [128, NF * T], BF16)
    attn_dram = nc.dram_tensor("attn_dram", [HL, 128, T], BF16)
    brows_dram = nc.dram_tensor("brows_dram", [128, 128], F32)
    cc_in = nc.dram_tensor("cc_in", [HL, 128, 1024], BF16)
    cc_out = nc.dram_tensor("cc_out", [2, HL, 128, 1024], BF16)
    groups = [[0, 1], [2, 3], [4, 5], [6, 7]]

    dbg = {}
    if DEBUG:
        for nm, shp, dt in [("attn", [HL, 128, T], BF16),
                            ("brows", [128, 128], F32),
                            ("xn", [128, NF * T], BF16),
                            ("q", [128, HL * T], BF16),
                            ("k", [128, HL * T], BF16),
                            ("v", [128, HL * T], BF16),
                            ("pscr", [128, T], BF16),
                            ("ssum", [128, 128], F32),
                            ("ccout", [2, HL, 128, 1024], BF16),
                            ("expb0", [128, 512], BF16),
                            ("mexpb0", [128, 512], BF16),
                            ("brow0", [1, T], F32),
                            ("abuf0", [128, 512], BF16)]:
            dbg[nm] = nc.dram_tensor("dbg_" + nm, shp, dt, kind="ExternalOutput")

    def sbuf(stack, name, shape, dt, side="left"):
        return stack.enter_context(nc.sbuf_tensor(name, shape, dt, side=side))

    def psum(stack, name, shape, dt):
        return stack.enter_context(nc.psum_tensor(name, shape, dt))

    sems = {}

    def S(name):
        s = Sem(es.enter_context(nc.semaphore(name)))
        sems[name] = s
        return s

    sd = S("sd")
    sg = S("sg")
    dsem = {n: S("dma_" + n) for n in
            ["cst", "cs2", "wq", "wk", "wv", "x0", "x1", "xnb", "brow",
             "at0", "at1", "xno", "wa", "rw", "ful", "xo",
             "wo0", "wo1", "out0", "out1", "dbg2", "brr"]}
    vsf = S("vsf")
    ssf = S("ssf")

    def vbar(v, inst):
        vsf.v += 1
        inst.then_inc(vsf.h, 1)
        v.wait_ge(vsf.h, vsf.v)

    def sbar(sc, inst):
        ssf.v += 1
        inst.then_inc(ssf.h, 1)
        sc.wait_ge(ssf.h, ssf.v)
    ld = {}

    identb = sbuf(es, "identb", [128, 128], BF16)
    onesr_sb = sbuf(es, "onesr_sb", [1, 128], F32)
    onesc_sb = sbuf(es, "onesc_sb", [128, 1], BF16)
    one1_sb = sbuf(es, "one1_sb", [1, 1], F32)

    es_p2c = contextlib.ExitStack()
    xn_fm = sbuf(es_p2c, "xn_fm", [128, NF * T], BF16)
    identf = sbuf(es_p2c, "identf_sb", [128, 128], F32)
    maskS_sb = sbuf(es_p2c, "maskS_sb", [128, 128], BF16)
    maskST_sb = sbuf(es_p2c, "maskST_sb", [128, 128], BF16)

    es_rope = contextlib.ExitStack()
    cos_sb = sbuf(es_rope, "cos_sb", [128, T], BF16)
    sin_sb = sbuf(es_rope, "sin_sb", [128, T], BF16)
    rmat_sb = sbuf(es_rope, "rmat_sb", [128, 128], BF16)

    es_p1s = contextlib.ExitStack()
    wslab = [sbuf(es_p1s, f"wslab{i}", [128, NF * 3 * HD], BF16) for i in range(2)]
    praw = [sbuf(es_p1s, f"praw{i}", [128, 512], BF16) for i in range(2)]
    t1s = sbuf(es_p1s, "t1s", [128, 512], BF16)
    t2s = sbuf(es_p1s, "t2s", [128, 512], BF16)
    p1stat = sbuf(es_p1s, "p1stat", [128, 64], F32)
    pscr1 = sbuf(es_p1s, "pscr1", [128, 1024], BF16)

    es_mid = contextlib.ExitStack()

    p0 = {n: S("p0_" + n) for n in ["lnv", "lns", "st", "rst", "xn", "tr", "ev"]}
    es_p0 = contextlib.ExitStack()
    xstg = [sbuf(es_p0, f"xstg{i}", [128, D], F32, side="right") for i in range(2)]
    sqscr = [sbuf(es_p0, f"sqscr{i}", [128, D], BF16, side="right") for i in range(2)]
    xn_tok = [sbuf(es_p0, f"xn_tok{i}", [128, D], BF16, side="right") for i in range(2)]
    stat = sbuf(es_p0, "stat", [128, 8 * NT], F32, side="right")
    c_ = lambda j, tt: stat[:, j * NT + tt: j * NT + tt + 1]
    es_ptr = contextlib.ExitStack()
    ps_tr = [psum(es_ptr, f"ps_tr{i}", [128, 512], BF16) for i in range(2)]

    with nc.Block("p0") as block:
        @block.gpsimd
        def _(g):
            dsem["cst"].inc(g.dma_start(out=identb[:, :], in_=din["identf"][:, :]), 16)
            dsem["cst"].inc(g.dma_start(out=cos_sb[:, :], in_=din["cosf"][:, :]), 16)
            dsem["cst"].inc(g.dma_start(out=sin_sb[:, :], in_=din["sinf"][:, :]), 16)
            dsem["cst"].inc(g.dma_start(out=rmat_sb[:, :], in_=din["rmat"][:, :]), 16)
            dsem["cst"].inc(g.dma_start(out=maskS_sb[:, :], in_=din["maskS"][:, :]), 16)
            dsem["cst"].inc(g.dma_start(out=maskST_sb[:, :], in_=din["maskST"][:, :]), 16)
            dsem["cst"].inc(g.dma_start(out=onesc_sb[:, :], in_=din["ones_col"][:, :]), 16)
            for h in range(2):
                dsem["wq" if h % 2 == 0 else "wk"].inc(g.dma_start(
                    out=wslab[h % 2][:, :].rearrange("p (f c) -> p f c", f=NF),
                    in_=din["wqkv"][h].rearrange("(f p) c -> p f c", p=128)), 16)

        @block.sync
        def _(s):
            dsem["cs2"].inc(s.dma_start(out=identf[:, :], in_=din["identf"][:, :]), 16)
            dsem["cs2"].inc(s.dma_start(out=onesr_sb[:, :], in_=din["ones_row"][:, :]), 16)
            dsem["cs2"].inc(s.dma_start(out=one1_sb[:, :], in_=din["one1"][:, :]), 16)
            for tt in range(NT):
                if tt >= 2:
                    s.wait_ge(p0["xn"].h, tt - 1)
                    s.wait_ge(p0["lns"].h, tt - 1)
                dsem["x%d" % (tt % 2)].inc(
                    s.dma_start(out=xstg[tt % 2][:, :],
                                in_=din["xb"][tt * 128:(tt + 1) * 128, :]), 16)

        def xwait(eng, tt):
            eng.wait_ge(dsem["x%d" % (tt % 2)].h, 16 * (tt // 2 + 1))

        @block.vector
        def _(v):
            for tt in range(NT):
                xwait(v, tt)
                vbar(v, v.reduce_sum(c_(0, tt), xstg[tt % 2][:, :], axis=AX))
                v.wait_ge(p0["lns"].h, tt + 1)
                vbar(v, v.tensor_scalar(c_(2, tt), c_(0, tt), 1.0 / D, None,
                                        ALU.mult))
                vbar(v, v.tensor_tensor(c_(3, tt), c_(2, tt), c_(2, tt), ALU.mult))
                vbar(v, v.tensor_scalar(c_(4, tt), c_(1, tt), 1.0 / D, c_(3, tt),
                                        ALU.mult, ALU.subtract))
                vbar(v, v.tensor_scalar(c_(5, tt), c_(4, tt), EPS, None, ALU.add))
                p0["st"].inc(v.reciprocal(c_(6, tt), c_(5, tt)))
                v.wait_ge(p0["rst"].h, tt + 1)
                if tt >= 2:
                    v.wait_ge(p0["tr"].h, 4 * (tt - 1))
                p0["xn"].inc(
                    v.tensor_scalar(xn_tok[tt % 2][:, :], xstg[tt % 2][:, :],
                                    c_(2, tt), c_(7, tt), ALU.subtract, ALU.mult))

        @block.tensor
        def _(t):
            t.wait_ge(dsem["cst"].h, 112)
            for tt in range(NT):
                t.wait_ge(p0["xn"].h, tt + 1)
                for g4 in range(4):
                    idx = tt * 4 + g4
                    if idx >= 2:
                        t.wait_ge(p0["ev"].h, idx - 1)
                    for j in range(4):
                        fc = g4 * 4 + j
                        tr = t.transpose(ps_tr[idx % 2][:, j * 128:(j + 1) * 128],
                                         xn_tok[tt % 2][:, fc * 128:(fc + 1) * 128],
                                         identb[:, :])
                    p0["tr"].inc(tr)

        @block.scalar
        def _(sc):
            for tt in range(NT):
                xwait(sc, tt)
                p0["lns"].inc(
                    sc.activation(sqscr[tt % 2][:, :], xstg[tt % 2][:, :],
                                  AF.Square, accum_out=c_(1, tt)))
                sc.wait_ge(p0["st"].h, tt + 1)
                p0["rst"].inc(sc.activation(c_(7, tt), c_(6, tt), AF.Sqrt))
                if tt >= 1:
                    for g4 in range(4):
                        idx = (tt - 1) * 4 + g4
                        sc.wait_ge(p0["tr"].h, idx + 1)
                        dst = xn_fm[:, :].rearrange("p (f t) -> p f t", f=NF)[
                            :, g4 * 4:(g4 + 1) * 4, (tt - 1) * 128: tt * 128]
                        p0["ev"].inc(sc.copy(
                            dst, ps_tr[idx % 2][:, :].rearrange("p (f t) -> p f t", f=4)))
            for g4 in range(4):
                idx = (NT - 1) * 4 + g4
                sc.wait_ge(p0["tr"].h, idx + 1)
                dst = xn_fm[:, :].rearrange("p (f t) -> p f t", f=NF)[
                    :, g4 * 4:(g4 + 1) * 4, (NT - 1) * 128: NT * 128]
                p0["ev"].inc(sc.copy(
                    dst, ps_tr[idx % 2][:, :].rearrange("p (f t) -> p f t", f=4)))

    es_p0.close()
    es_ptr.close()

    q_sb = sbuf(es_mid, "q_sb", [128, HL * T], BF16, side="right")
    k_sb = sbuf(es_mid, "k_sb", [128, HL * T], BF16, side="right")
    v_sb = sbuf(es_mid, "v_sb", [128, HL * T], BF16, side="right")
    ssum = sbuf(es_mid, "ssum", [128, 128], F32, side="right")
    es_pq = contextlib.ExitStack()
    ps_q = [psum(es_pq, f"ps_q{i}", [128, 512], F32) for i in range(3)]
    ps_r = psum(es_pq, "ps_r", [128, 512], F32)
    ps_s = [psum(es_pq, f"ps_s{i}", [128, 1024], F32) for i in range(2)]

    CH = [(h, m, ts) for h in range(HL) for m in range(3) for ts in range(4)]
    CIDX = {c: i for i, c in enumerate(CH)}
    RCH = [(h, m, ts) for h in range(HL) for m in range(2) for ts in range(4)]
    RIDX = {c: i for i, c in enumerate(RCH)}
    P1CH = []
    for h in range(HL):
        for qt in range(NT):
            kend = (qt + 1) * 128
            if kend <= 1024:
                P1CH.append((h, qt, 0, 0, kend, True))
            else:
                P1CH.append((h, qt, 0, 0, 1024, False))
                P1CH.append((h, qt, 1, 1024, kend - 1024, False))
    P1OF = {}
    for i, (h, qt, ci, k0, w, sg_) in enumerate(P1CH):
        P1OF[(h, qt, ci)] = i
    P1H = {h: [i for i, c in enumerate(P1CH) if c[0] == h] for h in range(HL)}
    QKV_SB = [q_sb, k_sb, v_sb]

    p1 = {n: S("p1_" + n) for n in ["m", "e", "r", "v", "sm", "x", "he"]}

    with nc.Block("p1s") as block:
        @block.gpsimd
        def _(g):
            for h in range(2, HL):
                g.wait_ge(p1["m"].h, 12 * (h - 1))
                dsem["wq" if h % 2 == 0 else "wk"].inc(g.dma_start(
                    out=wslab[h % 2][:, :].rearrange("p (f c) -> p f c", f=NF),
                    in_=din["wqkv"][h].rearrange("(f p) c -> p f c", p=128)), 16)

        @block.tensor
        def _(t):
            t.wait_ge(dsem["cst"].h, 112)

            def emit_p1(i):
                h, qt, ci, k0, w, single = P1CH[i]
                if i >= 2:
                    t.wait_ge(p1["x"].h, i - 1)
                last = single or ci == 1
                qtile = q_sb[:, h * T + qt * 128: h * T + (qt + 1) * 128]
                hi = w - 128 if last else w
                for j in range((hi + 511) // 512):
                    lo = j * 512
                    n = min(512, hi - lo)
                    mm = t.matmul(
                        ps_s[i % 2][:, lo: lo + n], qtile,
                        k_sb[:, h * T + k0 + lo: h * T + k0 + lo + n],
                        start=True, stop=True)
                if last:
                    t.matmul(ps_s[i % 2][:, w - 128:w], identb[:, :],
                             maskS_sb[:, :], start=True, stop=False)
                    mm = t.matmul(
                        ps_s[i % 2][:, w - 128: w], qtile,
                        k_sb[:, h * T + k0 + w - 128: h * T + k0 + w],
                        start=False, stop=True)
                p1["sm"].inc(mm)

            def emit_rot(ri):
                h, m, ts = RCH[ri]
                cq = CIDX[(h, m, ts)]
                t.wait_ge(p1["e"].h, cq + 1)
                if ri >= 1:
                    t.wait_ge(p1["v"].h, ri)
                p1["r"].inc(t.matmul(ps_r[:, :], rmat_sb[:, :],
                                     praw[ri % 2][:, :], start=True, stop=True))

            for cq, (h, m, ts) in enumerate(CH):
                t.wait_ge(p0["ev"].h, 16 * (ts + 1))
                t.wait_ge(dsem["wq" if h % 2 == 0 else "wk"].h,
                          16 * (h // 2 + 1))
                if cq >= 3:
                    t.wait_ge(p1["e"].h, cq - 2)
                for fc in range(NF):
                    mm = t.matmul(
                        ps_q[cq % 3][:, :],
                        wslab[h % 2][:, fc * 3 * HD + m * HD:
                                     fc * 3 * HD + (m + 1) * HD],
                        xn_fm[:, fc * T + ts * 512: fc * T + (ts + 1) * 512],
                        start=(fc == 0), stop=(fc == NF - 1))
                p1["m"].inc(mm)
                if m < 2:
                    ri = RIDX[(h, m, ts)]
                    if ri >= 1 and ri % 8 != 0:
                        emit_rot(ri - 1)
                if m == 2 and ts == 0:
                    emit_rot(RIDX[(h, 1, 3)])
                if h >= 1:
                    c12 = m * 4 + ts
                    if c12 == 0:
                        t.wait_ge(p1["v"].h, 8 * h)
                    per = len(P1H[h - 1])
                    lo = c12 * per // 12
                    hi = (c12 + 1) * per // 12
                    for i in P1H[h - 1][lo:hi]:
                        emit_p1(i)
            t.wait_ge(p1["v"].h, 8 * HL)
            for i in P1H[HL - 1]:
                emit_p1(i)

        @block.scalar
        def _(sc):
            nev = [0]

            def emit_exp(i):
                h, qt, ci, k0, w, single = P1CH[i]
                sc.wait_ge(p1["sm"].h, i + 1)
                j = qt - 8
                par = (h % 2) * 8
                if single:
                    acc = ssum[:, h * 16 + qt: h * 16 + qt + 1]
                elif ci == 0:
                    if j == 0 and h >= 2:
                        sc.wait_ge(p1["he"].h, h - 1)
                    acc = ssum[:, h * 16 + qt: h * 16 + qt + 1]
                else:
                    acc = p1stat[:, par + j: par + j + 1]
                p1["x"].inc(sc.activation(pscr1[:, 0:w], ps_s[i % 2][:, 0:w],
                                          AF.Exp, accum_out=acc))

            for cq, (h, m, ts) in enumerate(CH):
                sc.wait_ge(p1["m"].h, cq + 1)
                if m < 2:
                    ri = RIDX[(h, m, ts)]
                    if ri >= 2:
                        sc.wait_ge(p1["v"].h, ri - 1)
                    p1["e"].inc(sc.activation(praw[ri % 2][:, :],
                                              ps_q[cq % 3][:, :], AF.Copy,
                                              scale=SCALE if m == 0 else 1.0))
                else:
                    p1["e"].inc(sc.activation(
                        v_sb[:, h * T + ts * 512: h * T + (ts + 1) * 512],
                        ps_q[cq % 3][:, :], AF.Copy))
                if h >= 1:
                    c12 = m * 4 + ts
                    per = len(P1H[h - 1])
                    lo = c12 * per // 12
                    hi = (c12 + 1) * per // 12
                    for i in P1H[h - 1][lo:hi]:
                        emit_exp(i)
            for i in P1H[HL - 1]:
                emit_exp(i)

        @block.vector
        def _(v):
            v.wait_ge(dsem["cst"].h, 112)
            nrope = [0]

            def emit_rope_v(ri):
                h, m, ts = RCH[ri]
                v.wait_ge(p1["r"].h, ri + 1)
                v.tensor_tensor(t2s[:, :], ps_r[:, :],
                                sin_sb[:, ts * 512:(ts + 1) * 512], ALU.mult)
                vbar(v, v.tensor_tensor(t1s[:, :], praw[ri % 2][:, :],
                                        cos_sb[:, ts * 512:(ts + 1) * 512],
                                        ALU.mult))
                p1["v"].inc(v.tensor_tensor(
                    QKV_SB[m][:, h * T + ts * 512: h * T + (ts + 1) * 512],
                    t1s[:, :], t2s[:, :], ALU.add))

            def emit_combine(h):
                par = (h % 2) * 8
                v.wait_ge(p1["x"].h, P1OF[(h, NT - 1, 1)] + 1)
                p1["he"].inc(v.tensor_tensor(
                    ssum[:, h * 16 + 8: h * 16 + 16],
                    ssum[:, h * 16 + 8: h * 16 + 16],
                    p1stat[:, par: par + 8], ALU.add))

            for cq, (h, m, ts) in enumerate(CH):
                if m < 2:
                    ri = RIDX[(h, m, ts)]
                    emit_rope_v(ri)
                if h >= 2 and m == 2 and ts == 3:
                    emit_combine(h - 2)
            emit_combine(HL - 2)
            emit_combine(HL - 1)

    es_pq.close()
    es_p1s.close()
    es_rope.close()

    abuf = [sbuf(es_mid, f"abuf{i}", [128, 512], BF16, side="right")
            for i in range(2)]
    pscr = sbuf(es_mid, "pscr", [128, T], BF16, side="right")
    bh = sbuf(es_mid, "bh", [128, 128], F32, side="right")
    rsum = sbuf(es_mid, "rsum", [128, 128], F32, side="right")
    browb = [sbuf(es_mid, f"browb{i}", [1, T], BF16, side="right")
             for i in range(2)]
    onesr_bf = sbuf(es_mid, "onesr_bf", [1, 128], BF16, side="right")
    expb = [sbuf(es_mid, f"expb{i}", [128, 512], BF16, side="right") for i in range(2)]
    mexpb = [sbuf(es_mid, f"mexpb{i}", [128, 512], BF16, side="right") for i in range(2)]

    ccsem = S("ccsem")
    p2 = {n: S("p2_" + n) for n in ["s1", "msk", "exp", "st", "tB", "eB",
                                    "dg", "pt", "avd", "ae", "eb", "bc",
                                    "ebe", "pb", "vt", "vte"]}

    def kchunks(qt):
        kend = (qt + 1) * 128
        return [(kc * 512, min(512, kend - kc * 512))
                for kc in range((kend + 511) // 512)]

    EV2 = [(h, qs, kt) for h in range(HL) for qs in range(4)
           for kt in range(4 * qs + 4)]
    EIDX = {e: i for i, e in enumerate(EV2)}
    LASTE = {(h, qs): EIDX[(h, qs, 4 * qs + 3)] for h in range(HL)
             for qs in range(4)}
    DGC = np.cumsum([1 if kt >= 4 * qs else 0 for (h, qs, kt) in EV2]).tolist()
    brow_done = {}
    adma_done = {}

    with nc.Block("p2") as block:
        es_ps2 = contextlib.ExitStack()
        ps_vt = psum(es_ps2, "ps_vt", [128, 512], BF16)
        ps_st2 = [psum(es_ps2, f"ps_st2{i}", [128, 1024], F32) for i in range(2)]
        ps_av = [psum(es_ps2, f"ps_av{i}", [128, 512], F32) for i in range(2)]
        ps_bc = [psum(es_ps2, "ps_bc0", [128, 512], F32)] * 2

        @block.sync
        def _(s):
            s.wait_ge(p0["ev"].h, 64)
            dsem["xnb"].inc(s.dma_start(out=xn_dram[:, :], in_=xn_fm[:, :]), 16)

        @block.tensor
        def _(t):
            t.wait_ge(p2["st"].h, 1)
            t.wait_ge(dsem["cs2"].h, 48)
            p2["tB"].inc(t.transpose(ps_bc[0][:, 0:128], rsum[:, :],
                                     identf[:, :]))
            for h in range(HL):
                for g4 in range(4):
                    gi = h * 4 + g4
                    if gi >= 1:
                        t.wait_ge(p2["vte"].h, gi)
                    for j in range(4):
                        kt = g4 * 4 + j
                        tr = t.transpose(ps_vt[:, j * 128:(j + 1) * 128],
                                         v_sb[:, h * T + kt * 128:
                                              h * T + (kt + 1) * 128],
                                         identb[:, :])
                    p2["vt"].inc(tr)

        @block.scalar
        def _(sc):
            for gi in range(32):
                h, g4 = gi // 4, gi % 4
                sc.wait_ge(p2["vt"].h, gi + 1)
                p2["vte"].inc(sc.copy(
                    v_sb[:, h * T + g4 * 512: h * T + (g4 + 1) * 512],
                    ps_vt[:, :]))

        @block.vector
        def _(v):
            p2["st"].inc(v.reciprocal(rsum[:, :], ssum[:, :]))
            v.wait_ge(p2["tB"].h, 1)
            p2["eB"].inc(v.tensor_copy(bh[:, :], ps_bc[0][:, 0:128]))

        @block.sync
        def _(s):
            s.wait_ge(p2["eB"].h, 1)
            dsem["brow"].inc(s.dma_start(out=brows_dram[:, :], in_=bh[:, :]), 16)
            s.wait_ge(dsem["brow"].h, 16)

            def emit_chunk(hh, qs):
                gidx = hh * 4 + qs
                s.wait_ge(p2["ae"].h, gidx + 1)
                dsem["at%d" % (gidx % 2)].inc(s.dma_start(
                    out=attn_dram[hh, :, qs * 512:(qs + 1) * 512],
                    in_=abuf[gidx % 2][:, :]), 16)

            for h in range(1, HL):
                for qs in range(4):
                    emit_chunk(h - 1, qs)
            for qs in range(4):
                emit_chunk(HL - 1, qs)

        @block.gpsimd
        def _(g):
            dsem["cst"].inc(g.dma_start(out=onesr_bf[:, :], in_=din["ones_row"][:, :]), 16)
            g.wait_ge(dsem["brow"].h, 16)
            for h in range(HL):
                if h >= 2:
                    g.wait_ge(p2["bc"].h, 4 * (h - 1))
                if DEBUG and h == 2:
                    dsem["dbg2"].inc(g.dma_start(out=dbg["brow0"].ap(),
                                                 in_=browb[0][:, :]), 16)
                    g.wait_ge(dsem["dbg2"].h, 16)
                dsem["brr"].inc(g.dma_start(
                    out=browb[h % 2][:, :],
                    in_=brows_dram[:, :].rearrange("(h r) c -> h (r c)", h=HL)[
                        h:h + 1, :]), 16)
            if DEBUG:
                g.wait_ge(p2["pb"].h, 1)
                dsem["dbg2"].inc(g.dma_start(out=dbg["expb0"].ap(), in_=expb[0][:, :]), 16)
                dsem["dbg2"].inc(g.dma_start(out=dbg["mexpb0"].ap(), in_=mexpb[0][:, :]), 16)
                g.wait_ge(p2["ae"].h, 1)
                dsem["dbg2"].inc(g.dma_start(out=dbg["abuf0"].ap(), in_=abuf[0][:, :]), 16)
            pid = g.partition_id()
            oth = (1 - pid % 2) * 1024
            g.wait_ge(dsem["at0"].h, 16 * 16)
            g.wait_ge(dsem["at1"].h, 16 * 16)
            sg.inc(g.dma_start(out=cc_in[:, :, :],
                               in_=attn_dram[:, :, bass.ds(oth, 1024)]), 16)
            g.wait_ge(sg.h, sg.v)
            ccsem.inc(g.collective_compute(
                "AllGather", ALU.bypass, replica_groups=groups,
                ins=[cc_in.ap().opt()], outs=[cc_out.ap().opt()]))
            if DEBUG:
                sg.inc(g.dma_start(out=dbg["ssum"].ap(), in_=ssum[:, :]), 16)
                sg.inc(g.dma_start(out=dbg["q"].ap(), in_=q_sb[:, :]), 16)
                sg.inc(g.dma_start(out=dbg["k"].ap(), in_=k_sb[:, :]), 16)
                sg.inc(g.dma_start(out=dbg["v"].ap(), in_=v_sb[:, :]), 16)
                sg.inc(g.dma_start(out=dbg["pscr"].ap(), in_=pscr[:, :]), 16)
                g.wait_ge(sg.h, sg.v)

        def PB(c):
            return (c // 2 % 2) * 1024 + (c % 2) * 512

        @block.tensor
        def _(t):
            def OFF(qs_, kt_):
                return max(0, kt_ - 4 * qs_) * 128

            t.wait_ge(dsem["cst"].h, 128)
            next_av = [0]

            def emit_av():
                j = next_av[0]
                jh, jqs, jkt = EV2[j]
                jo = OFF(jqs, jkt)
                t.wait_ge(p2["vte"].h, 4 * (jh + 1))
                t.wait_ge(p2["pb"].h, j + 1)
                p2["avd"].inc(t.matmul(
                    ps_av[(jh * 4 + jqs) % 2][:, jo:512],
                    v_sb[:, jh * T + jkt * 128: jh * T + (jkt + 1) * 128],
                    pscr[:, PB(j) + jo: PB(j) + 512],
                    start=(jkt == 0), stop=(jkt == 4 * jqs + 3)))
                next_av[0] += 1

            for i, (h, qs, kt) in enumerate(EV2):
                if kt == 0:
                    while next_av[0] < i:
                        emit_av()
                    gidx = h * 4 + qs
                    if gidx >= 2:
                        t.wait_ge(p2["ae"].h, gidx - 1)
                    t.wait_ge(dsem["brr"].h, 16 * (h + 1))
                    if gidx >= 1:
                        t.wait_ge(p2["ebe"].h, gidx)
                    p2["bc"].inc(t.matmul(
                        ps_bc[gidx % 2][:, :], onesr_bf[:, :],
                        browb[h % 2][0:1, qs * 512:(qs + 1) * 512],
                        start=True, stop=True))
                if i % 2 == 0 and i >= 4:
                    t.wait_ge(p2["pt"].h, i // 2 - 1)
                off = OFF(qs, kt)
                p2["dg"].inc(t.matmul(
                    ps_st2[i // 2 % 2][:, (i % 2) * 512 + off:
                                       (i % 2) * 512 + 512],
                    k_sb[:, h * T + kt * 128: h * T + (kt + 1) * 128],
                    q_sb[:, h * T + qs * 512 + off: h * T + (qs + 1) * 512],
                    start=True, stop=True))
                while i - next_av[0] >= 3:
                    emit_av()
            while next_av[0] < len(EV2):
                emit_av()

        @block.vector
        def _(v):
            def OFF(qs_, kt_):
                return max(0, kt_ - 4 * qs_) * 128

            tri_b = bass.AP(maskST_sb[:, :].tensor, 0,
                            [[128, 128], [0, 4], [1, 128]])

            def biasmul(i):
                jh, jqs, jkt = EV2[i]
                jo = OFF(jqs, jkt)
                jg = jh * 4 + jqs
                pb_ = PB(i)
                v.wait_ge(p2["pt"].h, i // 2 + 1)
                if jkt >= 4 * jqs:
                    mtt = v.tensor_tensor(
                        pscr[:, pb_ + jo: pb_ + jo + 128],
                        pscr[:, pb_ + jo: pb_ + jo + 128],
                        mexpb[jg % 2][:, jo:jo + 128], ALU.mult)
                    if jo + 128 < 512:
                        p2["pb"].inc(v.tensor_tensor(
                            pscr[:, pb_ + jo + 128: pb_ + 512],
                            pscr[:, pb_ + jo + 128: pb_ + 512],
                            expb[jg % 2][:, jo + 128:512], ALU.mult))
                    else:
                        p2["pb"].inc(mtt)
                else:
                    p2["pb"].inc(v.tensor_tensor(
                        pscr[:, pb_ + jo: pb_ + 512],
                        pscr[:, pb_ + jo: pb_ + 512],
                        expb[jg % 2][:, jo:512], ALU.mult))

            for i, (h, qs, kt) in enumerate(EV2):
                if i >= 1:
                    biasmul(i - 1)
                if kt == 0:
                    gidx = h * 4 + qs
                    if DEBUG and gidx == 2:
                        v.wait_ge(dsem["dbg2"].h, 64)
                    v.wait_ge(p2["bc"].h, gidx + 1)
                    p2["ebe"].inc(v.tensor_copy(expb[gidx % 2][:, :],
                                                ps_bc[gidx % 2][:, :]))
                    v.wait_ge(p2["ebe"].h, gidx + 1)
                    v.tensor_tensor(
                        mexpb[gidx % 2][:, :].rearrange("p (a b) -> p a b", a=4),
                        expb[gidx % 2][:, :].rearrange("p (a b) -> p a b", a=4),
                        tri_b, ALU.mult)
            biasmul(len(EV2) - 1)

        @block.scalar
        def _(sc):
            for i, (h, qs, kt) in enumerate(EV2):
                gidx = h * 4 + qs
                if i % 2 == 1:
                    p = i // 2
                    sc.wait_ge(p2["dg"].h, i + 1)
                    if p >= 2:
                        sc.wait_ge(p2["avd"].h, 2 * p - 2)
                    offA = max(0, (kt - 1) - 4 * qs) * 128
                    offB = max(0, kt - 4 * qs) * 128
                    if offB == 0:
                        p2["pt"].inc(sc.activation(
                            pscr[:, PB(i - 1): PB(i - 1) + 1024],
                            ps_st2[p % 2][:, 0:1024], AF.Exp))
                    else:
                        sc.activation(
                            pscr[:, PB(i - 1) + offA: PB(i - 1) + 512],
                            ps_st2[p % 2][:, offA:512], AF.Exp)
                        p2["pt"].inc(sc.activation(
                            pscr[:, PB(i) + offB: PB(i) + 512],
                            ps_st2[p % 2][:, 512 + offB:1024], AF.Exp))
                if kt == 4 * qs + 3:
                    sc.wait_ge(p2["avd"].h, i + 1)
                    if gidx >= 2:
                        sc.wait_ge(dsem["at%d" % (gidx % 2)].h,
                                   16 * ((gidx - 2) // 2 + 1))
                    p2["ae"].inc(sc.copy(abuf[gidx % 2][:, :],
                                         ps_av[gidx % 2][:, :]))

        es_ps2.close()

    es_mid.close()
    es_p2c.close()


    es_p5l = contextlib.ExitStack()
    attn_full = sbuf(es_p5l, "attn_full", [128, 16 * 1024], BF16)
    at_sb = sbuf(es_p5l, "at_sb", [128, 8 * 1024], BF16)
    wout_sb = [sbuf(es_p5l, f"wout_sb{i}", [128, 16 * 512], BF16) for i in range(2)]
    wb_sb = [sbuf(es_p5l, f"wb_sb{i}", [128, 8 * 512], BF16) for i in range(2)]
    wdone = {}
    es_p34 = contextlib.ExitStack()
    xn_own = sbuf(es_p34, "xn_own", [128, NF * 1024], BF16)
    wa_sb = sbuf(es_p34, "wa_sb", [128, NF * 1024], BF16)
    rw_sb = sbuf(es_p34, "rw_sb", [128, 128], BF16)
    as_sb = sbuf(es_p34, "as_sb", [128, 8 * 1024], BF16)
    sqb = [sbuf(es_p34, f"sqb{i}", [128, 1024], BF16) for i in range(2)]
    ssrow = sbuf(es_p34, "ssrow", [1, 1024], F32)
    nrow = sbuf(es_p34, "nrow", [1, 1024], F32)
    rsrow = sbuf(es_p34, "rsrow", [1, 1024], F32)
    rscol = sbuf(es_p34, "rscol", [128, 8], F32)
    lcol = sbuf(es_p34, "lcol", [128, 8], F32)
    ecol = sbuf(es_p34, "ecol", [128, 8], F32)
    mrcol = sbuf(es_p34, "mrcol", [128, 8], F32)
    indcol = sbuf(es_p34, "indcol", [128, 8], F32)
    cmb = sbuf(es_p34, "cmb", [128, 64], F32)
    tstat = sbuf(es_p34, "tstat", [128, 8], F32)

    p3 = {n: S("p3_" + n) for n in ["sq", "ss", "se", "rs", "rc", "lg", "le",
                                    "tp", "am", "ae", "at", "ate"]}

    with nc.Block("p34") as block:
        @block.gpsimd
        def _(g):
            pid = g.partition_id()
            own = (pid % 2) * 1024
            partner = 1 - (pid % 2)
            g.wait_ge(dsem["xnb"].h, 16)
            dsem["xno"].inc(g.dma_start(
                out=xn_own[:, :].rearrange("p (f t) -> p f t", f=NF),
                in_=xn_dram[:, :].rearrange("p (f t) -> p f t", f=NF)[
                    :, :, bass.ds(own, 1024)]), 16)
            dsem["wa"].inc(g.dma_start(
                out=wa_sb[:, :].rearrange("p (f c) -> p f c", f=NF),
                in_=din["wa"][:, :].rearrange("(f p) c -> p f c", p=128)), 16)
            dsem["rw"].inc(g.dma_start(
                out=rw_sb[:, 0:NF * 8].rearrange("p (f c) -> p f c", f=NF),
                in_=din["router"][:, :, :].rearrange("f p c -> p f c")), 16)
            g.wait_ge(dsem["at0"].h, 16 * 16)
            g.wait_ge(dsem["at1"].h, 16 * 16)
            dsem["ful"].inc(g.dma_start(
                out=attn_full[:, 0:HL * 1024].rearrange("p (h t) -> p h t", h=HL),
                in_=attn_dram[:, :, bass.ds(own, 1024)].rearrange(
                    "h p t -> p h t")), 16)
            for os in range(2):
                dsem["wo%d" % os].inc(g.dma_start(
                    out=wout_sb[os][:, :].rearrange("p (gi c) -> p gi c", gi=16),
                    in_=din["wout"][os].rearrange("(gi p) c -> p gi c", p=128)), 16)
                dsem["wo%d" % os].inc(g.dma_start(
                    out=wb_sb[os][:, :].rearrange("p (e c) -> p e c", e=8),
                    in_=din["wb"][os].rearrange("e p c -> p e c")), 16)
            g.wait_ge(ccsem.h, 1)
            dsem["ful"].inc(g.dma_start(
                out=attn_full[:, HL * 1024:].rearrange("p (h t) -> p h t", h=HL),
                in_=cc_out[partner].rearrange("h p t -> p h t")), 16)

        @block.scalar
        def _(sc):
            sc.wait_ge(dsem["xno"].h, 16)
            for fc in range(NF):
                if fc >= 2:
                    sc.wait_ge(p3["ss"].h, fc - 1)
                p3["sq"].inc(sc.activation(sqb[fc % 2][:, :],
                                           xn_own[:, fc * 1024:(fc + 1) * 1024],
                                           AF.Square))

        es_ps34 = contextlib.ExitStack()
        ps_ss = [psum(es_ps34, f"ps_ss{i}", [1, 512], F32) for i in range(2)]
        ps_rc = psum(es_ps34, "ps_rc", [128, 8], F32)
        ps_lg = [psum(es_ps34, f"ps_lg{i}", [128, 8], F32) for i in range(2)]
        ps_a = [psum(es_ps34, f"ps_a{i}", [128, 512], F32) for i in range(2)]
        ps_t4 = psum(es_ps34, "ps_t4", [128, 512], BF16)

        @block.tensor
        def _(t):
            for fc in range(NF):
                t.wait_ge(p3["sq"].h, fc + 1)
                for th in range(2):
                    mm = t.matmul(ps_ss[th][:, :], onesc_sb[:, :],
                                  sqb[fc % 2][:, th * 512:(th + 1) * 512],
                                  start=(fc == 0), stop=(fc == NF - 1))
                p3["ss"].inc(mm)
            t.wait_ge(p3["rs"].h, 1)
            for tt in range(8):
                mm = t.matmul(ps_rc[:, tt:tt + 1],
                              rsrow[0:1, tt * 128:(tt + 1) * 128],
                              one1_sb[:, :], start=(tt == 0), stop=(tt == 7))
            p3["rc"].inc(mm)
            t.wait_ge(dsem["rw"].h, 16)
            for tt in range(8):
                if tt >= 2:
                    t.wait_ge(p3["le"].h, tt - 1)
                for fc in range(NF):
                    mm = t.matmul(ps_lg[tt % 2][:, :],
                                  xn_own[:, fc * 1024 + tt * 128:
                                         fc * 1024 + (tt + 1) * 128],
                                  rw_sb[:, fc * 8:(fc + 1) * 8],
                                  start=(fc == 0), stop=(fc == NF - 1))
                p3["lg"].inc(mm)
            t.wait_ge(dsem["wa"].h, 16)
            for tt in range(8):
                for eg in range(2):
                    aidx = tt * 2 + eg
                    if aidx >= 2:
                        t.wait_ge(p3["ae"].h, aidx - 1)
                    for fc in range(NF):
                        mm = t.matmul(ps_a[aidx % 2][:, :],
                                      xn_own[:, fc * 1024 + tt * 128:
                                             fc * 1024 + (tt + 1) * 128],
                                      wa_sb[:, fc * 1024 + eg * 512:
                                            fc * 1024 + (eg + 1) * 512],
                                      start=(fc == 0), stop=(fc == NF - 1))
                    p3["am"].inc(mm)
            for e in range(8):
                for tg in range(2):
                    tidx = e * 2 + tg
                    need_aidx = (tg * 4 + 3) * 2 + (e // 4)
                    t.wait_ge(p3["ae"].h, need_aidx + 1)
                    if tidx >= 1:
                        t.wait_ge(p3["ate"].h, tidx)
                    for j in range(4):
                        tt = tg * 4 + j
                        tr = t.transpose(ps_t4[:, j * 128:(j + 1) * 128],
                                         as_sb[:, tt * 1024 + e * 128:
                                               tt * 1024 + (e + 1) * 128],
                                         identb[:, :])
                    p3["at"].inc(tr)

        @block.vector
        def _(v):
            v.wait_ge(p3["ss"].h, NF)
            v.tensor_copy(ssrow[:, 0:512], ps_ss[0][:, :])
            p3["se"].inc(v.tensor_copy(ssrow[:, 512:1024], ps_ss[1][:, :]))

        @block.scalar
        def _(sc):
            sc.wait_ge(p3["se"].h, 1)
            p3["se"].inc(sc.activation(nrow[:, :], ssrow[:, :], AF.Sqrt))

        @block.vector
        def _(v):
            v.wait_ge(p3["se"].h, 2)
            vbar(v, v.tensor_scalar(ssrow[:, :], nrow[:, :], 1e-12, None,
                                    ALU.max))
            p3["rs"].inc(v.reciprocal(rsrow[:, :], ssrow[:, :]))
            v.wait_ge(p3["rc"].h, 1)
            for tt in range(8):
                cp = v.tensor_copy(rscol[:, tt:tt + 1], ps_rc[:, tt:tt + 1])
            vbar(v, cp)
            for tt in range(8):
                v.wait_ge(p3["lg"].h, tt + 1)
                vbar(v, v.tensor_scalar(lcol[:, :], ps_lg[tt % 2][:, :],
                                        rscol[:, tt:tt + 1], None, ALU.mult))
                p3["le"].inc(v.nop())
                p3["tp"].inc(v.reduce_max(tstat[:, 0:1], lcol[:, :],
                                          axis=AX, negate=True))
                v.wait_ge(p3["tp"].h, 2 * (tt + 1))
                vbar(v, v.reduce_max(tstat[:, 1:2], ecol[:, :], axis=AX))
                vbar(v, v.tensor_single_scalar(mrcol[:, :], ecol[:, :],
                                               tstat[:, 1:2], ALU.is_lt))
                vbar(v, v.tensor_tensor(mrcol[:, :], mrcol[:, :], ecol[:, :],
                                        ALU.mult))
                vbar(v, v.reduce_max(tstat[:, 2:3], mrcol[:, :], axis=AX))
                vbar(v, v.tensor_single_scalar(indcol[:, :], ecol[:, :],
                                               tstat[:, 2:3], ALU.is_ge))
                vbar(v, v.tensor_tensor(indcol[:, :], indcol[:, :], ecol[:, :],
                                        ALU.mult))
                vbar(v, v.reduce_sum(tstat[:, 3:4], indcol[:, :], axis=AX))
                vbar(v, v.reciprocal(tstat[:, 4:5], tstat[:, 3:4]))
                vbar(v, v.tensor_scalar(cmb[:, tt * 8:(tt + 1) * 8],
                                        indcol[:, :], tstat[:, 4:5], None,
                                        ALU.mult))
            for tt in range(8):
                for eg in range(2):
                    aidx = tt * 2 + eg
                    v.wait_ge(p3["am"].h, aidx + 1)
                    for j in range(4):
                        e = eg * 4 + j
                        ts_ = v.tensor_scalar(
                            as_sb[:, tt * 1024 + e * 128:tt * 1024 + (e + 1) * 128],
                            ps_a[aidx % 2][:, j * 128:(j + 1) * 128],
                            cmb[:, tt * 8 + e: tt * 8 + e + 1], None, ALU.mult)
                    p3["ae"].inc(ts_)

        @block.scalar
        def _(sc):
            for tt in range(8):
                sc.wait_ge(p3["tp"].h, 2 * tt + 1)
                p3["tp"].inc(sc.activation(ecol[:, :], lcol[:, :], AF.Exp,
                                           bias=tstat[:, 0:1]))
            for e in range(8):
                for tg in range(2):
                    tidx = e * 2 + tg
                    sc.wait_ge(p3["at"].h, tidx + 1)
                    p3["ate"].inc(sc.copy(
                        at_sb[:, e * 1024 + tg * 512: e * 1024 + (tg + 1) * 512],
                        ps_t4[:, :]))

        es_ps34.close()

    es_p34.close()

    es_p5 = contextlib.ExitStack()
    x_own = sbuf(es_p5, "x_own", [128, 8 * D], F32)
    gscr = [sbuf(es_p5, f"gscr{i}", [128, 512], F32) for i in range(2)]
    sgscr = [sbuf(es_p5, f"sgscr{i}", [128, 512], F32) for i in range(2)]
    oscr = [sbuf(es_p5, f"oscr{i}", [128, 512], F32) for i in range(2)]
    p5 = {n: S("p5_" + n) for n in ["s", "y", "sil", "c"]}
    odone = {}

    with nc.Block("p5") as block:
        es_ps5 = contextlib.ExitStack()
        ps_s5 = [psum(es_ps5, f"ps_s5{i}", [128, 512], F32) for i in range(2)]
        ps_y5 = [psum(es_ps5, f"ps_y5{i}", [128, 512], F32) for i in range(2)]

        @block.gpsimd
        def _(g):
            pid = g.partition_id()
            xb3 = din["xb"][:, :].rearrange("(a r) c -> a r c", a=2)
            dsem["xo"].inc(g.dma_start(
                out=x_own[:, :].rearrange("p (tt c) -> p tt c", tt=8),
                in_=xb3[pid % 2].rearrange("(tt p) c -> p tt c", p=128)), 16)
            for os in range(2, 4):
                g.wait_ge(p5["s"].h, 8 * (os - 1))
                g.wait_ge(p5["y"].h, 8 * (os - 1))
                dsem["wo%d" % (os % 2)].inc(g.dma_start(
                    out=wout_sb[os % 2][:, :].rearrange("p (gi c) -> p gi c", gi=16),
                    in_=din["wout"][os].rearrange("(gi p) c -> p gi c", p=128)), 16)
                dsem["wo%d" % (os % 2)].inc(g.dma_start(
                    out=wb_sb[os % 2][:, :].rearrange("p (e c) -> p e c", e=8),
                    in_=din["wb"][os].rearrange("e p c -> p e c")), 16)
            g.wait_ge(dsem["xo"].h, 16)

        @block.tensor
        def _(t):
            t.wait_ge(dsem["ful"].h, 32)
            t.wait_ge(p3["ate"].h, 16)
            for os in range(4):
                t.wait_ge(dsem["wo%d" % (os % 2)].h, 32 * (os // 2 + 1))
                for tt in range(8):
                    idx = os * 8 + tt
                    if idx >= 2:
                        t.wait_ge(p5["c"].h, idx - 1)
                        t.wait_ge(p5["sil"].h, idx - 1)
                    for gi in range(16):
                        mm = t.matmul(ps_s5[idx % 2][:, :],
                                      attn_full[:, gi * 1024 + tt * 128:
                                                gi * 1024 + (tt + 1) * 128],
                                      wout_sb[os % 2][:, gi * 512:(gi + 1) * 512],
                                      start=(gi == 0), stop=(gi == 15))
                    p5["s"].inc(mm)
                    for e in range(8):
                        mm = t.matmul(ps_y5[idx % 2][:, :],
                                      at_sb[:, e * 1024 + tt * 128:
                                            e * 1024 + (tt + 1) * 128],
                                      wb_sb[os % 2][:, e * 512:(e + 1) * 512],
                                      start=(e == 0), stop=(e == 7))
                    p5["y"].inc(mm)

        @block.scalar
        def _(sc):
            for os in range(4):
                for tt in range(8):
                    idx = os * 8 + tt
                    sc.wait_ge(p5["y"].h, idx + 1)
                    if idx >= 2:
                        sc.wait_ge(p5["c"].h, idx - 1)
                    p5["sil"].inc(sc.activation(gscr[idx % 2][:, :],
                                                ps_y5[idx % 2][:, :], AF.Silu))

        @block.sync
        def _(s):
            for os in range(4):
                for tt in range(8):
                    idx = os * 8 + tt
                    s.wait_ge(p5["c"].h, idx + 1)
                    dsem["out%d" % (idx % 2)].inc(s.dma_start(
                        out=out_d[tt * 128:(tt + 1) * 128, os * 512:(os + 1) * 512],
                        in_=oscr[idx % 2][:, :]), 16)
            s.wait_ge(dsem["out0"].h, 16 * 16)
            s.wait_ge(dsem["out1"].h, 16 * 16)

        @block.vector
        def _(v):
            v.wait_ge(dsem["xo"].h, 16)
            for os in range(4):
                for tt in range(8):
                    idx = os * 8 + tt
                    v.wait_ge(p5["s"].h, idx + 1)
                    v.wait_ge(p5["sil"].h, idx + 1)
                    if idx >= 2:
                        v.wait_ge(dsem["out%d" % (idx % 2)].h,
                                  16 * ((idx - 2) // 2 + 1))
                    vbar(v, v.scalar_tensor_tensor(sgscr[idx % 2][:, :],
                                                   ps_s5[idx % 2][:, :], 1.0,
                                                   gscr[idx % 2][:, :],
                                                   ALU.mult, ALU.mult))
                    p5["c"].inc(v.tensor_tensor(
                        oscr[idx % 2][:, :], sgscr[idx % 2][:, :],
                        x_own[:, tt * D + os * 512: tt * D + (os + 1) * 512],
                        ALU.add))

        if DEBUG:
            @block.gpsimd
            def _(g):
                g.wait_ge(sd.h, sd.v)
                sg.inc(g.dma_start(out=dbg["attn"].ap(), in_=attn_dram.ap()), 16)
                sg.inc(g.dma_start(out=dbg["brows"].ap(), in_=brows_dram.ap()), 16)
                sg.inc(g.dma_start(out=dbg["xn"].ap(), in_=xn_dram.ap()), 16)
                sg.inc(g.dma_start(out=dbg["ccout"].ap(), in_=cc_out.ap()), 16)
                g.wait_ge(sg.h, sg.v)

        es_ps5.close()
    es_p5.close()
    es_p5l.close()
    return nc


def _host_inputs(x, w_qkv, w_out, router_w, Wa, Wb):
    import jax
    import jax.numpy as jnp
    x = np.asarray(x, dtype=np.float32)
    w_qkv = np.asarray(w_qkv, dtype=np.float32)
    w_out = np.asarray(w_out, dtype=np.float32)
    router_w = np.asarray(router_w, dtype=np.float32)
    Wa = np.asarray(Wa, dtype=np.float32)
    Wb = np.asarray(Wb, dtype=np.float32)

    with jax.default_device(jax.devices("cpu")[0]):
        inv = jnp.asarray(1.0) / jnp.asarray(10000.0) ** (
            jnp.arange(0, HD, 2, dtype=jnp.float32) / HD)
        tv = jnp.arange(T, dtype=jnp.float32)
        freqs = jnp.einsum("i,j->ij", tv, inv)
        emb = jnp.concatenate([freqs, freqs], axis=-1)
        cosf = np.ascontiguousarray(np.asarray(jnp.cos(emb)).T)
        sinf = np.ascontiguousarray(np.asarray(jnp.sin(emb)).T)

    R = np.zeros((128, 128), dtype=np.float32)
    half = np.arange(64)
    R[half, half + 64] = -1.0
    R[half + 64, half] = 1.0
    rmat = np.ascontiguousarray(R.T)

    ii, jj = np.indices((128, 128))
    maskS = np.where(jj <= ii, 0.0, NEG).astype(np.float32)
    maskST = np.where(ii <= jj, 1.0, 0.0).astype(np.float32)
    identf = np.eye(128, dtype=np.float32)

    wa_arr = np.ascontiguousarray(Wa.transpose(1, 0, 2).reshape(D, 1024))
    wb_arr = np.ascontiguousarray(Wb.reshape(8, 128, 4, 512).transpose(2, 0, 1, 3))
    router_arr = np.ascontiguousarray(router_w.reshape(16, 128, 8))
    woutT = w_out.T

    in_maps = []
    for c in range(8):
        b, hg = c // 2, c % 2
        hs = slice(hg * 1024, (hg + 1) * 1024)
        wq_l = w_qkv[hs, :].reshape(HL, HD, D)
        wk_l = w_qkv[2048 + hg * 1024: 2048 + (hg + 1) * 1024, :].reshape(HL, HD, D)
        wv_l = w_qkv[4096 + hg * 1024: 4096 + (hg + 1) * 1024, :].reshape(HL, HD, D)
        wqkv_arr = np.ascontiguousarray(
            np.concatenate([wq_l.transpose(0, 2, 1), wk_l.transpose(0, 2, 1),
                            wv_l.transpose(0, 2, 1)], axis=2))
        os_ = slice((1 - hg) * 1024, (2 - hg) * 1024)
        wout_local = np.concatenate([woutT[hs], woutT[os_]], axis=0)
        wout_arr = np.ascontiguousarray(
            wout_local.reshape(D, 4, 512).transpose(1, 0, 2))
        in_maps.append({
            "xb": np.ascontiguousarray(x[b]),
            "wqkv": wqkv_arr,
            "wout": wout_arr,
            "router": router_arr,
            "wa": wa_arr,
            "wb": wb_arr,
            "cosf": cosf, "sinf": sinf, "rmat": rmat, "identf": identf,
            "maskS": maskS, "maskST": maskST,
            "ones_row": np.ones((1, 128), np.float32),
            "ones_col": np.ones((128, 1), np.float32),
            "one1": np.ones((1, 1), np.float32),
        })
    return in_maps


def kernel(L0_attn_x, L0_attn_w_qkv, L0_attn_w_out, L0_ffn_router_w,
           L0_ffn_Wa, L0_ffn_Wb):
    in_maps = _host_inputs(L0_attn_x, L0_attn_w_qkv, L0_attn_w_out,
                           L0_ffn_router_w, L0_ffn_Wa, L0_ffn_Wb)
    nc = build_nc()
    res = run_bass_kernel_spmd(nc, in_maps, core_ids=list(range(8)))
    out = np.empty((4, T, D), dtype=np.float32)
    for c in range(8):
        b, hg = c // 2, c % 2
        out[b, hg * 1024:(hg + 1) * 1024] = res.results[c]["out"]
    if DEBUG:
        kernel.last_results = res
    return out


# version 16 (final)
# speedup vs baseline: 1.3453x; 1.0711x over previous
import contextlib
import numpy as np
import concourse.bass as bass
import concourse.mybir as mybir
from concourse.bass_utils import run_bass_kernel_spmd

F32 = mybir.dt.float32
BF16 = mybir.dt.bfloat16
AF = mybir.ActivationFunctionType
ALU = mybir.AluOpType
AX = mybir.AxisListType.X

T = 2048
D = 2048
HL = 8
HD = 128
NT = 16
NF = 16
NEG = -1e30
EPS = 1e-5
SCALE = float(HD) ** -0.5

DEBUG = False


class Sem:
    __slots__ = ("h", "v")

    def __init__(self, h):
        self.h = h
        self.v = 0

    def inc(self, inst, n=1):
        self.v += n
        return inst.then_inc(self.h, n)


def build_nc():
    nc = bass.Bass("TRN2", target_bir_lowering=False,
                   detect_race_conditions=False)
    es = contextlib.ExitStack()

    din = {}
    for name, shape in [
        ("xb", [T, D]), ("wqkv", [HL, D, 3 * HD]),
        ("wout", [4, D, 512]), ("router", [16, 128, 8]),
        ("wa", [D, 1024]), ("wb", [4, 8, 128, 512]),
        ("cosf", [128, T]), ("sinf", [128, T]),
        ("rmat", [128, 128]), ("identf", [128, 128]),
        ("maskS", [128, 128]), ("maskST", [128, 128]),
        ("ones_row", [1, 128]), ("ones_col", [128, 1]), ("one1", [1, 1]),
    ]:
        din[name] = nc.dram_tensor(name, shape, F32, kind="ExternalInput")
    out_d = nc.dram_tensor("out", [1024, D], F32, kind="ExternalOutput")

    xn_dram = nc.dram_tensor("xn_dram", [128, NF * T], BF16)
    attn_dram = nc.dram_tensor("attn_dram", [HL, 128, T], BF16)
    brows_dram = nc.dram_tensor("brows_dram", [128, 128], F32)
    cc_in = nc.dram_tensor("cc_in", [HL, 128, 1024], BF16)
    cc_out = nc.dram_tensor("cc_out", [2, HL, 128, 1024], BF16)
    groups = [[0, 1], [2, 3], [4, 5], [6, 7]]

    dbg = {}
    if DEBUG:
        for nm, shp, dt in [("attn", [HL, 128, T], BF16),
                            ("brows", [128, 128], F32),
                            ("xn", [128, NF * T], BF16),
                            ("q", [128, HL * T], BF16),
                            ("k", [128, HL * T], BF16),
                            ("v", [128, HL * T], BF16),
                            ("pscr", [128, T], BF16),
                            ("ssum", [128, 128], F32),
                            ("ccout", [2, HL, 128, 1024], BF16),
                            ("expb0", [128, 512], BF16),
                            ("mexpb0", [128, 512], BF16),
                            ("brow0", [1, T], F32),
                            ("abuf0", [128, 512], BF16)]:
            dbg[nm] = nc.dram_tensor("dbg_" + nm, shp, dt, kind="ExternalOutput")

    def sbuf(stack, name, shape, dt, side="left"):
        return stack.enter_context(nc.sbuf_tensor(name, shape, dt, side=side))

    def psum(stack, name, shape, dt):
        return stack.enter_context(nc.psum_tensor(name, shape, dt))

    sems = {}

    def S(name):
        s = Sem(es.enter_context(nc.semaphore(name)))
        sems[name] = s
        return s

    sd = S("sd")
    sg = S("sg")
    dsem = {n: S("dma_" + n) for n in
            ["cst", "cs2", "wq", "wk", "wv", "x0", "x1", "xnb", "brow",
             "at0", "at1", "xno", "wa", "rw", "ful", "xo",
             "wo0", "wo1", "out0", "out1", "dbg2", "brr"]}
    vsf = S("vsf")
    ssf = S("ssf")

    def vbar(v, inst):
        vsf.v += 1
        inst.then_inc(vsf.h, 1)
        v.wait_ge(vsf.h, vsf.v)

    def sbar(sc, inst):
        ssf.v += 1
        inst.then_inc(ssf.h, 1)
        sc.wait_ge(ssf.h, ssf.v)
    ld = {}

    identb = sbuf(es, "identb", [128, 128], BF16)
    onesr_sb = sbuf(es, "onesr_sb", [1, 128], F32)
    onesc_sb = sbuf(es, "onesc_sb", [128, 1], BF16)
    one1_sb = sbuf(es, "one1_sb", [1, 1], F32)

    es_p2c = contextlib.ExitStack()
    xn_fm = sbuf(es_p2c, "xn_fm", [128, NF * T], BF16)
    identf = sbuf(es_p2c, "identf_sb", [128, 128], F32)
    maskS_sb = sbuf(es_p2c, "maskS_sb", [128, 128], BF16)
    maskST_sb = sbuf(es_p2c, "maskST_sb", [128, 128], BF16)

    es_rope = contextlib.ExitStack()
    cos_sb = sbuf(es_rope, "cos_sb", [128, T], BF16)
    sin_sb = sbuf(es_rope, "sin_sb", [128, T], BF16)
    rmat_sb = sbuf(es_rope, "rmat_sb", [128, 128], BF16)

    es_p1s = contextlib.ExitStack()
    wslab = [sbuf(es_p1s, f"wslab{i}", [128, NF * 3 * HD], BF16) for i in range(2)]
    praw = [sbuf(es_p1s, f"praw{i}", [128, 512], BF16) for i in range(2)]
    t1s = sbuf(es_p1s, "t1s", [128, 512], BF16)
    t2s = sbuf(es_p1s, "t2s", [128, 512], BF16)
    p1stat = sbuf(es_p1s, "p1stat", [128, 64], F32)
    pscr1 = sbuf(es_p1s, "pscr1", [128, 1024], BF16)

    es_mid = contextlib.ExitStack()

    p0 = {n: S("p0_" + n) for n in ["lnv", "lns", "st", "rst", "xn", "tr", "ev"]}
    es_p0 = contextlib.ExitStack()
    xstg = [sbuf(es_p0, f"xstg{i}", [128, D], F32, side="right") for i in range(2)]
    sqscr = [sbuf(es_p0, f"sqscr{i}", [128, D], BF16, side="right") for i in range(2)]
    xn_tok = [sbuf(es_p0, f"xn_tok{i}", [128, D], BF16, side="right") for i in range(2)]
    stat = sbuf(es_p0, "stat", [128, 8 * NT], F32, side="right")
    c_ = lambda j, tt: stat[:, j * NT + tt: j * NT + tt + 1]
    es_ptr = contextlib.ExitStack()
    ps_tr = [psum(es_ptr, f"ps_tr{i}", [128, 512], BF16) for i in range(2)]

    with nc.Block("p0") as block:
        @block.gpsimd
        def _(g):
            dsem["cst"].inc(g.dma_start(out=identb[:, :], in_=din["identf"][:, :]), 16)
            dsem["cst"].inc(g.dma_start(out=cos_sb[:, :], in_=din["cosf"][:, :]), 16)
            dsem["cst"].inc(g.dma_start(out=sin_sb[:, :], in_=din["sinf"][:, :]), 16)
            dsem["cst"].inc(g.dma_start(out=rmat_sb[:, :], in_=din["rmat"][:, :]), 16)
            dsem["cst"].inc(g.dma_start(out=maskS_sb[:, :], in_=din["maskS"][:, :]), 16)
            dsem["cst"].inc(g.dma_start(out=maskST_sb[:, :], in_=din["maskST"][:, :]), 16)
            dsem["cst"].inc(g.dma_start(out=onesc_sb[:, :], in_=din["ones_col"][:, :]), 16)
            for h in range(2):
                dsem["wq" if h % 2 == 0 else "wk"].inc(g.dma_start(
                    out=wslab[h % 2][:, :].rearrange("p (f c) -> p f c", f=NF),
                    in_=din["wqkv"][h].rearrange("(f p) c -> p f c", p=128)), 16)

        @block.sync
        def _(s):
            dsem["cs2"].inc(s.dma_start(out=identf[:, :], in_=din["identf"][:, :]), 16)
            dsem["cs2"].inc(s.dma_start(out=onesr_sb[:, :], in_=din["ones_row"][:, :]), 16)
            dsem["cs2"].inc(s.dma_start(out=one1_sb[:, :], in_=din["one1"][:, :]), 16)
            for tt in range(NT):
                if tt >= 2:
                    s.wait_ge(p0["xn"].h, tt - 1)
                    s.wait_ge(p0["lns"].h, tt - 1)
                dsem["x%d" % (tt % 2)].inc(
                    s.dma_start(out=xstg[tt % 2][:, :],
                                in_=din["xb"][tt * 128:(tt + 1) * 128, :]), 16)

        def xwait(eng, tt):
            eng.wait_ge(dsem["x%d" % (tt % 2)].h, 16 * (tt // 2 + 1))

        @block.vector
        def _(v):
            for tt in range(NT):
                xwait(v, tt)
                vbar(v, v.reduce_sum(c_(0, tt), xstg[tt % 2][:, :], axis=AX))
                v.wait_ge(p0["lns"].h, tt + 1)
                vbar(v, v.tensor_scalar(c_(2, tt), c_(0, tt), 1.0 / D, None,
                                        ALU.mult))
                vbar(v, v.tensor_tensor(c_(3, tt), c_(2, tt), c_(2, tt), ALU.mult))
                vbar(v, v.tensor_scalar(c_(4, tt), c_(1, tt), 1.0 / D, c_(3, tt),
                                        ALU.mult, ALU.subtract))
                vbar(v, v.tensor_scalar(c_(5, tt), c_(4, tt), EPS, None, ALU.add))
                p0["st"].inc(v.reciprocal(c_(6, tt), c_(5, tt)))
                v.wait_ge(p0["rst"].h, tt + 1)
                if tt >= 2:
                    v.wait_ge(p0["tr"].h, 4 * (tt - 1))
                p0["xn"].inc(
                    v.tensor_scalar(xn_tok[tt % 2][:, :], xstg[tt % 2][:, :],
                                    c_(2, tt), c_(7, tt), ALU.subtract, ALU.mult))

        @block.tensor
        def _(t):
            t.wait_ge(dsem["cst"].h, 112)
            for tt in range(NT):
                t.wait_ge(p0["xn"].h, tt + 1)
                for g4 in range(4):
                    idx = tt * 4 + g4
                    if idx >= 2:
                        t.wait_ge(p0["ev"].h, idx - 1)
                    for j in range(4):
                        fc = g4 * 4 + j
                        tr = t.transpose(ps_tr[idx % 2][:, j * 128:(j + 1) * 128],
                                         xn_tok[tt % 2][:, fc * 128:(fc + 1) * 128],
                                         identb[:, :])
                    p0["tr"].inc(tr)

        @block.scalar
        def _(sc):
            for tt in range(NT):
                xwait(sc, tt)
                p0["lns"].inc(
                    sc.activation(sqscr[tt % 2][:, :], xstg[tt % 2][:, :],
                                  AF.Square, accum_out=c_(1, tt)))
                sc.wait_ge(p0["st"].h, tt + 1)
                p0["rst"].inc(sc.activation(c_(7, tt), c_(6, tt), AF.Sqrt))
                if tt >= 1:
                    for g4 in range(4):
                        idx = (tt - 1) * 4 + g4
                        sc.wait_ge(p0["tr"].h, idx + 1)
                        dst = xn_fm[:, :].rearrange("p (f t) -> p f t", f=NF)[
                            :, g4 * 4:(g4 + 1) * 4, (tt - 1) * 128: tt * 128]
                        p0["ev"].inc(sc.copy(
                            dst, ps_tr[idx % 2][:, :].rearrange("p (f t) -> p f t", f=4)))
            for g4 in range(4):
                idx = (NT - 1) * 4 + g4
                sc.wait_ge(p0["tr"].h, idx + 1)
                dst = xn_fm[:, :].rearrange("p (f t) -> p f t", f=NF)[
                    :, g4 * 4:(g4 + 1) * 4, (NT - 1) * 128: NT * 128]
                p0["ev"].inc(sc.copy(
                    dst, ps_tr[idx % 2][:, :].rearrange("p (f t) -> p f t", f=4)))

    es_p0.close()
    es_ptr.close()

    q_sb = sbuf(es_mid, "q_sb", [128, HL * T], BF16, side="right")
    k_sb = sbuf(es_mid, "k_sb", [128, HL * T], BF16, side="right")
    v_sb = sbuf(es_mid, "v_sb", [128, HL * T], BF16, side="right")
    ssum = sbuf(es_mid, "ssum", [128, 128], F32, side="right")
    es_pq = contextlib.ExitStack()
    ps_q = [psum(es_pq, f"ps_q{i}", [128, 512], F32) for i in range(3)]
    ps_r = psum(es_pq, "ps_r", [128, 512], F32)
    ps_s = [psum(es_pq, f"ps_s{i}", [128, 1024], F32) for i in range(2)]

    CH = [(h, m, ts) for h in range(HL) for m in range(3) for ts in range(4)]
    CIDX = {c: i for i, c in enumerate(CH)}
    RCH = [(h, m, ts) for h in range(HL) for m in range(2) for ts in range(4)]
    RIDX = {c: i for i, c in enumerate(RCH)}
    P1CH = []
    for h in range(HL):
        for qt in range(NT):
            kend = (qt + 1) * 128
            if kend <= 1024:
                P1CH.append((h, qt, 0, 0, kend, True))
            else:
                P1CH.append((h, qt, 0, 0, 1024, False))
                P1CH.append((h, qt, 1, 1024, kend - 1024, False))
    P1OF = {}
    for i, (h, qt, ci, k0, w, sg_) in enumerate(P1CH):
        P1OF[(h, qt, ci)] = i
    P1H = {h: [i for i, c in enumerate(P1CH) if c[0] == h] for h in range(HL)}
    QKV_SB = [q_sb, k_sb, v_sb]

    p1 = {n: S("p1_" + n) for n in ["m", "e", "r", "v", "sm", "x", "he"]}

    with nc.Block("p1s") as block:
        @block.gpsimd
        def _(g):
            for h in range(2, HL):
                g.wait_ge(p1["m"].h, 12 * (h - 1))
                dsem["wq" if h % 2 == 0 else "wk"].inc(g.dma_start(
                    out=wslab[h % 2][:, :].rearrange("p (f c) -> p f c", f=NF),
                    in_=din["wqkv"][h].rearrange("(f p) c -> p f c", p=128)), 16)

        @block.tensor
        def _(t):
            t.wait_ge(dsem["cst"].h, 112)

            def emit_p1(i):
                h, qt, ci, k0, w, single = P1CH[i]
                if i >= 2:
                    t.wait_ge(p1["x"].h, i - 1)
                last = single or ci == 1
                qtile = q_sb[:, h * T + qt * 128: h * T + (qt + 1) * 128]
                hi = w - 128 if last else w
                for j in range((hi + 511) // 512):
                    lo = j * 512
                    n = min(512, hi - lo)
                    mm = t.matmul(
                        ps_s[i % 2][:, lo: lo + n], qtile,
                        k_sb[:, h * T + k0 + lo: h * T + k0 + lo + n],
                        start=True, stop=True)
                if last:
                    t.matmul(ps_s[i % 2][:, w - 128:w], identb[:, :],
                             maskS_sb[:, :], start=True, stop=False)
                    mm = t.matmul(
                        ps_s[i % 2][:, w - 128: w], qtile,
                        k_sb[:, h * T + k0 + w - 128: h * T + k0 + w],
                        start=False, stop=True)
                p1["sm"].inc(mm)

            def emit_rot(ri):
                h, m, ts = RCH[ri]
                cq = CIDX[(h, m, ts)]
                t.wait_ge(p1["e"].h, cq + 1)
                if ri >= 1:
                    t.wait_ge(p1["v"].h, ri)
                p1["r"].inc(t.matmul(ps_r[:, :], rmat_sb[:, :],
                                     praw[ri % 2][:, :], start=True, stop=True))

            for cq, (h, m, ts) in enumerate(CH):
                t.wait_ge(p0["ev"].h, 16 * (ts + 1))
                t.wait_ge(dsem["wq" if h % 2 == 0 else "wk"].h,
                          16 * (h // 2 + 1))
                if cq >= 3:
                    t.wait_ge(p1["e"].h, cq - 2)
                for fc in range(NF):
                    mm = t.matmul(
                        ps_q[cq % 3][:, :],
                        wslab[h % 2][:, fc * 3 * HD + m * HD:
                                     fc * 3 * HD + (m + 1) * HD],
                        xn_fm[:, fc * T + ts * 512: fc * T + (ts + 1) * 512],
                        start=(fc == 0), stop=(fc == NF - 1))
                p1["m"].inc(mm)
                if m < 2:
                    ri = RIDX[(h, m, ts)]
                    if ri >= 1 and ri % 8 != 0:
                        emit_rot(ri - 1)
                if m == 2 and ts == 0:
                    emit_rot(RIDX[(h, 1, 3)])
                if h >= 1:
                    c12 = m * 4 + ts
                    if c12 == 0:
                        t.wait_ge(p1["v"].h, 8 * h)
                    per = len(P1H[h - 1])
                    lo = c12 * per // 12
                    hi = (c12 + 1) * per // 12
                    for i in P1H[h - 1][lo:hi]:
                        emit_p1(i)
            t.wait_ge(p1["v"].h, 8 * HL)
            for i in P1H[HL - 1]:
                emit_p1(i)

        @block.scalar
        def _(sc):
            nev = [0]

            def emit_exp(i):
                h, qt, ci, k0, w, single = P1CH[i]
                sc.wait_ge(p1["sm"].h, i + 1)
                j = qt - 8
                par = (h % 2) * 8
                if single:
                    acc = ssum[:, h * 16 + qt: h * 16 + qt + 1]
                elif ci == 0:
                    if j == 0 and h >= 2:
                        sc.wait_ge(p1["he"].h, h - 1)
                    acc = ssum[:, h * 16 + qt: h * 16 + qt + 1]
                else:
                    acc = p1stat[:, par + j: par + j + 1]
                p1["x"].inc(sc.activation(pscr1[:, 0:w], ps_s[i % 2][:, 0:w],
                                          AF.Exp, accum_out=acc))

            for cq, (h, m, ts) in enumerate(CH):
                sc.wait_ge(p1["m"].h, cq + 1)
                if m < 2:
                    ri = RIDX[(h, m, ts)]
                    if ri >= 2:
                        sc.wait_ge(p1["v"].h, ri - 1)
                    p1["e"].inc(sc.activation(praw[ri % 2][:, :],
                                              ps_q[cq % 3][:, :], AF.Copy,
                                              scale=SCALE if m == 0 else 1.0))
                else:
                    p1["e"].inc(sc.activation(
                        v_sb[:, h * T + ts * 512: h * T + (ts + 1) * 512],
                        ps_q[cq % 3][:, :], AF.Copy))
                if h >= 1:
                    c12 = m * 4 + ts
                    per = len(P1H[h - 1])
                    lo = c12 * per // 12
                    hi = (c12 + 1) * per // 12
                    for i in P1H[h - 1][lo:hi]:
                        emit_exp(i)
            for i in P1H[HL - 1]:
                emit_exp(i)

        @block.vector
        def _(v):
            v.wait_ge(dsem["cst"].h, 112)
            nrope = [0]

            def emit_rope_v(ri):
                h, m, ts = RCH[ri]
                v.wait_ge(p1["r"].h, ri + 1)
                v.tensor_tensor(t2s[:, :], ps_r[:, :],
                                sin_sb[:, ts * 512:(ts + 1) * 512], ALU.mult)
                vbar(v, v.tensor_tensor(t1s[:, :], praw[ri % 2][:, :],
                                        cos_sb[:, ts * 512:(ts + 1) * 512],
                                        ALU.mult))
                p1["v"].inc(v.tensor_tensor(
                    QKV_SB[m][:, h * T + ts * 512: h * T + (ts + 1) * 512],
                    t1s[:, :], t2s[:, :], ALU.add))

            def emit_combine(h):
                par = (h % 2) * 8
                v.wait_ge(p1["x"].h, P1OF[(h, NT - 1, 1)] + 1)
                p1["he"].inc(v.tensor_tensor(
                    ssum[:, h * 16 + 8: h * 16 + 16],
                    ssum[:, h * 16 + 8: h * 16 + 16],
                    p1stat[:, par: par + 8], ALU.add))

            for cq, (h, m, ts) in enumerate(CH):
                if m < 2:
                    ri = RIDX[(h, m, ts)]
                    emit_rope_v(ri)
                if h >= 2 and m == 2 and ts == 3:
                    emit_combine(h - 2)
            emit_combine(HL - 2)
            emit_combine(HL - 1)

    es_pq.close()
    es_p1s.close()
    es_rope.close()

    abuf = [sbuf(es_mid, f"abuf{i}", [128, 512], BF16, side="right")
            for i in range(2)]
    pscr = sbuf(es_mid, "pscr", [128, T], BF16, side="right")
    bh = sbuf(es_mid, "bh", [128, 128], F32, side="right")
    rsum = sbuf(es_mid, "rsum", [128, 128], F32, side="right")
    browb = [sbuf(es_mid, f"browb{i}", [1, T], BF16, side="right")
             for i in range(2)]
    onesr_bf = sbuf(es_mid, "onesr_bf", [1, 128], BF16, side="right")
    expb = [sbuf(es_mid, f"expb{i}", [128, 512], BF16, side="right") for i in range(2)]
    mexpb = [sbuf(es_mid, f"mexpb{i}", [128, 512], BF16, side="right") for i in range(2)]

    ccsem = S("ccsem")
    p2 = {n: S("p2_" + n) for n in ["s1", "msk", "exp", "st", "tB", "eB",
                                    "dg", "pt", "avd", "ae", "eb", "bc",
                                    "ebe", "pb", "vt", "vte"]}

    def kchunks(qt):
        kend = (qt + 1) * 128
        return [(kc * 512, min(512, kend - kc * 512))
                for kc in range((kend + 511) // 512)]

    EV2 = [(h, qs, kt) for h in range(HL) for qs in range(4)
           for kt in range(4 * qs + 4)]
    EIDX = {e: i for i, e in enumerate(EV2)}
    LASTE = {(h, qs): EIDX[(h, qs, 4 * qs + 3)] for h in range(HL)
             for qs in range(4)}
    DGC = np.cumsum([1 if kt >= 4 * qs else 0 for (h, qs, kt) in EV2]).tolist()
    brow_done = {}
    adma_done = {}

    with nc.Block("p2") as block:
        es_ps2 = contextlib.ExitStack()
        ps_vt = psum(es_ps2, "ps_vt", [128, 512], BF16)
        ps_st = [psum(es_ps2, f"ps_st{i}", [128, 512], F32) for i in range(4)]
        ps_av = [psum(es_ps2, f"ps_av{i}", [128, 512], F32) for i in range(2)]
        ps_bc = [psum(es_ps2, "ps_bc0", [128, 512], F32)] * 2

        @block.sync
        def _(s):
            s.wait_ge(p0["ev"].h, 64)
            dsem["xnb"].inc(s.dma_start(out=xn_dram[:, :], in_=xn_fm[:, :]), 16)

        @block.tensor
        def _(t):
            t.wait_ge(p2["st"].h, 1)
            t.wait_ge(dsem["cs2"].h, 48)
            p2["tB"].inc(t.transpose(ps_bc[0][:, 0:128], rsum[:, :],
                                     identf[:, :]))
            for h in range(HL):
                for g4 in range(4):
                    gi = h * 4 + g4
                    if gi >= 1:
                        t.wait_ge(p2["vte"].h, gi)
                    for j in range(4):
                        kt = g4 * 4 + j
                        tr = t.transpose(ps_vt[:, j * 128:(j + 1) * 128],
                                         v_sb[:, h * T + kt * 128:
                                              h * T + (kt + 1) * 128],
                                         identb[:, :])
                    p2["vt"].inc(tr)

        @block.scalar
        def _(sc):
            for gi in range(32):
                h, g4 = gi // 4, gi % 4
                sc.wait_ge(p2["vt"].h, gi + 1)
                p2["vte"].inc(sc.copy(
                    v_sb[:, h * T + g4 * 512: h * T + (g4 + 1) * 512],
                    ps_vt[:, :]))

        @block.vector
        def _(v):
            p2["st"].inc(v.reciprocal(rsum[:, :], ssum[:, :]))
            v.wait_ge(p2["tB"].h, 1)
            p2["eB"].inc(v.tensor_copy(bh[:, :], ps_bc[0][:, 0:128]))

        @block.sync
        def _(s):
            s.wait_ge(p2["eB"].h, 1)
            dsem["brow"].inc(s.dma_start(out=brows_dram[:, :], in_=bh[:, :]), 16)
            s.wait_ge(dsem["brow"].h, 16)

            def emit_chunk(hh, qs):
                gidx = hh * 4 + qs
                s.wait_ge(p2["ae"].h, gidx + 1)
                dsem["at%d" % (gidx % 2)].inc(s.dma_start(
                    out=attn_dram[hh, :, qs * 512:(qs + 1) * 512],
                    in_=abuf[gidx % 2][:, :]), 16)

            for h in range(1, HL):
                for qs in range(4):
                    emit_chunk(h - 1, qs)
            for qs in range(4):
                emit_chunk(HL - 1, qs)

        @block.gpsimd
        def _(g):
            dsem["cst"].inc(g.dma_start(out=onesr_bf[:, :], in_=din["ones_row"][:, :]), 16)
            g.wait_ge(dsem["brow"].h, 16)
            for h in range(HL):
                if h >= 2:
                    g.wait_ge(p2["bc"].h, 4 * (h - 1))
                if DEBUG and h == 2:
                    dsem["dbg2"].inc(g.dma_start(out=dbg["brow0"].ap(),
                                                 in_=browb[0][:, :]), 16)
                    g.wait_ge(dsem["dbg2"].h, 16)
                dsem["brr"].inc(g.dma_start(
                    out=browb[h % 2][:, :],
                    in_=brows_dram[:, :].rearrange("(h r) c -> h (r c)", h=HL)[
                        h:h + 1, :]), 16)
            if DEBUG:
                g.wait_ge(p2["pb"].h, 1)
                dsem["dbg2"].inc(g.dma_start(out=dbg["expb0"].ap(), in_=expb[0][:, :]), 16)
                dsem["dbg2"].inc(g.dma_start(out=dbg["mexpb0"].ap(), in_=mexpb[0][:, :]), 16)
                g.wait_ge(p2["ae"].h, 1)
                dsem["dbg2"].inc(g.dma_start(out=dbg["abuf0"].ap(), in_=abuf[0][:, :]), 16)
            pid = g.partition_id()
            oth = (1 - pid % 2) * 1024
            g.wait_ge(dsem["at0"].h, 16 * 16)
            g.wait_ge(dsem["at1"].h, 16 * 16)
            sg.inc(g.dma_start(out=cc_in[:, :, :],
                               in_=attn_dram[:, :, bass.ds(oth, 1024)]), 16)
            g.wait_ge(sg.h, sg.v)
            ccsem.inc(g.collective_compute(
                "AllGather", ALU.bypass, replica_groups=groups,
                ins=[cc_in.ap().opt()], outs=[cc_out.ap().opt()]))
            if DEBUG:
                sg.inc(g.dma_start(out=dbg["ssum"].ap(), in_=ssum[:, :]), 16)
                sg.inc(g.dma_start(out=dbg["q"].ap(), in_=q_sb[:, :]), 16)
                sg.inc(g.dma_start(out=dbg["k"].ap(), in_=k_sb[:, :]), 16)
                sg.inc(g.dma_start(out=dbg["v"].ap(), in_=v_sb[:, :]), 16)
                sg.inc(g.dma_start(out=dbg["pscr"].ap(), in_=pscr[:, :]), 16)
                g.wait_ge(sg.h, sg.v)

        def PB(c):
            return (c % 4) * 512

        @block.tensor
        def _(t):
            def OFF(qs_, kt_):
                return max(0, kt_ - 4 * qs_) * 128

            t.wait_ge(dsem["cst"].h, 128)
            next_av = [0]

            def emit_av():
                j = next_av[0]
                jh, jqs, jkt = EV2[j]
                jo = OFF(jqs, jkt)
                t.wait_ge(p2["vte"].h, 4 * (jh + 1))
                t.wait_ge(p2["pb"].h, j + 1)
                p2["avd"].inc(t.matmul(
                    ps_av[(jh * 4 + jqs) % 2][:, jo:512],
                    v_sb[:, jh * T + jkt * 128: jh * T + (jkt + 1) * 128],
                    pscr[:, PB(j) + jo: PB(j) + 512],
                    start=(jkt == 0), stop=(jkt == 4 * jqs + 3)))
                next_av[0] += 1

            for i, (h, qs, kt) in enumerate(EV2):
                if kt == 0:
                    while next_av[0] < i:
                        emit_av()
                    gidx = h * 4 + qs
                    if gidx >= 2:
                        t.wait_ge(p2["ae"].h, gidx - 1)
                    t.wait_ge(dsem["brr"].h, 16 * (h + 1))
                    if gidx >= 1:
                        t.wait_ge(p2["ebe"].h, gidx)
                    p2["bc"].inc(t.matmul(
                        ps_bc[gidx % 2][:, :], onesr_bf[:, :],
                        browb[h % 2][0:1, qs * 512:(qs + 1) * 512],
                        start=True, stop=True))
                if i >= 4:
                    t.wait_ge(p2["pt"].h, i - 3)
                off = OFF(qs, kt)
                p2["dg"].inc(t.matmul(
                    ps_st[i % 4][:, off:512],
                    k_sb[:, h * T + kt * 128: h * T + (kt + 1) * 128],
                    q_sb[:, h * T + qs * 512 + off: h * T + (qs + 1) * 512],
                    start=True, stop=True))
                while i - next_av[0] >= 3:
                    emit_av()
            while next_av[0] < len(EV2):
                emit_av()

        @block.vector
        def _(v):
            def OFF(qs_, kt_):
                return max(0, kt_ - 4 * qs_) * 128

            tri_b = bass.AP(maskST_sb[:, :].tensor, 0,
                            [[128, 128], [0, 4], [1, 128]])

            def biasmul(i):
                jh, jqs, jkt = EV2[i]
                jo = OFF(jqs, jkt)
                jg = jh * 4 + jqs
                pb_ = PB(i)
                v.wait_ge(p2["pt"].h, i + 1)
                if jkt >= 4 * jqs:
                    mtt = v.tensor_tensor(
                        pscr[:, pb_ + jo: pb_ + jo + 128],
                        pscr[:, pb_ + jo: pb_ + jo + 128],
                        mexpb[jg % 2][:, jo:jo + 128], ALU.mult)
                    if jo + 128 < 512:
                        p2["pb"].inc(v.tensor_tensor(
                            pscr[:, pb_ + jo + 128: pb_ + 512],
                            pscr[:, pb_ + jo + 128: pb_ + 512],
                            expb[jg % 2][:, jo + 128:512], ALU.mult))
                    else:
                        p2["pb"].inc(mtt)
                else:
                    p2["pb"].inc(v.tensor_tensor(
                        pscr[:, pb_ + jo: pb_ + 512],
                        pscr[:, pb_ + jo: pb_ + 512],
                        expb[jg % 2][:, jo:512], ALU.mult))

            for i, (h, qs, kt) in enumerate(EV2):
                if i >= 1:
                    biasmul(i - 1)
                if kt == 0:
                    gidx = h * 4 + qs
                    if DEBUG and gidx == 2:
                        v.wait_ge(dsem["dbg2"].h, 64)
                    v.wait_ge(p2["bc"].h, gidx + 1)
                    p2["ebe"].inc(v.tensor_copy(expb[gidx % 2][:, :],
                                                ps_bc[gidx % 2][:, :]))
                    v.wait_ge(p2["ebe"].h, gidx + 1)
                    v.tensor_tensor(
                        mexpb[gidx % 2][:, :].rearrange("p (a b) -> p a b", a=4),
                        expb[gidx % 2][:, :].rearrange("p (a b) -> p a b", a=4),
                        tri_b, ALU.mult)
            biasmul(len(EV2) - 1)

        @block.scalar
        def _(sc):
            for i, (h, qs, kt) in enumerate(EV2):
                gidx = h * 4 + qs
                sc.wait_ge(p2["dg"].h, i + 1)
                if i >= 4:
                    sc.wait_ge(p2["avd"].h, i - 3)
                off = max(0, kt - 4 * qs) * 128
                p2["pt"].inc(sc.activation(
                    pscr[:, PB(i) + off: PB(i) + 512],
                    ps_st[i % 4][:, off:512], AF.Exp))
                if kt == 4 * qs + 3:
                    sc.wait_ge(p2["avd"].h, i + 1)
                    if gidx >= 2:
                        sc.wait_ge(dsem["at%d" % (gidx % 2)].h,
                                   16 * ((gidx - 2) // 2 + 1))
                    p2["ae"].inc(sc.copy(abuf[gidx % 2][:, :],
                                         ps_av[gidx % 2][:, :]))

        es_ps2.close()

    es_mid.close()
    es_p2c.close()


    es_p5l = contextlib.ExitStack()
    attn_full = sbuf(es_p5l, "attn_full", [128, 16 * 1024], BF16)
    at_sb = sbuf(es_p5l, "at_sb", [128, 8 * 1024], BF16)
    wout_sb = [sbuf(es_p5l, f"wout_sb{i}", [128, 16 * 512], BF16) for i in range(2)]
    wb_sb = [sbuf(es_p5l, f"wb_sb{i}", [128, 8 * 512], BF16) for i in range(2)]
    wdone = {}
    es_p34 = contextlib.ExitStack()
    xn_own = sbuf(es_p34, "xn_own", [128, NF * 1024], BF16)
    wa_sb = sbuf(es_p34, "wa_sb", [128, NF * 1024], BF16)
    rw_sb = sbuf(es_p34, "rw_sb", [128, 128], BF16)
    as_sb = sbuf(es_p34, "as_sb", [128, 8 * 1024], BF16)
    sqb = [sbuf(es_p34, f"sqb{i}", [128, 1024], BF16) for i in range(2)]
    ssrow = sbuf(es_p34, "ssrow", [1, 1024], F32)
    nrow = sbuf(es_p34, "nrow", [1, 1024], F32)
    rsrow = sbuf(es_p34, "rsrow", [1, 1024], F32)
    rscol = sbuf(es_p34, "rscol", [128, 8], F32)
    lcol = sbuf(es_p34, "lcol", [128, 8], F32)
    ecol = sbuf(es_p34, "ecol", [128, 8], F32)
    mrcol = sbuf(es_p34, "mrcol", [128, 8], F32)
    indcol = sbuf(es_p34, "indcol", [128, 8], F32)
    cmb = sbuf(es_p34, "cmb", [128, 64], F32)
    tstat = sbuf(es_p34, "tstat", [128, 8], F32)

    p3 = {n: S("p3_" + n) for n in ["sq", "ss", "se", "rs", "rc", "lg", "le",
                                    "tp", "am", "ae", "at", "ate"]}

    with nc.Block("p34") as block:
        @block.gpsimd
        def _(g):
            pid = g.partition_id()
            own = (pid % 2) * 1024
            partner = 1 - (pid % 2)
            g.wait_ge(dsem["xnb"].h, 16)
            dsem["xno"].inc(g.dma_start(
                out=xn_own[:, :].rearrange("p (f t) -> p f t", f=NF),
                in_=xn_dram[:, :].rearrange("p (f t) -> p f t", f=NF)[
                    :, :, bass.ds(own, 1024)]), 16)
            dsem["wa"].inc(g.dma_start(
                out=wa_sb[:, :].rearrange("p (f c) -> p f c", f=NF),
                in_=din["wa"][:, :].rearrange("(f p) c -> p f c", p=128)), 16)
            dsem["rw"].inc(g.dma_start(
                out=rw_sb[:, 0:NF * 8].rearrange("p (f c) -> p f c", f=NF),
                in_=din["router"][:, :, :].rearrange("f p c -> p f c")), 16)
            g.wait_ge(dsem["at0"].h, 16 * 16)
            g.wait_ge(dsem["at1"].h, 16 * 16)
            dsem["ful"].inc(g.dma_start(
                out=attn_full[:, 0:HL * 1024].rearrange("p (h t) -> p h t", h=HL),
                in_=attn_dram[:, :, bass.ds(own, 1024)].rearrange(
                    "h p t -> p h t")), 16)
            for os in range(2):
                dsem["wo%d" % os].inc(g.dma_start(
                    out=wout_sb[os][:, :].rearrange("p (gi c) -> p gi c", gi=16),
                    in_=din["wout"][os].rearrange("(gi p) c -> p gi c", p=128)), 16)
                dsem["wo%d" % os].inc(g.dma_start(
                    out=wb_sb[os][:, :].rearrange("p (e c) -> p e c", e=8),
                    in_=din["wb"][os].rearrange("e p c -> p e c")), 16)
            g.wait_ge(ccsem.h, 1)
            dsem["ful"].inc(g.dma_start(
                out=attn_full[:, HL * 1024:].rearrange("p (h t) -> p h t", h=HL),
                in_=cc_out[partner].rearrange("h p t -> p h t")), 16)

        @block.scalar
        def _(sc):
            sc.wait_ge(dsem["xno"].h, 16)
            for fc in range(NF):
                if fc >= 2:
                    sc.wait_ge(p3["ss"].h, fc - 1)
                p3["sq"].inc(sc.activation(sqb[fc % 2][:, :],
                                           xn_own[:, fc * 1024:(fc + 1) * 1024],
                                           AF.Square))

        es_ps34 = contextlib.ExitStack()
        ps_ss = [psum(es_ps34, f"ps_ss{i}", [1, 512], F32) for i in range(2)]
        ps_rc = psum(es_ps34, "ps_rc", [128, 8], F32)
        ps_lg = [psum(es_ps34, f"ps_lg{i}", [128, 8], F32) for i in range(2)]
        ps_a = [psum(es_ps34, f"ps_a{i}", [128, 512], F32) for i in range(2)]
        ps_t4 = psum(es_ps34, "ps_t4", [128, 512], BF16)

        @block.tensor
        def _(t):
            for fc in range(NF):
                t.wait_ge(p3["sq"].h, fc + 1)
                for th in range(2):
                    mm = t.matmul(ps_ss[th][:, :], onesc_sb[:, :],
                                  sqb[fc % 2][:, th * 512:(th + 1) * 512],
                                  start=(fc == 0), stop=(fc == NF - 1))
                p3["ss"].inc(mm)
            t.wait_ge(p3["rs"].h, 1)
            for tt in range(8):
                mm = t.matmul(ps_rc[:, tt:tt + 1],
                              rsrow[0:1, tt * 128:(tt + 1) * 128],
                              one1_sb[:, :], start=(tt == 0), stop=(tt == 7))
            p3["rc"].inc(mm)
            t.wait_ge(dsem["rw"].h, 16)
            for tt in range(8):
                if tt >= 2:
                    t.wait_ge(p3["le"].h, tt - 1)
                for fc in range(NF):
                    mm = t.matmul(ps_lg[tt % 2][:, :],
                                  xn_own[:, fc * 1024 + tt * 128:
                                         fc * 1024 + (tt + 1) * 128],
                                  rw_sb[:, fc * 8:(fc + 1) * 8],
                                  start=(fc == 0), stop=(fc == NF - 1))
                p3["lg"].inc(mm)
            t.wait_ge(dsem["wa"].h, 16)
            for tt in range(8):
                for eg in range(2):
                    aidx = tt * 2 + eg
                    if aidx >= 2:
                        t.wait_ge(p3["ae"].h, aidx - 1)
                    for fc in range(NF):
                        mm = t.matmul(ps_a[aidx % 2][:, :],
                                      xn_own[:, fc * 1024 + tt * 128:
                                             fc * 1024 + (tt + 1) * 128],
                                      wa_sb[:, fc * 1024 + eg * 512:
                                            fc * 1024 + (eg + 1) * 512],
                                      start=(fc == 0), stop=(fc == NF - 1))
                    p3["am"].inc(mm)
            for e in range(8):
                for tg in range(2):
                    tidx = e * 2 + tg
                    need_aidx = (tg * 4 + 3) * 2 + (e // 4)
                    t.wait_ge(p3["ae"].h, need_aidx + 1)
                    if tidx >= 1:
                        t.wait_ge(p3["ate"].h, tidx)
                    for j in range(4):
                        tt = tg * 4 + j
                        tr = t.transpose(ps_t4[:, j * 128:(j + 1) * 128],
                                         as_sb[:, tt * 1024 + e * 128:
                                               tt * 1024 + (e + 1) * 128],
                                         identb[:, :])
                    p3["at"].inc(tr)

        @block.vector
        def _(v):
            v.wait_ge(p3["ss"].h, NF)
            v.tensor_copy(ssrow[:, 0:512], ps_ss[0][:, :])
            p3["se"].inc(v.tensor_copy(ssrow[:, 512:1024], ps_ss[1][:, :]))

        @block.scalar
        def _(sc):
            sc.wait_ge(p3["se"].h, 1)
            p3["se"].inc(sc.activation(nrow[:, :], ssrow[:, :], AF.Sqrt))

        @block.vector
        def _(v):
            v.wait_ge(p3["se"].h, 2)
            vbar(v, v.tensor_scalar(ssrow[:, :], nrow[:, :], 1e-12, None,
                                    ALU.max))
            p3["rs"].inc(v.reciprocal(rsrow[:, :], ssrow[:, :]))
            v.wait_ge(p3["rc"].h, 1)
            for tt in range(8):
                cp = v.tensor_copy(rscol[:, tt:tt + 1], ps_rc[:, tt:tt + 1])
            vbar(v, cp)
            for tt in range(8):
                v.wait_ge(p3["lg"].h, tt + 1)
                vbar(v, v.tensor_scalar(lcol[:, :], ps_lg[tt % 2][:, :],
                                        rscol[:, tt:tt + 1], None, ALU.mult))
                p3["le"].inc(v.nop())
                p3["tp"].inc(v.reduce_max(tstat[:, 0:1], lcol[:, :],
                                          axis=AX, negate=True))
                v.wait_ge(p3["tp"].h, 2 * (tt + 1))
                vbar(v, v.reduce_max(tstat[:, 1:2], ecol[:, :], axis=AX))
                vbar(v, v.tensor_single_scalar(mrcol[:, :], ecol[:, :],
                                               tstat[:, 1:2], ALU.is_lt))
                vbar(v, v.tensor_tensor(mrcol[:, :], mrcol[:, :], ecol[:, :],
                                        ALU.mult))
                vbar(v, v.reduce_max(tstat[:, 2:3], mrcol[:, :], axis=AX))
                vbar(v, v.tensor_single_scalar(indcol[:, :], ecol[:, :],
                                               tstat[:, 2:3], ALU.is_ge))
                vbar(v, v.tensor_tensor(indcol[:, :], indcol[:, :], ecol[:, :],
                                        ALU.mult))
                vbar(v, v.reduce_sum(tstat[:, 3:4], indcol[:, :], axis=AX))
                vbar(v, v.reciprocal(tstat[:, 4:5], tstat[:, 3:4]))
                vbar(v, v.tensor_scalar(cmb[:, tt * 8:(tt + 1) * 8],
                                        indcol[:, :], tstat[:, 4:5], None,
                                        ALU.mult))
            for tt in range(8):
                for eg in range(2):
                    aidx = tt * 2 + eg
                    v.wait_ge(p3["am"].h, aidx + 1)
                    for j in range(4):
                        e = eg * 4 + j
                        ts_ = v.tensor_scalar(
                            as_sb[:, tt * 1024 + e * 128:tt * 1024 + (e + 1) * 128],
                            ps_a[aidx % 2][:, j * 128:(j + 1) * 128],
                            cmb[:, tt * 8 + e: tt * 8 + e + 1], None, ALU.mult)
                    p3["ae"].inc(ts_)

        @block.scalar
        def _(sc):
            for tt in range(8):
                sc.wait_ge(p3["tp"].h, 2 * tt + 1)
                p3["tp"].inc(sc.activation(ecol[:, :], lcol[:, :], AF.Exp,
                                           bias=tstat[:, 0:1]))
            for e in range(8):
                for tg in range(2):
                    tidx = e * 2 + tg
                    sc.wait_ge(p3["at"].h, tidx + 1)
                    p3["ate"].inc(sc.copy(
                        at_sb[:, e * 1024 + tg * 512: e * 1024 + (tg + 1) * 512],
                        ps_t4[:, :]))

        es_ps34.close()

    es_p34.close()

    es_p5 = contextlib.ExitStack()
    x_own = sbuf(es_p5, "x_own", [128, 8 * D], F32)
    gscr = [sbuf(es_p5, f"gscr{i}", [128, 512], F32) for i in range(2)]
    sgscr = [sbuf(es_p5, f"sgscr{i}", [128, 512], F32) for i in range(2)]
    oscr = [sbuf(es_p5, f"oscr{i}", [128, 512], F32) for i in range(2)]
    p5 = {n: S("p5_" + n) for n in ["s", "y", "sil", "c"]}
    odone = {}

    with nc.Block("p5") as block:
        es_ps5 = contextlib.ExitStack()
        ps_s5 = [psum(es_ps5, f"ps_s5{i}", [128, 512], F32) for i in range(2)]
        ps_y5 = [psum(es_ps5, f"ps_y5{i}", [128, 512], F32) for i in range(2)]

        @block.gpsimd
        def _(g):
            pid = g.partition_id()
            xb3 = din["xb"][:, :].rearrange("(a r) c -> a r c", a=2)
            dsem["xo"].inc(g.dma_start(
                out=x_own[:, :].rearrange("p (tt c) -> p tt c", tt=8),
                in_=xb3[pid % 2].rearrange("(tt p) c -> p tt c", p=128)), 16)
            for os in range(2, 4):
                g.wait_ge(p5["s"].h, 8 * (os - 1))
                g.wait_ge(p5["y"].h, 8 * (os - 1))
                dsem["wo%d" % (os % 2)].inc(g.dma_start(
                    out=wout_sb[os % 2][:, :].rearrange("p (gi c) -> p gi c", gi=16),
                    in_=din["wout"][os].rearrange("(gi p) c -> p gi c", p=128)), 16)
                dsem["wo%d" % (os % 2)].inc(g.dma_start(
                    out=wb_sb[os % 2][:, :].rearrange("p (e c) -> p e c", e=8),
                    in_=din["wb"][os].rearrange("e p c -> p e c")), 16)
            g.wait_ge(dsem["xo"].h, 16)

        @block.tensor
        def _(t):
            t.wait_ge(dsem["ful"].h, 32)
            t.wait_ge(p3["ate"].h, 16)
            for os in range(4):
                t.wait_ge(dsem["wo%d" % (os % 2)].h, 32 * (os // 2 + 1))
                for tt in range(8):
                    idx = os * 8 + tt
                    if idx >= 2:
                        t.wait_ge(p5["c"].h, idx - 1)
                        t.wait_ge(p5["sil"].h, idx - 1)
                    for gi in range(16):
                        mm = t.matmul(ps_s5[idx % 2][:, :],
                                      attn_full[:, gi * 1024 + tt * 128:
                                                gi * 1024 + (tt + 1) * 128],
                                      wout_sb[os % 2][:, gi * 512:(gi + 1) * 512],
                                      start=(gi == 0), stop=(gi == 15))
                    p5["s"].inc(mm)
                    for e in range(8):
                        mm = t.matmul(ps_y5[idx % 2][:, :],
                                      at_sb[:, e * 1024 + tt * 128:
                                            e * 1024 + (tt + 1) * 128],
                                      wb_sb[os % 2][:, e * 512:(e + 1) * 512],
                                      start=(e == 0), stop=(e == 7))
                    p5["y"].inc(mm)

        @block.scalar
        def _(sc):
            for os in range(4):
                for tt in range(8):
                    idx = os * 8 + tt
                    sc.wait_ge(p5["y"].h, idx + 1)
                    if idx >= 2:
                        sc.wait_ge(p5["c"].h, idx - 1)
                    p5["sil"].inc(sc.activation(gscr[idx % 2][:, :],
                                                ps_y5[idx % 2][:, :], AF.Silu))

        @block.sync
        def _(s):
            for os in range(4):
                for tt in range(8):
                    idx = os * 8 + tt
                    s.wait_ge(p5["c"].h, idx + 1)
                    dsem["out%d" % (idx % 2)].inc(s.dma_start(
                        out=out_d[tt * 128:(tt + 1) * 128, os * 512:(os + 1) * 512],
                        in_=oscr[idx % 2][:, :]), 16)
            s.wait_ge(dsem["out0"].h, 16 * 16)
            s.wait_ge(dsem["out1"].h, 16 * 16)

        @block.vector
        def _(v):
            v.wait_ge(dsem["xo"].h, 16)
            for os in range(4):
                for tt in range(8):
                    idx = os * 8 + tt
                    v.wait_ge(p5["s"].h, idx + 1)
                    v.wait_ge(p5["sil"].h, idx + 1)
                    if idx >= 2:
                        v.wait_ge(dsem["out%d" % (idx % 2)].h,
                                  16 * ((idx - 2) // 2 + 1))
                    vbar(v, v.scalar_tensor_tensor(sgscr[idx % 2][:, :],
                                                   ps_s5[idx % 2][:, :], 1.0,
                                                   gscr[idx % 2][:, :],
                                                   ALU.mult, ALU.mult))
                    p5["c"].inc(v.tensor_tensor(
                        oscr[idx % 2][:, :], sgscr[idx % 2][:, :],
                        x_own[:, tt * D + os * 512: tt * D + (os + 1) * 512],
                        ALU.add))

        if DEBUG:
            @block.gpsimd
            def _(g):
                g.wait_ge(sd.h, sd.v)
                sg.inc(g.dma_start(out=dbg["attn"].ap(), in_=attn_dram.ap()), 16)
                sg.inc(g.dma_start(out=dbg["brows"].ap(), in_=brows_dram.ap()), 16)
                sg.inc(g.dma_start(out=dbg["xn"].ap(), in_=xn_dram.ap()), 16)
                sg.inc(g.dma_start(out=dbg["ccout"].ap(), in_=cc_out.ap()), 16)
                g.wait_ge(sg.h, sg.v)

        es_ps5.close()
    es_p5.close()
    es_p5l.close()
    return nc


def _host_inputs(x, w_qkv, w_out, router_w, Wa, Wb):
    import jax
    import jax.numpy as jnp
    x = np.asarray(x, dtype=np.float32)
    w_qkv = np.asarray(w_qkv, dtype=np.float32)
    w_out = np.asarray(w_out, dtype=np.float32)
    router_w = np.asarray(router_w, dtype=np.float32)
    Wa = np.asarray(Wa, dtype=np.float32)
    Wb = np.asarray(Wb, dtype=np.float32)

    with jax.default_device(jax.devices("cpu")[0]):
        inv = jnp.asarray(1.0) / jnp.asarray(10000.0) ** (
            jnp.arange(0, HD, 2, dtype=jnp.float32) / HD)
        tv = jnp.arange(T, dtype=jnp.float32)
        freqs = jnp.einsum("i,j->ij", tv, inv)
        emb = jnp.concatenate([freqs, freqs], axis=-1)
        cosf = np.ascontiguousarray(np.asarray(jnp.cos(emb)).T)
        sinf = np.ascontiguousarray(np.asarray(jnp.sin(emb)).T)

    R = np.zeros((128, 128), dtype=np.float32)
    half = np.arange(64)
    R[half, half + 64] = -1.0
    R[half + 64, half] = 1.0
    rmat = np.ascontiguousarray(R.T)

    ii, jj = np.indices((128, 128))
    maskS = np.where(jj <= ii, 0.0, NEG).astype(np.float32)
    maskST = np.where(ii <= jj, 1.0, 0.0).astype(np.float32)
    identf = np.eye(128, dtype=np.float32)

    wa_arr = np.ascontiguousarray(Wa.transpose(1, 0, 2).reshape(D, 1024))
    wb_arr = np.ascontiguousarray(Wb.reshape(8, 128, 4, 512).transpose(2, 0, 1, 3))
    router_arr = np.ascontiguousarray(router_w.reshape(16, 128, 8))
    woutT = w_out.T

    in_maps = []
    for c in range(8):
        b, hg = c // 2, c % 2
        hs = slice(hg * 1024, (hg + 1) * 1024)
        wq_l = w_qkv[hs, :].reshape(HL, HD, D)
        wk_l = w_qkv[2048 + hg * 1024: 2048 + (hg + 1) * 1024, :].reshape(HL, HD, D)
        wv_l = w_qkv[4096 + hg * 1024: 4096 + (hg + 1) * 1024, :].reshape(HL, HD, D)
        wqkv_arr = np.ascontiguousarray(
            np.concatenate([wq_l.transpose(0, 2, 1), wk_l.transpose(0, 2, 1),
                            wv_l.transpose(0, 2, 1)], axis=2))
        os_ = slice((1 - hg) * 1024, (2 - hg) * 1024)
        wout_local = np.concatenate([woutT[hs], woutT[os_]], axis=0)
        wout_arr = np.ascontiguousarray(
            wout_local.reshape(D, 4, 512).transpose(1, 0, 2))
        in_maps.append({
            "xb": np.ascontiguousarray(x[b]),
            "wqkv": wqkv_arr,
            "wout": wout_arr,
            "router": router_arr,
            "wa": wa_arr,
            "wb": wb_arr,
            "cosf": cosf, "sinf": sinf, "rmat": rmat, "identf": identf,
            "maskS": maskS, "maskST": maskST,
            "ones_row": np.ones((1, 128), np.float32),
            "ones_col": np.ones((128, 1), np.float32),
            "one1": np.ones((1, 1), np.float32),
        })
    return in_maps


def kernel(L0_attn_x, L0_attn_w_qkv, L0_attn_w_out, L0_ffn_router_w,
           L0_ffn_Wa, L0_ffn_Wb):
    in_maps = _host_inputs(L0_attn_x, L0_attn_w_qkv, L0_attn_w_out,
                           L0_ffn_router_w, L0_ffn_Wa, L0_ffn_Wb)
    nc = build_nc()
    res = run_bass_kernel_spmd(nc, in_maps, core_ids=list(range(8)))
    out = np.empty((4, T, D), dtype=np.float32)
    for c in range(8):
        b, hg = c // 2, c % 2
        out[b, hg * 1024:(hg + 1) * 1024] = res.results[c]["out"]
    if DEBUG:
        kernel.last_results = res
    return out
